# version 03
# baseline (speedup 1.0000x reference)
import contextlib
import numpy as np
import ml_dtypes
import concourse.bass as bass
import concourse.mybir as mybir
from concourse.bass_utils import run_bass_kernel_spmd

F32 = mybir.dt.float32
BF16 = mybir.dt.bfloat16
AF = mybir.ActivationFunctionType
ALU = mybir.AluOpType
AX = mybir.AxisListType
BF = ml_dtypes.bfloat16

D = 2048
AD = 512
KD = 16
ADT = 4
C = 128
E = 1024
ET = 8
NU = 8
HT = 16
TB = 512
EPS = 1e-5


class Sems:
    def __init__(self, nc, stack, names):
        self.h = {}
        self.c = {}
        for n in names:
            self.h[n] = stack.enter_context(nc.semaphore(n))
            self.c[n] = 0

    def dma(self, eng, dst, src, name):
        eng.dma_start(dst, src).then_inc(self.h[name], 16)
        self.c[name] += 16
        return self.c[name]

    def inc(self, inst, name):
        inst.then_inc(self.h[name], 1)
        self.c[name] += 1
        return self.c[name]

    def flush(self, eng, inst, name="fl"):
        inst.then_inc(self.h[name], 1)
        self.c[name] += 1
        eng.wait_ge(self.h[name], self.c[name])


def rep_ap(t, offset_elems, pstride, reps, inner):
    h = t.tensor if hasattr(t, "tensor") else t
    return bass.AP(h, offset_elems, [[pstride, 128], [0, reps], [1, inner]])


def build_nc(T, dbg=False):
    KIND = dict(kind="ExternalOutput") if dbg else {}
    NCH = T // C
    NB = T // TB
    nc = bass.Bass("TRN2", target_bir_lowering=False)

    xT = nc.dram_tensor("xT", [D, T], F32, kind="ExternalInput")
    wq = nc.dram_tensor("wq", [D, AD], F32, kind="ExternalInput")
    wk = nc.dram_tensor("wk", [3, D, AD], F32, kind="ExternalInput")
    wv = nc.dram_tensor("wv", [3, D, AD], F32, kind="ExternalInput")
    wo = nc.dram_tensor("wo", [AD, D], F32, kind="ExternalInput")
    wg3h = nc.dram_tensor("wg3h", [128, 48], F32, kind="ExternalInput")
    dlog = nc.dram_tensor("dlog", [1, 3], F32, kind="ExternalInput")
    oscale = nc.dram_tensor("oscale", [1, 1], F32, kind="ExternalInput")
    fgh = nc.dram_tensor("fgh", [2, 128, 128], F32, kind="ExternalInput")
    bgh = nc.dram_tensor("bgh", [1, 16], F32, kind="ExternalInput")
    fwup = nc.dram_tensor("fwup", [6, D, E], F32, kind="ExternalInput")
    fwsw = nc.dram_tensor("fwsw", [6, E, E], F32, kind="ExternalInput")
    fwdn = nc.dram_tensor("fwdn", [6, E, D], F32, kind="ExternalInput")
    fwsu = nc.dram_tensor("fwsu", [D, D], F32, kind="ExternalInput")
    fwsa = nc.dram_tensor("fwsa", [E, E], F32, kind="ExternalInput")
    fwsd = nc.dram_tensor("fwsd", [D, D], F32, kind="ExternalInput")
    biascols = nc.dram_tensor("biascols", [128, 120], F32, kind="ExternalInput")
    bexth = nc.dram_tensor("bexth", [7, D], F32, kind="ExternalInput")
    eye_bf_d = nc.dram_tensor("eye_bfd", [128, 128], BF16, kind="ExternalInput")
    eye_f_d = nc.dram_tensor("eye_fd", [128, 128], F32, kind="ExternalInput")
    ones1f_d = nc.dram_tensor("ones1fd", [1, 128], F32, kind="ExternalInput")
    onescol_d = nc.dram_tensor("onescold", [128, 1], BF16, kind="ExternalInput")
    selbf_d = nc.dram_tensor("selbfd", [8, 768], BF16, kind="ExternalInput")
    ubits_d = nc.dram_tensor("ubitsd", [128, 7], F32, kind="ExternalInput")
    c1bits_d = nc.dram_tensor("c1bitsd", [128, 7], F32, kind="ExternalInput")
    expnbits_d = nc.dram_tensor("expnbitsd", [128, 896], BF16, kind="ExternalInput")
    causT_d = nc.dram_tensor("causTd", [128, 128], BF16, kind="ExternalInput")

    y_s = nc.dram_tensor("y_s", [D, T], F32, **KIND)
    r0_s = nc.dram_tensor("r0_s", [D, T], F32, **KIND)
    r0hi_s = nc.dram_tensor("r0hi_s", [D, T], BF16, **KIND)
    r0lo_s = nc.dram_tensor("r0lo_s", [D, T], BF16, **KIND)
    xt1hi_s = nc.dram_tensor("xt1hi_s", [D, T], BF16, **KIND)
    xt1lo_s = nc.dram_tensor("xt1lo_s", [D, T], BF16, **KIND)
    wup_s = nc.dram_tensor("wup_s", [NU, D, E], BF16)
    wsw_s = nc.dram_tensor("wsw_s", [NU, E, E], BF16)
    wdn_s = nc.dram_tensor("wdn_s", [HT, 128, NU, ET, 128], BF16)
    x2T = nc.dram_tensor("x2T", [D, T], F32, kind="ExternalOutput")
    dbgout = nc.dram_tensor("dbgout", [128, 32768], F32, kind="ExternalOutput") if dbg else None

    def fm(dr):
        return dr.rearrange("(k p) t -> p k t", p=128)

    with contextlib.ExitStack() as top:
        S = Sems(nc, top, ["ld", "st", "xld", "wld", "mm", "ev", "prog", "fl"])
        sb = lambda name, shape, dt: top.enter_context(nc.sbuf_tensor(name, shape, dt))

        eye_bf = sb("eye_bf", [128, 128], BF16)
        eye_f = sb("eye_f", [128, 128], F32)
        ones1f = sb("ones1f", [1, 128], F32)
        onescol = sb("onescol", [128, 1], BF16)
        selbf = sb("selbf", [8, 768], BF16)
        wghi = sb("wghi", [128, 2, 16, 8], BF16)
        wglo = sb("wglo", [128, 2, 16, 8], BF16)
        bgb = sb("bgb", [128, 16], F32)
        bcols = sb("bcols", [128, 120], F32)
        bext_bf = sb("bext_bf", [8, D], BF16)
        osc_col = sb("osc_col", [128, 1], F32)

        with contextlib.ExitStack() as pa:
            sa = lambda name, shape, dt: pa.enter_context(nc.sbuf_tensor(name, shape, dt))

            wo_sb = sa("wo_sb", [128, ADT, D], BF16)
            wloc = sa("wloc", [128, 3, 128], BF16)
            cmb = sa("cmb", [128, 3, 128], BF16)
            Isc = sa("Isc", [128, 3, 128], BF16)
            wtmp = sa("wtmp", [128, 128], BF16)
            dvec = sa("dvec", [128, 12], F32)
            dvecs = [dvec[:, i : i + 1] for i in range(12)]
            decu_sb = sa("decu_sb", [128, 8], F32)
            rowd = sa("rowd", [1, 4], F32)
            paA1w = pa.enter_context(contextlib.ExitStack())
            sw_ = lambda name, shape, dt: paA1w.enter_context(nc.sbuf_tensor(name, shape, dt))
            wq_sb = sw_("wq_sb", [128, KD, AD], BF16)
            wk_sb = sw_("wk_sb", [128, 3, KD, AD], BF16)
            wv_sb = sw_("wv_sb", [128, 3, KD, AD], BF16)
            wg3_sb = sw_("wg3_sb", [128, KD, 3], BF16)

            with contextlib.ExitStack() as ws:
                psA_misc = ws.enter_context(nc.psum_tensor("psW_misc", [128, 512], F32))
                scr_f = ws.enter_context(nc.sbuf_tensor("scr_f", [128, 160], F32))
                ubits = ws.enter_context(nc.sbuf_tensor("ubits", [128, 7], F32))
                c1bits = ws.enter_context(nc.sbuf_tensor("c1bits", [128, 7], F32))
                expnbits = ws.enter_context(nc.sbuf_tensor("expnbits", [128, 896], BF16))
                causT = ws.enter_context(nc.sbuf_tensor("causT", [128, 128], BF16))

                with nc.Block() as blk:
                    @blk.vector
                    def _(v):
                        v.memset(bext_bf[:], 0.0)
                with nc.Block() as blk:
                    @blk.sync
                    def _(e):
                        for dst, src in [
                            (eye_bf[:], eye_bf_d), (eye_f[:], eye_f_d), (ones1f[:], ones1f_d),
                            (onescol[:], onescol_d), (selbf[:], selbf_d), (ubits[:], ubits_d),
                            (c1bits[:], c1bits_d), (expnbits[:], expnbits_d), (causT[:], causT_d),
                            (bcols[:], biascols), (rowd[0:1, 0:3], dlog),
                            (scr_f[0:1, 0:1], oscale), (scr_f[0:1, 1:17], bgh),
                        ]:
                            S.dma(e, dst, src[:], "ld")

                    @blk.gpsimd
                    def _(g):
                        S.dma(g, bext_bf[0:7, :], bexth[:], "st")
                        S.dma(g, wq_sb[:], wq.rearrange("(k p) d -> p k d", p=128), "st")
                        for i in range(3):
                            S.dma(g, wk_sb[:, i], wk[i].rearrange("(k p) d -> p k d", p=128), "st")
                            S.dma(g, wv_sb[:, i], wv[i].rearrange("(k p) d -> p k d", p=128), "st")
                        S.dma(g, wo_sb[:], wo.rearrange("(k p) d -> p k d", p=128), "st")
                        S.dma(g, wg3_sb[:], wg3h.rearrange("p (k d) -> p k d", d=3), "st")
                        for u in range(NU):
                            if u < 6:
                                S.dma(g, wup_s[u].rearrange("(k p) e -> p k e", p=128),
                                      fwup[u].rearrange("(k p) e -> p k e", p=128), "st")
                                S.dma(g, wsw_s[u].rearrange("(k p) e -> p k e", p=128),
                                      fwsw[u].rearrange("(k p) e -> p k e", p=128), "st")
                                dnsrc = fwdn[u].rearrange("(k p) (m j) -> p k m j", p=128, j=128)
                            else:
                                S.dma(g, wup_s[u].rearrange("(k p) e -> p k e", p=128),
                                      fwsu.rearrange("(k p) e -> p k e", p=128)[:, :, (u - 6) * E : (u - 5) * E], "st")
                                S.dma(g, wsw_s[u].rearrange("(k p) e -> p k e", p=128),
                                      fwsa.rearrange("(k p) e -> p k e", p=128), "st")
                                dnsrc = fwsd[(u - 6) * E : (u - 5) * E, :].rearrange("(k p) (m j) -> p k m j", p=128, j=128)
                            for kk in range(ET):
                                S.dma(g, wdn_s[:, :, u, kk, :].rearrange("m p j -> p m j"),
                                      dnsrc[:, kk], "st")

                with nc.Block() as blk:
                    @blk.tensor
                    def _(t):
                        t.wait_ge(S.h["ld"], S.c["ld"])
                        t.matmul(psA_misc[:, 0:1], ones1f[:], scr_f[0:1, 0:1], start=True, stop=True)
                        t.matmul(psA_misc[:, 16:32], ones1f[:], scr_f[0:1, 1:17], start=True, stop=True)
                with nc.Block() as blk:
                    @blk.vector
                    def _(v):
                        v.tensor_copy(osc_col[:], psA_misc[:, 0:1])
                        v.tensor_copy(bgb[:], psA_misc[:, 16:32])
                with nc.Block() as blk:
                    @blk.vector
                    def _(v):
                        v.wait_ge(S.h["st"], S.c["st"])

                for j in range(2):
                    with nc.Block() as blk:
                        @blk.sync
                        def _(e, j=j):
                            S.dma(e, scr_f[:, 32:160], fgh[j], "ld")
                    with nc.Block() as blk:
                        @blk.vector
                        def _(v, j=j):
                            v.wait_ge(S.h["ld"], S.c["ld"])
                            hi = wghi[:, j].rearrange("p k d -> p (k d)")
                            lo = wglo[:, j].rearrange("p k d -> p (k d)")
                            v.tensor_copy(hi, scr_f[:, 32:160])
                            v.tensor_tensor(lo, scr_f[:, 32:160], hi, ALU.subtract)

                for i in range(3):
                    with nc.Block() as blk:
                        @blk.tensor
                        def _(t, i=i):
                            t.matmul(psA_misc[:, 32:33], ones1f[:], rowd[0:1, i : i + 1], start=True, stop=True)
                    with nc.Block() as blk:
                        @blk.scalar
                        def _(sc):
                            sc.activation(dvecs[0], psA_misc[:, 32:33], AF.Sigmoid)
                    with nc.Block() as blk:
                        @blk.vector
                        def _(v, i=i):
                            F = lambda inst: S.flush(v, inst)
                            for b in range(1, 7):
                                F(v.tensor_mul(dvecs[b], dvecs[b - 1], dvecs[b - 1]))
                            F(v.tensor_mul(dvecs[7], dvecs[6], dvecs[6]))
                            for tgt, bits in ((i, ubits), (3 + i, c1bits)):
                                cur = 8
                                for b in range(7):
                                    F(v.tensor_scalar_add(dvecs[10], dvecs[b], -1.0))
                                    F(v.tensor_scalar(dvecs[11], bits[:, b : b + 1], dvecs[10], 1.0, ALU.mult, ALU.add))
                                    if b == 0:
                                        F(v.tensor_copy(dvecs[cur], dvecs[11]))
                                    else:
                                        F(v.tensor_mul(dvecs[cur ^ 1], dvecs[cur], dvecs[11]))
                                        cur ^= 1
                                F(v.tensor_copy(decu_sb[:, tgt : tgt + 1], dvecs[cur]))
                            F(v.tensor_copy(wloc[:, i], causT[:]))
                            cur = 0
                            for b in range(7):
                                F(v.tensor_scalar_add(dvecs[10], dvecs[b], -1.0))
                                F(v.tensor_scalar(wtmp[:], expnbits[:, b * 128 : (b + 1) * 128], dvecs[10], 1.0, ALU.mult, ALU.add))
                                if cur == 0:
                                    F(v.tensor_mul(cmb[:, i], wloc[:, i], wtmp[:]))
                                else:
                                    F(v.tensor_mul(wloc[:, i], cmb[:, i], wtmp[:]))
                                cur ^= 1
                            if cur == 1:
                                F(v.tensor_copy(wloc[:, i], cmb[:, i]))
                            F(v.tensor_scalar(Isc[:, i], eye_bf[:], dvecs[7], None, ALU.mult))

            RQ, RKF, RKT, RV, RGB, RW = 0, 512, 2048, 3584, 5120, 5888
            qkv_s = nc.dram_tensor("qkv_s", [NB, 128, 4, RW], BF16)
            ao_s = nc.dram_tensor("ao_s", [D, T], F32)

            def emit_rows512(v_or_none, ps_sum, ps_sq, rows, RSW):
                with nc.Block() as blk:
                    @blk.vector
                    def _(v):
                        F = lambda inst: S.flush(v, inst)
                        F(v.tensor_scalar(RSW[0], ps_sum[0:1, 0:512], 1.0 / D, None, ALU.mult))
                        F(v.tensor_scalar(RSW[1], ps_sq[0:1, 0:512], 1.0 / D, None, ALU.mult))
                        F(v.tensor_mul(RSW[2], RSW[0], RSW[0]))
                        F(v.tensor_sub(RSW[4], RSW[1], RSW[2]))
                        F(v.tensor_scalar_add(RSW[3], RSW[4], EPS))
                with nc.Block() as blk:
                    @blk.scalar
                    def _(sc):
                        S.flush(sc, sc.activation(RSW[2], RSW[3], AF.Ln))
                        S.flush(sc, sc.activation(RSW[4], RSW[2], AF.Exp, bias=0.0, scale=-0.5))
                with nc.Block() as blk:
                    @blk.vector
                    def _(v):
                        F = lambda inst: S.flush(v, inst)
                        F(v.tensor_mul(RSW[1], RSW[4], RSW[4]))
                        F(v.tensor_mul(RSW[2], RSW[1], RSW[3]))
                        F(v.tensor_scalar(RSW[1], RSW[2], -0.5, 1.5, ALU.mult, ALU.add))
                        F(v.tensor_mul(RSW[6], RSW[4], RSW[1]))
                        F(v.tensor_mul(RSW[5], RSW[0], RSW[6]))

            with contextlib.ExitStack() as pa1:
                s1 = lambda name, shape, dt: pa1.enter_context(nc.sbuf_tensor(name, shape, dt))
                ps_proj = pa1.enter_context(nc.psum_tensor("ps_proj", [128, 2048], F32))
                ps_tp = pa1.enter_context(nc.psum_tensor("ps_tp", [128, 1024], BF16))
                ps_s1 = pa1.enter_context(nc.psum_tensor("ps_s1", [128, 512], F32))
                ps_s2 = pa1.enter_context(nc.psum_tensor("ps_s2", [128, 512], F32))
                ps_b1 = pa1.enter_context(nc.psum_tensor("ps_b1", [128, 512], F32))

                xq = s1("xq", [128, 2, 1024], F32)
                st1 = s1("st1", [128, 1024], BF16)
                st2 = s1("st2", [128, 1024], BF16)
                ta = s1("ta", [128, 1024], F32)
                xn_blk = s1("xn_blk", [128, KD, 512], BF16)
                rows = s1("rows", [1, 2560], F32)
                bc_blk = s1("bc_blk", [128, 1024], F32)
                qtok = s1("qtok", [128, AD], BF16)
                ktok3 = s1("ktok3", [128, 3, AD], BF16)
                qfm_s1 = s1("qfm_s1", [128, 512], BF16)
                kfm_s1 = s1("kfm_s1", [128, 1536], BF16)
                kt_s1 = s1("kt_s1", [128, 1536], BF16)
                v_s1 = s1("v_s1", [128, 1536], BF16)
                gb_s1 = s1("gb_s1", [128, 768], BF16)
                g3b = s1("g3b", [128, 12], F32)
                e_blk = s1("e_blk", [128, 12], F32)
                gsum = s1("gsum", [128, 8], F32)
                gd_blk = s1("gd_blk", [128, 4, 6], F32)
                gdT_blk = s1("gdT_blk", [8, 512], BF16)
                RSA = [rows[0:1, i * 512 : (i + 1) * 512] for i in range(5)]
                bc_h = bc_blk.tensor if hasattr(bc_blk, "tensor") else bc_blk
                a1_st = {}

                def rows5_and_bcast(psum1, psum2):
                    with nc.Block() as blk:
                        @blk.vector
                        def _(v):
                            F = lambda inst: S.flush(v, inst)
                            F(v.tensor_scalar(RSA[0], psum1[0:1, 0:512], 1.0 / D, None, ALU.mult))
                            F(v.tensor_scalar(RSA[1], psum2[0:1, 0:512], 1.0 / D, None, ALU.mult))
                            F(v.tensor_mul(RSA[2], RSA[0], RSA[0]))
                            F(v.tensor_sub(RSA[3], RSA[1], RSA[2]))
                            F(v.tensor_scalar_add(RSA[1], RSA[3], EPS))
                    with nc.Block() as blk:
                        @blk.scalar
                        def _(sc):
                            S.flush(sc, sc.activation(RSA[2], RSA[1], AF.Ln))
                            S.flush(sc, sc.activation(RSA[3], RSA[2], AF.Exp, bias=0.0, scale=-0.5))
                    with nc.Block() as blk:
                        @blk.vector
                        def _(v):
                            F = lambda inst: S.flush(v, inst)
                            F(v.tensor_mul(RSA[2], RSA[3], RSA[3]))
                            F(v.tensor_mul(RSA[4], RSA[2], RSA[1]))
                            F(v.tensor_scalar(RSA[2], RSA[4], -0.5, 1.5, ALU.mult, ALU.add))
                            F(v.tensor_mul(RSA[4], RSA[3], RSA[2]))
                            F(v.tensor_mul(RSA[3], RSA[0], RSA[4]))
                    with nc.Block() as blk:
                        @blk.tensor
                        def _(t):
                            t.matmul(ps_b1[:], ones1f[:], RSA[3], start=True, stop=True)
                            t.matmul(psum1[:, 0:512], ones1f[:], RSA[4], start=True, stop=True)
                    with nc.Block() as blk:
                        @blk.vector
                        def _(v):
                            S.flush(v, v.tensor_copy(bc_blk[:, 0:512], ps_b1[:]))
                            S.flush(v, v.tensor_copy(bc_blk[:, 512:1024], psum1[:, 0:512]))

                for g in range(NB):
                    for q8 in range(8):
                        with nc.Block() as blk:
                            @blk.sync
                            def _(e, g=g, q8=q8):
                                S.dma(e, xq[:, q8 % 2].rearrange("p (k t) -> p k t", t=64),
                                      fm(xT)[:, :, g * TB + q8 * 64 : g * TB + (q8 + 1) * 64], "xld")
                        with nc.Block() as blk:
                            @blk.scalar
                            def _(sc, q8=q8):
                                sc.wait_ge(S.h["xld"], S.c["xld"])
                                sc.activation(st1[:], xq[:, q8 % 2], AF.Identity)
                                sc.activation(st2[:], xq[:, q8 % 2], AF.Square)
                        with nc.Block() as blk:
                            @blk.tensor
                            def _(t, q8=q8):
                                for k in range(KD):
                                    t.matmul(ps_s1[0:1, q8 * 64 : (q8 + 1) * 64], onescol[:], st1[:, k * 64 : (k + 1) * 64],
                                             start=(k == 0), stop=(k == KD - 1))
                                for k in range(KD):
                                    t.matmul(ps_s2[0:1, q8 * 64 : (q8 + 1) * 64], onescol[:], st2[:, k * 64 : (k + 1) * 64],
                                             start=(k == 0), stop=(k == KD - 1))
                    rows5_and_bcast(ps_s1, ps_s2)
                    for q8 in range(8):
                        with nc.Block() as blk:
                            @blk.sync
                            def _(e, g=g, q8=q8):
                                S.dma(e, xq[:, q8 % 2].rearrange("p (k t) -> p k t", t=64),
                                      fm(xT)[:, :, g * TB + q8 * 64 : g * TB + (q8 + 1) * 64], "xld")
                        with nc.Block() as blk:
                            @blk.vector
                            def _(v, q8=q8):
                                v.wait_ge(S.h["xld"], S.c["xld"])
                                rrep = bass.AP(bc_h, 512 + q8 * 64, [[1024, 128], [0, KD], [1, 64]])
                                mrep = bass.AP(bc_h, q8 * 64, [[1024, 128], [0, KD], [1, 64]])
                                v3 = lambda buf: buf.rearrange("p (k t) -> p k t", t=64)
                                S.flush(v, v.tensor_tensor(v3(ta[:]), v3(xq[:, q8 % 2]), rrep, ALU.mult))
                                v.tensor_tensor(xn_blk[:, :, q8 * 64 : (q8 + 1) * 64], v3(ta[:]), mrep, ALU.subtract)

                    for sub in range(4):
                        projs = [("q", None), ("k", 0), ("v", 0), ("k", 1), ("v", 1), ("k", 2), ("v", 2)]
                        base_mm = S.c["mm"]
                        base_ev = S.c["ev"]
                        with nc.Block() as blk:
                            @blk.tensor
                            def _(t, sub=sub, base_ev=base_ev):
                                for gi, (kind, i) in enumerate(projs):
                                    if gi >= 4:
                                        t.wait_ge(S.h["ev"], base_ev + gi - 3)
                                    sl = ps_proj[:, (gi % 4) * 512 : (gi % 4 + 1) * 512]
                                    w = wq_sb[:] if kind == "q" else (wk_sb[:, i] if kind == "k" else wv_sb[:, i])
                                    for k in range(KD):
                                        mmi = t.matmul(sl, xn_blk[:, k, sub * 128 : (sub + 1) * 128], w[:, k, :],
                                                       start=(k == 0), stop=(k == KD - 1))
                                    S.inc(mmi, "mm")
                                for k in range(KD):
                                    mmi = t.matmul(ps_s1[:, 0:3], xn_blk[:, k, sub * 128 : (sub + 1) * 128], wg3_sb[:, k, :],
                                                   start=(k == 0), stop=(k == KD - 1))
                                S.inc(mmi, "mm")

                            @blk.vector
                            def _(v, g=g, sub=sub, base_mm=base_mm):
                                prev = a1_st.get((g, sub - 1)) if sub > 0 else (a1_st.get((g - 1, 3)) if g > 0 else None)
                                if prev:
                                    v.wait_ge(S.h["st"], prev)
                                for gi, (kind, i) in enumerate(projs):
                                    v.wait_ge(S.h["mm"], base_mm + gi + 1)
                                    sl = ps_proj[:, (gi % 4) * 512 : (gi % 4 + 1) * 512]
                                    if kind == "q":
                                        ev = v.tensor_copy(qtok[:], sl)
                                    elif kind == "k":
                                        v.tensor_copy(ktok3[:, i], sl)
                                        ev = v.tensor_scalar(kt_s1[:, i * 512 : (i + 1) * 512], sl, decu_sb[:, i : i + 1], None, ALU.mult)
                                    else:
                                        ev = v.tensor_copy(v_s1[:, i * 512 : (i + 1) * 512], sl)
                                    S.inc(ev, "ev")
                                v.wait_ge(S.h["mm"], base_mm + 8)
                                v.tensor_copy(g3b[:, sub * 3 : sub * 3 + 3], ps_s1[:, 0:3])

                        base_mm = S.c["mm"]
                        base_ev = S.c["ev"]
                        with nc.Block() as blk:
                            @blk.tensor
                            def _(t, base_ev=base_ev):
                                for bi2, srcs in enumerate([[("q", None), ("k", 0)], [("k", 1), ("k", 2)]]):
                                    if bi2 == 1:
                                        t.wait_ge(S.h["ev"], base_ev + 1)
                                    for si, (kind, i) in enumerate(srcs):
                                        src2 = qtok[:] if kind == "q" else ktok3[:, i]
                                        for jj in range(ADT):
                                            mmi = t.transpose(ps_tp[:, (si * 4 + jj) * 128 : (si * 4 + jj + 1) * 128],
                                                              src2[:, jj * 128 : (jj + 1) * 128], eye_bf[:])
                                    S.inc(mmi, "mm")

                            @blk.vector
                            def _(v, base_mm=base_mm):
                                v.wait_ge(S.h["mm"], base_mm + 1)
                                v.tensor_copy(qfm_s1[:], ps_tp[:, 0:512])
                                ev = v.tensor_copy(kfm_s1[:, 0:512], ps_tp[:, 512:1024])
                                S.inc(ev, "ev")
                                v.wait_ge(S.h["mm"], base_mm + 2)
                                v.tensor_copy(kfm_s1[:, 512:1024], ps_tp[:, 0:512])
                                v.tensor_copy(kfm_s1[:, 1024:1536], ps_tp[:, 512:1024])

                        with nc.Block() as blk:
                            @blk.gpsimd
                            def _(gp, g=g, sub=sub):
                                S.dma(gp, qkv_s[g][:, sub, RQ : RQ + 512], qfm_s1[:], "st")
                                S.dma(gp, qkv_s[g][:, sub, RKF : RKF + 1536], kfm_s1[:], "st")
                                S.dma(gp, qkv_s[g][:, sub, RKT : RKT + 1536], kt_s1[:], "st")
                                S.dma(gp, qkv_s[g][:, sub, RV : RV + 1536], v_s1[:], "st")
                                a1_st[(g, sub)] = S.c["st"]

                    with nc.Block() as blk:
                        @blk.scalar
                        def _(sc):
                            sc.activation(e_blk[:], g3b[:], AF.Exp)
                    with nc.Block() as blk:
                        @blk.vector
                        def _(v):
                            F = lambda inst: S.flush(v, inst)
                            for sub in range(4):
                                F(v.reduce_sum(gsum[:, sub : sub + 1], e_blk[:, sub * 3 : sub * 3 + 3], axis=AX.X))
                            F(v.reciprocal(gsum[:, 4:8], gsum[:, 0:4]))
                            for sub in range(4):
                                F(v.tensor_scalar(gd_blk[:, sub, 0:3], e_blk[:, sub * 3 : sub * 3 + 3], gsum[:, 4 + sub : 5 + sub], None, ALU.mult))
                            for i in range(3):
                                F(v.tensor_scalar(gd_blk[:, :, 3 + i : 4 + i], gd_blk[:, :, i : i + 1], decu_sb[:, 3 + i : 4 + i], None, ALU.mult))
                    base_mm = S.c["mm"]
                    with nc.Block() as blk:
                        @blk.tensor
                        def _(t):
                            for sub in range(4):
                                mmi = t.transpose(ps_s2[0:6, sub * 128 : (sub + 1) * 128], gd_blk[:, sub, 0:6], eye_f[:])
                            S.inc(mmi, "mm")
                        @blk.vector
                        def _(v, base_mm=base_mm):
                            v.wait_ge(S.h["mm"], base_mm + 1)
                            v.tensor_copy(gdT_blk[0:6, :], ps_s2[0:6, 0:512])
                    for sub in range(4):
                        base_mm = S.c["mm"]
                        base_ev = S.c["ev"]
                        with nc.Block() as blk:
                            @blk.tensor
                            def _(t, sub=sub, base_ev=base_ev):
                                for n in range(6):
                                    if n >= 2:
                                        t.wait_ge(S.h["ev"], base_ev + n - 1)
                                    ps = ps_b1[:, 0:128] if n % 2 == 0 else ps_s1[:, 0:128]
                                    mmi = t.matmul(ps, selbf[0:6, n * 128 : (n + 1) * 128],
                                                   gdT_blk[0:6, sub * 128 : (sub + 1) * 128], start=True, stop=True)
                                    S.inc(mmi, "mm")
                            @blk.vector
                            def _(v, g=g, sub=sub, base_mm=base_mm):
                                prev = a1_st.get((g, sub - 1)) if sub > 0 else None
                                if prev:
                                    v.wait_ge(S.h["st"], prev)
                                for n in range(6):
                                    v.wait_ge(S.h["mm"], base_mm + n + 1)
                                    ps = ps_b1[:, 0:128] if n % 2 == 0 else ps_s1[:, 0:128]
                                    i = n % 3
                                    off = i * 128 if n < 3 else 384 + i * 128
                                    ev = v.tensor_copy(gb_s1[:, off : off + 128], ps)
                                    S.inc(ev, "ev")
                        with nc.Block() as blk:
                            @blk.gpsimd
                            def _(gp, g=g, sub=sub):
                                S.dma(gp, qkv_s[g][:, sub, RGB : RGB + 768], gb_s1[:], "st")
                                a1_st[(g, sub)] = S.c["st"]

            paA1w.close()

            with contextlib.ExitStack() as pa2:
                s2 = lambda name, shape, dt: pa2.enter_context(nc.sbuf_tensor(name, shape, dt))
                ps_big = pa2.enter_context(nc.psum_tensor("psA2_big", [128, 2048], F32))
                ps_sc = pa2.enter_context(nc.psum_tensor("psA2_sc", [128, 512], F32))
                ps_retr = pa2.enter_context(nc.psum_tensor("psA2_retr", [128, 512], F32))

                S_sb = s2("S_sb", [128, 3, ADT, AD], BF16)
                rec = s2("rec", [128, 2, RW], BF16)
                qt = s2("qt", [128, 3, AD], BF16)
                cmbA = s2("cmbA", [128, 3, 128], BF16)
                scb = s2("scb", [128, 3, 128], BF16)
                retrbf = s2("retrbf", [128, AD], BF16)
                aobuf = s2("aobuf", [128, 2048], F32)
                rec_h = rec.tensor if hasattr(rec, "tensor") else rec
                a2_store_at = {}
                a2_prog_after = {}
                a2_ld_val = {}
                with nc.Block() as blk:
                    @blk.vector
                    def _(v):
                        for i in range(3):
                            v.memset(S_sb[:, i].rearrange("p a b -> p (a b)"), 0.0)

                for ci in range(NCH):
                    c = NCH - 1 - ci
                    bi = ci % 2
                    roff = bi * RW

                    with nc.Block() as blk:
                        @blk.sync
                        def _(e, ci=ci, c=c):
                            if ci == 0:
                                e.wait_ge(S.h["st"], S.c["st"])
                                a2_ld_val[0] = S.dma(e, rec[:, 0], qkv_s[c // 4][:, c % 4, :], "ld")
                            if ci + 1 < NCH:
                                if ci >= 1:
                                    e.wait_ge(S.h["prog"], a2_prog_after[ci - 1])
                                cn = NCH - 1 - (ci + 1)
                                a2_ld_val[ci + 1] = S.dma(e, rec[:, (ci + 1) % 2], qkv_s[cn // 4][:, cn % 4, :], "ld")
                    with nc.Block() as blk:
                        @blk.vector
                        def _(v, ci=ci, bi=bi, roff=roff):
                            v.wait_ge(S.h["ld"], a2_ld_val[ci])
                            if ci >= 1:
                                v.wait_ge(S.h["st"], a2_store_at[ci - 1])
                            for i in range(3):
                                S.flush(v, v.tensor_tensor(cmbA[:, i], wloc[:, i], rec[:, bi, RGB + i * 128 : RGB + (i + 1) * 128], ALU.mult))
                                brep = bass.AP(rec_h, roff + RGB + 384 + i * 128, [[2 * RW, 128], [0, ADT], [1, 128]])
                                S.flush(v, v.tensor_tensor(qt[:, i], rec[:, bi, RQ : RQ + 512].rearrange("p (a b) -> p a b", b=128), brep, ALU.mult))

                    first_retr = [True]
                    for i in range(3):
                        base_mm = S.c["mm"]
                        with nc.Block() as blk:
                            @blk.tensor
                            def _(t, i=i, bi=bi):
                                for kd in range(ADT):
                                    mmi = t.matmul(ps_sc[:, 128:256], rec[:, bi, RKF + i * 512 + kd * 128 : RKF + i * 512 + (kd + 1) * 128],
                                                   rec[:, bi, RQ + kd * 128 : RQ + (kd + 1) * 128], start=(kd == 0), stop=(kd == ADT - 1))
                                S.inc(mmi, "mm")
                        with nc.Block() as blk:
                            @blk.vector
                            def _(v, i=i, base_mm=base_mm):
                                v.wait_ge(S.h["mm"], base_mm + 1)
                                v.tensor_tensor(scb[:, i], ps_sc[:, 128:256], cmbA[:, i], ALU.mult)
                        with nc.Block() as blk:
                            @blk.tensor
                            def _(t, i=i, bi=bi):
                                for m in range(ADT):
                                    t.matmul(ps_retr[:, m * 128 : (m + 1) * 128],
                                             rec[:, bi, RV + i * 512 + m * 128 : RV + i * 512 + (m + 1) * 128],
                                             scb[:, i], start=first_retr[0], stop=False)
                                    first_retr[0] = False
                                for m in range(ADT):
                                    for kd in range(ADT):
                                        last = (i == 2 and m == ADT - 1 and kd == ADT - 1)
                                        t.matmul(ps_retr[:, m * 128 : (m + 1) * 128],
                                                 S_sb[:, i, kd, m * 128 : (m + 1) * 128],
                                                 qt[:, i, kd * 128 : (kd + 1) * 128], start=False, stop=last)
                    with nc.Block() as blk:
                        @blk.vector
                        def _(v):
                            v.tensor_scalar(retrbf[:], ps_retr[:], osc_col[:], None, ALU.mult)

                    base_mm = S.c["mm"]
                    base_ev = S.c["ev"]
                    with nc.Block() as blk:
                        @blk.tensor
                        def _(t, ci=ci, bi=bi, base_ev=base_ev):
                            for i in range(3):
                                if i >= 1:
                                    t.wait_ge(S.h["ev"], base_ev + i)
                                for m in range(ADT):
                                    t.matmul(ps_big[:, m * 512 : (m + 1) * 512],
                                             rec[:, bi, RKT + i * 512 + m * 128 : RKT + i * 512 + (m + 1) * 128],
                                             rec[:, bi, RV + i * 512 : RV + (i + 1) * 512], start=True, stop=False)
                                    mmi = t.matmul(ps_big[:, m * 512 : (m + 1) * 512], Isc[:, i],
                                                   S_sb[:, i, m, :], start=False, stop=True)
                                S.inc(mmi, "mm")
                            t.nop().then_inc(S.h["prog"], 1)
                            S.c["prog"] += 1
                            a2_prog_after[ci] = S.c["prog"]

                        @blk.vector
                        def _(v, base_mm=base_mm):
                            for i in range(3):
                                v.wait_ge(S.h["mm"], base_mm + i + 1)
                                ev = v.tensor_copy(S_sb[:, i].rearrange("p a b -> p (a b)"), ps_big[:])
                                S.inc(ev, "ev")

                    with nc.Block() as blk:
                        @blk.tensor
                        def _(t):
                            for m in range(HT):
                                for kd in range(ADT):
                                    t.matmul(ps_big[:, m * 128 : (m + 1) * 128], wo_sb[:, kd, m * 128 : (m + 1) * 128],
                                             retrbf[:, kd * 128 : (kd + 1) * 128], start=(kd == 0), stop=(kd == ADT - 1))
                    with nc.Block() as blk:
                        @blk.vector
                        def _(v):
                            v.tensor_copy(aobuf[:], ps_big[:])
                        @blk.gpsimd
                        def _(gp, c=c, ci=ci):
                            S.dma(gp, fm(ao_s)[:, :, c * C : (c + 1) * C], aobuf[:].rearrange("p (k t) -> p k t", t=C), "st")
                            a2_store_at[ci] = S.c["st"]

            with contextlib.ExitStack() as pa3:
                s3 = lambda name, shape, dt: pa3.enter_context(nc.sbuf_tensor(name, shape, dt))
                ps_s1 = pa3.enter_context(nc.psum_tensor("psA3_s1", [128, 512], F32))
                ps_s2 = pa3.enter_context(nc.psum_tensor("psA3_s2", [128, 512], F32))
                ps_b1 = pa3.enter_context(nc.psum_tensor("psA3_b1", [128, 512], F32))
                ps_b2 = pa3.enter_context(nc.psum_tensor("psA3_b2", [128, 512], F32))

                aq = s3("aq", [128, 2, 2048], F32)
                xq3 = s3("xq3", [128, 2, 2048], F32)
                yq = s3("yq", [128, 2, 2048], F32)
                ta3 = s3("ta3", [128, 2048], F32)
                r0q = s3("r0q", [128, 2, 2048], F32)
                hib = s3("hib", [128, 2, 2048], BF16)
                lob = s3("lob", [128, 2, 2048], BF16)
                st1b = s3("st1b", [128, D], BF16)
                st2b = s3("st2b", [128, D], BF16)
                rows3 = s3("rows3", [1, 3584], F32)
                bc3 = s3("bc3", [128, 1024], F32)
                RSW3 = [rows3[0:1, i * 512 : (i + 1) * 512] for i in range(7)]
                bc3_h = bc3.tensor if hasattr(bc3, "tensor") else bc3
                a3st = {}
                y_store_after = {}

                for g in range(NB):
                    for q in range(4):
                        with nc.Block() as blk:
                            @blk.sync
                            def _(e, g=g, q=q):
                                if g == 0 and q == 0:
                                    e.wait_ge(S.h["st"], S.c["st"])
                                S.dma(e, aq[:, q % 2].rearrange("p (k t) -> p k t", t=C),
                                      fm(ao_s)[:, :, g * TB + q * C : g * TB + (q + 1) * C], "ld")
                        with nc.Block() as blk:
                            @blk.scalar
                            def _(sc, q=q):
                                sc.wait_ge(S.h["ld"], S.c["ld"])
                                sc.activation(st1b[:], aq[:, q % 2], AF.Identity)
                                sc.activation(st2b[:], aq[:, q % 2], AF.Square)
                        with nc.Block() as blk:
                            @blk.tensor
                            def _(t, q=q):
                                for k in range(KD):
                                    t.matmul(ps_s1[0:1, q * 128 : (q + 1) * 128], onescol[:], st1b[:, k * 128 : (k + 1) * 128],
                                             start=(k == 0), stop=(k == KD - 1))
                                for k in range(KD):
                                    t.matmul(ps_s2[0:1, q * 128 : (q + 1) * 128], onescol[:], st2b[:, k * 128 : (k + 1) * 128],
                                             start=(k == 0), stop=(k == KD - 1))
                    emit_rows512(None, ps_s1, ps_s2, rows3, RSW3)
                    with nc.Block() as blk:
                        @blk.tensor
                        def _(t):
                            t.matmul(ps_b1[:], ones1f[:], RSW3[5], start=True, stop=True)
                            t.matmul(ps_b2[:], ones1f[:], RSW3[6], start=True, stop=True)
                    with nc.Block() as blk:
                        @blk.vector
                        def _(v):
                            S.flush(v, v.tensor_copy(bc3[:, 0:512], ps_b1[:]))
                            S.flush(v, v.tensor_copy(bc3[:, 512:1024], ps_b2[:]))
                    for q in range(4):
                        with nc.Block() as blk:
                            @blk.sync
                            def _(e, g=g, q=q):
                                S.dma(e, aq[:, q % 2].rearrange("p (k t) -> p k t", t=C),
                                      fm(ao_s)[:, :, g * TB + q * C : g * TB + (q + 1) * C], "ld")
                                S.dma(e, xq3[:, q % 2].rearrange("p (k t) -> p k t", t=C),
                                      fm(xT)[:, :, g * TB + q * C : g * TB + (q + 1) * C], "ld")
                        with nc.Block() as blk:
                            @blk.vector
                            def _(v, q=q):
                                v.wait_ge(S.h["ld"], S.c["ld"])
                                rrep = bass.AP(bc3_h, 512 + q * 128, [[1024, 128], [0, KD], [1, 128]])
                                mrep = bass.AP(bc3_h, q * 128, [[1024, 128], [0, KD], [1, 128]])
                                a3 = lambda buf: buf.rearrange("p (k t) -> p k t", t=C)
                                S.flush(v, v.tensor_tensor(a3(ta3[:]), a3(aq[:, q % 2]), rrep, ALU.mult))
                                S.flush(v, v.tensor_tensor(a3(aq[:, q % 2]), a3(ta3[:]), mrep, ALU.subtract))
                                v.tensor_tensor(yq[:, q % 2], aq[:, q % 2], xq3[:, q % 2], ALU.add)
                        with nc.Block() as blk:
                            @blk.scalar
                            def _(sc, q=q):
                                sc.activation(st1b[:], yq[:, q % 2], AF.Identity)
                                sc.activation(st2b[:], yq[:, q % 2], AF.Square)
                            @blk.gpsimd
                            def _(gp, g=g, q=q):
                                S.dma(gp, fm(y_s)[:, :, g * TB + q * C : g * TB + (q + 1) * C],
                                      yq[:, q % 2].rearrange("p (k t) -> p k t", t=C), "st")
                                y_store_after[g] = S.c["st"]
                        with nc.Block() as blk:
                            @blk.tensor
                            def _(t, q=q):
                                for k in range(KD):
                                    t.matmul(ps_s1[0:1, q * 128 : (q + 1) * 128], onescol[:], st1b[:, k * 128 : (k + 1) * 128],
                                             start=(k == 0), stop=(k == KD - 1))
                                for k in range(KD):
                                    t.matmul(ps_s2[0:1, q * 128 : (q + 1) * 128], onescol[:], st2b[:, k * 128 : (k + 1) * 128],
                                             start=(k == 0), stop=(k == KD - 1))
                    emit_rows512(None, ps_s1, ps_s2, rows3, RSW3)
                    with nc.Block() as blk:
                        @blk.tensor
                        def _(t):
                            t.matmul(ps_b1[:], ones1f[:], RSW3[5], start=True, stop=True)
                            t.matmul(ps_b2[:], ones1f[:], RSW3[6], start=True, stop=True)
                    with nc.Block() as blk:
                        @blk.vector
                        def _(v):
                            S.flush(v, v.tensor_copy(bc3[:, 0:512], ps_b1[:]))
                            S.flush(v, v.tensor_copy(bc3[:, 512:1024], ps_b2[:]))
                    for q in range(4):
                        with nc.Block() as blk:
                            @blk.sync
                            def _(e, g=g, q=q):
                                e.wait_ge(S.h["st"], y_store_after[g])
                                S.dma(e, yq[:, q % 2].rearrange("p (k t) -> p k t", t=C),
                                      fm(y_s)[:, :, g * TB + q * C : g * TB + (q + 1) * C], "ld")
                        with nc.Block() as blk:
                            @blk.vector
                            def _(v, g=g, q=q):
                                v.wait_ge(S.h["ld"], S.c["ld"])
                                n = g * 4 + q
                                if n >= 2 and (n - 2) in a3st:
                                    v.wait_ge(S.h["st"], a3st[n - 2])
                                rrep = bass.AP(bc3_h, 512 + q * 128, [[1024, 128], [0, KD], [1, 128]])
                                mrep = bass.AP(bc3_h, q * 128, [[1024, 128], [0, KD], [1, 128]])
                                a3 = lambda buf: buf.rearrange("p (k t) -> p k t", t=C)
                                S.flush(v, v.tensor_tensor(a3(ta3[:]), a3(yq[:, q % 2]), rrep, ALU.mult))
                                S.flush(v, v.tensor_tensor(a3(r0q[:, q % 2]), a3(ta3[:]), mrep, ALU.subtract))
                                S.flush(v, v.tensor_copy(hib[:, q % 2], r0q[:, q % 2]))
                                v.tensor_tensor(lob[:, q % 2], r0q[:, q % 2], hib[:, q % 2], ALU.subtract)
                        with nc.Block() as blk:
                            @blk.gpsimd
                            def _(gp, g=g, q=q):
                                sl = lambda dr: fm(dr)[:, :, g * TB + q * C : g * TB + (q + 1) * C]
                                v3 = lambda buf: buf.rearrange("p (k t) -> p k t", t=C)
                                S.dma(gp, sl(r0_s), v3(r0q[:, q % 2]), "st")
                                S.dma(gp, sl(r0hi_s), v3(hib[:, q % 2]), "st")
                                S.dma(gp, sl(r0lo_s), v3(lob[:, q % 2]), "st")
                                a3st[g * 4 + q] = S.c["st"]

        with contextlib.ExitStack() as pb:
            sbb = lambda name, shape, dt: pb.enter_context(nc.sbuf_tensor(name, shape, dt))
            psB_big = pb.enter_context(nc.psum_tensor("psB_big", [128, 2048], F32))
            psB_w = pb.enter_context(nc.psum_tensor("psB_w", [128, 512], F32))
            psB_g = pb.enter_context(nc.psum_tensor("psB_g", [128, 512], F32))
            psB_s1 = pb.enter_context(nc.psum_tensor("psB_s1", [128, 512], F32))
            psB_s2 = pb.enter_context(nc.psum_tensor("psB_s2", [128, 512], F32))

            wub = sbb("wub", [128, 2, KD, 512], BF16)
            wswb = sbb("wswb", [128, 2, ET, 512], BF16)
            wdnb = sbb("wdnb", [128, 2, 4, ET, 128], BF16)
            xtu = sbb("xtu", [128, KD, TB], BF16)
            xtlo = sbb("xtlo", [128, KD, 128], BF16)
            hst = sbb("hst", [128, NU, ET, TB], BF16)
            pre = sbb("pre", [128, ET, TB], BF16)
            ssw = sbb("ssw", [128, 4, TB], BF16)
            wText = sbb("wText", [8, T], BF16)
            r0t = sbb("r0t", [128, 2, 512], F32)
            wbn = sbb("wbn", [128, TB], BF16)
            zsq = sbb("zsq", [128, 2, 512], BF16)
            zlold = sbb("zlold", [128, 2, 512], BF16)
            ytl = sbb("ytl", [128, 2, 512], F32)
            stage = sbb("stage", [128, 4, 512], F32)
            sthi = sbb("sthi", [128, 2, 512], BF16)
            stlo = sbb("stlo", [128, 2, 512], BF16)
            gg = sbb("gg", [128, 56], F32)
            browsB = sbb("browsB", [1, 1536], F32)

            for j in range(2):
                hi_src = r0hi_s if j == 0 else xt1hi_s
                lo_src = r0lo_s if j == 0 else xt1lo_s
                round_store_base = S.c["st"]

                with nc.Block() as blk:
                    @blk.vector
                    def _(v):
                        v.memset(wText[:], 1.0)
                for g in range(NB):
                    with nc.Block() as blk:
                        @blk.sync
                        def _(e, g=g, round_store_base=round_store_base):
                            e.wait_ge(S.h["st"], round_store_base)
                            S.dma(e, xtu[:], fm(hi_src)[:, :, g * TB : (g + 1) * TB], "ld")
                    ld_hi = S.c["ld"]
                    for s_ in range(4):
                        with nc.Block() as blk:
                            @blk.sync
                            def _(e, g=g, s_=s_):
                                S.dma(e, xtlo[:],
                                      fm(lo_src)[:, :, g * TB + s_ * 128 : g * TB + (s_ + 1) * 128], "ld")
                        ld_lo = S.c["ld"]
                        with nc.Block() as blk:
                            @blk.tensor
                            def _(t, s_=s_, j=j, ld_lo=ld_lo):
                                t.wait_ge(S.h["ld"], ld_lo)
                                ps = psB_g[:, 0:8]
                                for k in range(KD):
                                    t.matmul(ps, xtu[:, k, s_ * 128 : (s_ + 1) * 128], wghi[:, j, k, :],
                                             start=(k == 0), stop=False)
                                for k in range(KD):
                                    t.matmul(ps, xtu[:, k, s_ * 128 : (s_ + 1) * 128], wglo[:, j, k, :],
                                             start=False, stop=False)
                                for k in range(KD):
                                    t.matmul(ps, xtlo[:, k, :], wghi[:, j, k, :],
                                             start=False, stop=(k == KD - 1))
                        with nc.Block() as blk:
                            @blk.vector
                            def _(v, j=j):
                                F = lambda inst: S.flush(v, inst)
                                G8 = gg[:, 0:8]
                                F(v.tensor_tensor(G8, psB_g[:, 0:8], bgb[:, j * 8 : (j + 1) * 8], ALU.add))
                                cur = 8
                                for m in range(8):
                                    if m == 0:
                                        F(v.tensor_single_scalar(gg[:, 8:16], G8, G8[:, 0:1], ALU.is_lt))
                                    else:
                                        F(v.tensor_single_scalar(gg[:, 24:32], G8, G8[:, m : m + 1], ALU.is_lt))
                                        nxt = 24 - cur
                                        F(v.tensor_add(gg[:, nxt : nxt + 8], gg[:, cur : cur + 8], gg[:, 24:32]))
                                        cur = nxt
                                F(v.tensor_single_scalar(gg[:, 32:40], gg[:, cur : cur + 8], 5.5, ALU.is_lt))
                                F(v.reduce_max(gg[:, 48:49], G8, axis=AX.X, negate=True))
                        with nc.Block() as blk:
                            @blk.scalar
                            def _(sc):
                                sc.activation(gg[:, 40:48], gg[:, 0:8], AF.Exp, bias=gg[:, 48:49], scale=1.0)
                        with nc.Block() as blk:
                            @blk.vector
                            def _(v):
                                F = lambda inst: S.flush(v, inst)
                                F(v.tensor_mul(gg[:, 24:32], gg[:, 40:48], gg[:, 32:40]))
                                F(v.reduce_sum(gg[:, 49:50], gg[:, 24:32], axis=AX.X))
                                F(v.reciprocal(gg[:, 50:51], gg[:, 49:50]))
                                F(v.tensor_scalar(gg[:, 0:8], gg[:, 24:32], gg[:, 50:51], None, ALU.mult))
                        with nc.Block() as blk:
                            @blk.tensor
                            def _(t):
                                t.transpose(psB_g[0:8, 128:256], gg[:, 0:8], eye_f[:])
                        with nc.Block() as blk:
                            @blk.vector
                            def _(v, g=g, s_=s_):
                                v.tensor_copy(wText[0:6, g * TB + s_ * 128 : g * TB + (s_ + 1) * 128],
                                              psB_g[0:6, 128:256])

                for g in range(NB):
                    with nc.Block() as blk:
                        @blk.sync
                        def _(e, g=g):
                            S.dma(e, xtu[:], fm(hi_src)[:, :, g * TB : (g + 1) * TB], "ld")
                            for h in range(2):
                                S.dma(e, wub[:, h],
                                      wup_s[0].rearrange("(k p) e -> p k e", p=128)[:, :, h * 512 : (h + 1) * 512], "wld")
                                S.dma(e, wswb[:, h],
                                      wsw_s[0].rearrange("(k p) e -> p k e", p=128)[:, :, h * 512 : (h + 1) * 512], "wld")
                    xtu_ld = S.c["ld"]
                    wld_up = {(0, 0): S.c["wld"], (0, 1): S.c["wld"]}
                    wld_sw = {(0, 0): S.c["wld"], (0, 1): S.c["wld"]}

                    for n in range(NU):
                        if n < 6:
                            base_mm = S.c["mm"]
                            with nc.Block() as blk:
                                @blk.tensor
                                def _(t, n=n, g=g):
                                    mmi = t.matmul(psB_w[:], selbf[0:8, n * 128 : (n + 1) * 128],
                                                   wText[0:8, g * TB : (g + 1) * TB], start=True, stop=True)
                                    S.inc(mmi, "mm")
                                @blk.vector
                                def _(v, base_mm=base_mm):
                                    v.wait_ge(S.h["mm"], base_mm + 1)
                                    v.tensor_copy(wbn[:], psB_w[:])

                        base_mm = S.c["mm"]
                        base_ev = S.c["ev"]
                        with nc.Block() as blk:
                            @blk.sync
                            def _(e, n=n, base_mm=base_mm):
                                if n + 1 < NU:
                                    for h in range(2):
                                        e.wait_ge(S.h["mm"], base_mm + (h + 1) * 4)
                                        S.dma(e, wub[:, h],
                                              wup_s[n + 1].rearrange("(k p) e -> p k e", p=128)[:, :, h * 512 : (h + 1) * 512], "wld")
                                        wld_up[(n + 1, h)] = S.c["wld"]

                            @blk.tensor
                            def _(t, n=n, base_ev=base_ev, xtu_ld=xtu_ld):
                                t.wait_ge(S.h["ld"], xtu_ld)
                                cnt = 0
                                for half in range(2):
                                    t.wait_ge(S.h["wld"], wld_up[(n, half)])
                                    for lm in range(4):
                                        m = half * 4 + lm
                                        if cnt >= 4:
                                            t.wait_ge(S.h["ev"], base_ev + cnt - 3)
                                        sl = psB_big[:, (m % 4) * 512 : (m % 4 + 1) * 512]
                                        for k in range(KD):
                                            mmi = t.matmul(sl, wub[:, half, k, lm * 128 : (lm + 1) * 128],
                                                           xtu[:, k, :], start=(k == 0), stop=(k == KD - 1))
                                        S.inc(mmi, "mm")
                                        cnt += 1

                            @blk.scalar
                            def _(sc, n=n, base_mm=base_mm):
                                for m in range(ET):
                                    sc.wait_ge(S.h["mm"], base_mm + m + 1)
                                    sl = psB_big[:, (m % 4) * 512 : (m % 4 + 1) * 512]
                                    if n < 6:
                                        bcol = bcols[:, n * 8 + m : n * 8 + m + 1]
                                    else:
                                        bcol = bcols[:, 96 + (n - 6) * 8 + m : 96 + (n - 6) * 8 + m + 1]
                                    ev = sc.activation(pre[:, m, :], sl, AF.Identity, bias=bcol, scale=1.0)
                                    S.inc(ev, "ev")

                        base_mm = S.c["mm"]
                        base_ev = S.c["ev"]
                        base_pg = S.c["prog"]
                        with nc.Block() as blk:
                            @blk.sync
                            def _(e, n=n, base_mm=base_mm):
                                if n + 1 < NU:
                                    for h in range(2):
                                        e.wait_ge(S.h["mm"], base_mm + (h + 1) * 4)
                                        S.dma(e, wswb[:, h],
                                              wsw_s[n + 1].rearrange("(k p) e -> p k e", p=128)[:, :, h * 512 : (h + 1) * 512], "wld")
                                        wld_sw[(n + 1, h)] = S.c["wld"]

                            @blk.tensor
                            def _(t, n=n, base_ev=base_ev):
                                cnt = 0
                                for half in range(2):
                                    t.wait_ge(S.h["wld"], wld_sw[(n, half)])
                                    for lm in range(4):
                                        m = half * 4 + lm
                                        if cnt >= 4:
                                            t.wait_ge(S.h["ev"], base_ev + cnt - 3)
                                        sl = psB_big[:, (m % 4) * 512 : (m % 4 + 1) * 512]
                                        for k in range(ET):
                                            mmi = t.matmul(sl, wswb[:, half, k, lm * 128 : (lm + 1) * 128],
                                                           pre[:, k, :], start=(k == 0), stop=(k == ET - 1))
                                        S.inc(mmi, "mm")
                                        cnt += 1

                            @blk.scalar
                            def _(sc, n=n, base_mm=base_mm, base_pg=base_pg):
                                for m in range(ET):
                                    sc.wait_ge(S.h["mm"], base_mm + m + 1)
                                    if m >= 4:
                                        sc.wait_ge(S.h["prog"], base_pg + 1)
                                    sl = psB_big[:, (m % 4) * 512 : (m % 4 + 1) * 512]
                                    if n < 6:
                                        bcol = bcols[:, 48 + n * 8 + m : 48 + n * 8 + m + 1]
                                    else:
                                        bcol = bcols[:, 112 + m : 112 + m + 1]
                                    ev = sc.activation(ssw[:, m % 4, :], sl, AF.Silu, bias=bcol, scale=1.0)
                                    S.inc(ev, "ev")

                            @blk.vector
                            def _(v, n=n, base_ev=base_ev):
                                for half in range(2):
                                    v.wait_ge(S.h["ev"], base_ev + (half + 1) * 4)
                                    so = ssw[:].rearrange("p a t -> p (a t)")
                                    po = pre[:, half * 4 : (half + 1) * 4, :].rearrange("p a t -> p (a t)")
                                    ho = hst[:, n, half * 4 : (half + 1) * 4, :].rearrange("p a t -> p (a t)")
                                    pg = v.tensor_tensor(ho, so, po, ALU.mult)
                                    if n < 6:
                                        pg = v.tensor_tensor(ho, ho, rep_ap(wbn, 0, TB, 4, TB), ALU.mult)
                                    pg.then_inc(S.h["prog"], 1)
                                    S.c["prog"] += 1

                    base_mm = S.c["mm"]
                    base_ev = S.c["ev"]
                    st_pre = S.c["st"]
                    wdn_ld = {}
                    r0_ld = {}
                    with nc.Block() as blk:
                        @blk.sync
                        def _(e, g=g, base_mm=base_mm, base_ev=base_ev):
                            for m in range(HT):
                                for uh in range(2):
                                    if m >= 1:
                                        e.wait_ge(S.h["mm"], base_mm + m)
                                    S.dma(e, wdnb[:, uh],
                                          wdn_s[m, :, uh * 4 : (uh + 1) * 4, :, :], "wld")
                                    wdn_ld[(m, uh)] = S.c["wld"]
                                if m >= 2:
                                    e.wait_ge(S.h["ev"], base_ev + m - 1)
                                S.dma(e, r0t[:, m % 2, :], fm(r0_s)[:, m, g * TB : (g + 1) * TB], "ld")
                                r0_ld[m] = S.c["ld"]

                        @blk.tensor
                        def _(t, g=g, base_ev=base_ev):
                            for m in range(HT):
                                if m >= 4:
                                    t.wait_ge(S.h["ev"], base_ev + m - 3)
                                sl = psB_big[:, (m % 4) * 512 : (m % 4 + 1) * 512]
                                first = True
                                for uh in range(2):
                                    t.wait_ge(S.h["wld"], wdn_ld[(m, uh)])
                                    for lu in range(4):
                                        u = uh * 4 + lu
                                        for k in range(ET):
                                            t.matmul(sl, wdnb[:, uh, lu, k, :], hst[:, u, k, :],
                                                     start=first, stop=False)
                                            first = False
                                mmi = t.matmul(sl, bext_bf[0:7, m * 128 : (m + 1) * 128],
                                               wText[0:7, g * TB : (g + 1) * TB], start=False, stop=True)
                                S.inc(mmi, "mm")

                        @blk.vector
                        def _(v, g=g, j=j, base_mm=base_mm, st_pre=st_pre):
                            for m in range(HT):
                                v.wait_ge(S.h["mm"], base_mm + m + 1)
                                v.wait_ge(S.h["ld"], r0_ld[m])
                                if m >= 2:
                                    v.wait_ge(S.h["st"], st_pre + 32 * (m - 1))
                                sl = psB_big[:, (m % 4) * 512 : (m % 4 + 1) * 512]
                                zt = stage[:, m % 4, :]
                                S.flush(v, v.tensor_tensor(zt, sl, r0t[:, m % 2, :], ALU.add))
                                if j == 0:
                                    S.flush(v, v.tensor_copy(sthi[:, m % 2, :], zt))
                                    ev = v.tensor_tensor(stlo[:, m % 2, :], zt, sthi[:, m % 2, :], ALU.subtract)
                                else:
                                    S.flush(v, v.tensor_copy(xtu[:, m, :], zt))
                                    S.flush(v, v.tensor_copy(sthi[:, m % 2, :], xtu[:, m, :]))
                                    ev = v.tensor_tensor(stlo[:, m % 2, :], zt, xtu[:, m, :], ALU.subtract)
                                S.inc(ev, "ev")

                        @blk.gpsimd
                        def _(gp, g=g, base_ev=base_ev):
                            for m in range(HT):
                                gp.wait_ge(S.h["ev"], base_ev + m + 1)
                                dst = lambda dr: fm(dr)[:, m, g * TB : (g + 1) * TB]
                                S.dma(gp, dst(xt1hi_s), sthi[:, m % 2, :], "st")
                                S.dma(gp, dst(xt1lo_s), stlo[:, m % 2, :], "st")

                    if j == 1:
                        base_mm = S.c["mm"]
                        base_ev = S.c["ev"]
                        lo_ldv = {}
                        with nc.Block() as blk:
                            @blk.sync
                            def _(e, g=g, base_mm=base_mm):
                                for m in range(HT):
                                    if m >= 2:
                                        e.wait_ge(S.h["mm"], base_mm + m - 1)
                                    S.dma(e, zlold[:, m % 2, :], fm(xt1lo_s)[:, m, g * TB : (g + 1) * TB], "ld")
                                    lo_ldv[m] = S.c["ld"]

                            @blk.scalar
                            def _(sc, base_mm=base_mm):
                                for m in range(HT):
                                    if m >= 2:
                                        sc.wait_ge(S.h["mm"], base_mm + m - 1)
                                    ev = sc.activation(zsq[:, m % 2, :], xtu[:, m, :], AF.Square)
                                    S.inc(ev, "ev")

                            @blk.tensor
                            def _(t, base_ev=base_ev):
                                for m in range(HT):
                                    t.wait_ge(S.h["ld"], lo_ldv[m])
                                    t.wait_ge(S.h["ev"], base_ev + m + 1)
                                    t.matmul(psB_s1[0:1, 0:512], onescol[:], xtu[:, m, :],
                                             start=(m == 0), stop=False)
                                    t.matmul(psB_s1[0:1, 0:512], onescol[:], zlold[:, m % 2, :],
                                             start=False, stop=(m == HT - 1))
                                    mmi = t.matmul(psB_s2[0:1, 0:512], onescol[:], zsq[:, m % 2, :],
                                                   start=(m == 0), stop=(m == HT - 1))
                                    S.inc(mmi, "mm")

                        t0 = stage[0:1, 0, 0:512]; t1 = stage[0:1, 1, 0:512]
                        t2 = stage[0:1, 2, 0:512]; t3 = stage[0:1, 3, 0:512]
                        b0 = browsB[0:1, 0:512]; b1 = browsB[0:1, 512:1024]
                        b2 = browsB[0:1, 1024:1536]
                        with nc.Block() as blk:
                            @blk.vector
                            def _(v):
                                F = lambda inst: S.flush(v, inst)
                                F(v.tensor_scalar(t0, psB_s1[0:1, 0:512], 1.0 / D, None, ALU.mult))
                                F(v.tensor_scalar(t1, psB_s2[0:1, 0:512], 1.0 / D, None, ALU.mult))
                                F(v.tensor_mul(t2, t0, t0))
                                F(v.tensor_sub(t3, t1, t2))
                                F(v.tensor_scalar_add(b2, t3, EPS))
                        with nc.Block() as blk:
                            @blk.scalar
                            def _(sc):
                                S.flush(sc, sc.activation(t1, b2, AF.Ln))
                                S.flush(sc, sc.activation(t2, t1, AF.Exp, bias=0.0, scale=-0.5))
                        with nc.Block() as blk:
                            @blk.vector
                            def _(v):
                                F = lambda inst: S.flush(v, inst)
                                F(v.tensor_mul(t1, t2, t2))
                                F(v.tensor_mul(t3, t1, b2))
                                F(v.tensor_scalar(t1, t3, -0.5, 1.5, ALU.mult, ALU.add))
                                F(v.tensor_mul(b1, t2, t1))
                                F(v.tensor_mul(b0, t0, b1))
                        with nc.Block() as blk:
                            @blk.tensor
                            def _(t):
                                t.matmul(psB_s1[:, 0:512], ones1f[:], b0, start=True, stop=True)
                                t.matmul(psB_s2[:, 0:512], ones1f[:], b1, start=True, stop=True)

                        base_ev = S.c["ev"]
                        st_pre2 = S.c["st"]
                        y_ldv = {}
                        lo_ldv2 = {}
                        with nc.Block() as blk:
                            @blk.sync
                            def _(e, g=g, base_ev=base_ev):
                                for m in range(HT):
                                    if m >= 2:
                                        e.wait_ge(S.h["ev"], base_ev + m - 1)
                                    S.dma(e, ytl[:, m % 2, :], fm(y_s)[:, m, g * TB : (g + 1) * TB], "ld")
                                    y_ldv[m] = S.c["ld"]
                                    S.dma(e, zlold[:, m % 2, :], fm(xt1lo_s)[:, m, g * TB : (g + 1) * TB], "ld")
                                    lo_ldv2[m] = S.c["ld"]

                            @blk.vector
                            def _(v, g=g, base_ev=base_ev, st_pre2=st_pre2):
                                for m in range(HT):
                                    v.wait_ge(S.h["ld"], lo_ldv2[m])
                                    if m >= 2:
                                        v.wait_ge(S.h["st"], st_pre2 + 16 * (m - 1))
                                    sA = stage[:, 2 * (m % 2), :]
                                    sB = stage[:, 2 * (m % 2) + 1, :]
                                    S.flush(v, v.tensor_tensor(sA, xtu[:, m, :], zlold[:, m % 2, :], ALU.add))
                                    S.flush(v, v.tensor_mul(sB, sA, psB_s2[:, 0:512]))
                                    S.flush(v, v.tensor_sub(sA, sB, psB_s1[:, 0:512]))
                                    ev = v.tensor_tensor(sB, sA, ytl[:, m % 2, :], ALU.add)
                                    S.inc(ev, "ev")

                            @blk.gpsimd
                            def _(gp, g=g, base_ev=base_ev):
                                for m in range(HT):
                                    gp.wait_ge(S.h["ev"], base_ev + m + 1)
                                    S.dma(gp, fm(x2T)[:, m, g * TB : (g + 1) * TB], stage[:, 2 * (m % 2) + 1, :], "st")

            with nc.Block() as blk:
                @blk.gpsimd
                def _(gp):
                    gp.wait_ge(S.h["st"], S.c["st"])

    return nc


def _consts():
    u = np.arange(128)
    ubits = ((u[:, None] >> np.arange(7)[None, :]) & 1).astype(np.float32)
    c1 = 127 - u
    c1bits = ((c1[:, None] >> np.arange(7)[None, :]) & 1).astype(np.float32)
    s = np.arange(128)[:, None]
    t = np.arange(128)[None, :]
    expn = np.maximum(s - t - 1, 0)
    expnbits = np.concatenate([((expn >> b) & 1).astype(np.float32) for b in range(7)], axis=1).astype(BF)
    causT = (s > t).astype(np.float32).astype(BF)
    sel = np.zeros((8, 768), np.float32)
    for n in range(6):
        sel[n, n * 128 : (n + 1) * 128] = 1.0
    return dict(
        eye_bfd=np.eye(128, dtype=np.float32).astype(BF),
        eye_fd=np.eye(128, dtype=np.float32),
        ones1fd=np.ones((1, 128), np.float32),
        onescold=np.ones((128, 1), np.float32).astype(BF),
        selbfd=sel.astype(BF),
        ubitsd=ubits, c1bitsd=c1bits, expnbitsd=expnbits, causTd=causT,
    )


def _prep_inputs(inputs, T):
    f = lambda k: np.ascontiguousarray(np.asarray(inputs[k], np.float32))
    wg3 = f("L0_attn_Wg")
    wg3h = np.ascontiguousarray(wg3.reshape(16, 128, 3).transpose(1, 0, 2).reshape(128, 48))
    fg = f("L0_ffn_Wg")
    fgh = np.ascontiguousarray(fg.reshape(2, 16, 128, 8).transpose(0, 2, 1, 3).reshape(2, 128, 128))
    bup = f("L0_ffn_bup").reshape(6, 8, 128).transpose(2, 0, 1).reshape(128, 48)
    bsw = f("L0_ffn_bsw").reshape(6, 8, 128).transpose(2, 0, 1).reshape(128, 48)
    bsu = f("L0_ffn_bsu").reshape(2, 8, 128).transpose(2, 0, 1).reshape(128, 16)
    bsa = f("L0_ffn_bsa").reshape(8, 128).transpose(1, 0).reshape(128, 8)
    biascols = np.ascontiguousarray(np.concatenate([bup, bsw, bsu, bsa], axis=1))
    bext = np.ascontiguousarray(np.concatenate([f("L0_ffn_bdn"), f("L0_ffn_bsd").reshape(1, -1)], 0))
    common = dict(
        wq=f("L0_attn_Wq"), wk=f("L0_attn_Wk"), wv=f("L0_attn_Wv"), wo=f("L0_attn_Wo"),
        wg3h=wg3h, dlog=f("L0_attn_decay_logits").reshape(1, 3),
        oscale=np.asarray(inputs["L0_attn_out_scale"], np.float32).reshape(1, 1),
        fgh=fgh, bgh=f("L0_ffn_bg").reshape(1, 16),
        fwup=f("L0_ffn_Wup"), fwsw=f("L0_ffn_Wsw"), fwdn=f("L0_ffn_Wdn"),
        fwsu=f("L0_ffn_Wsu"), fwsa=f("L0_ffn_Wsa"), fwsd=f("L0_ffn_Wsd"),
        biascols=biascols, bexth=bext,
    )
    common.update(_consts())
    x = np.asarray(inputs["L0_attn_x"], np.float32)
    maps = []
    for b in range(x.shape[0]):
        m = dict(common)
        m["xT"] = np.ascontiguousarray(x[b, :T].T)
        maps.append(m)
    return maps


def kernel(**inputs):
    T = 4096
    nc = build_nc(T)
    in_maps = _prep_inputs(inputs, T)
    res = run_bass_kernel_spmd(nc, in_maps, core_ids=list(range(8)))
    out = np.stack([np.ascontiguousarray(np.asarray(r["x2T"]).T) for r in res.results], 0)
    return out.astype(np.float32)


# version 05
# speedup vs baseline: 1.1932x; 1.1932x over previous
import contextlib
import numpy as np
import ml_dtypes
import concourse.bass as bass
import concourse.mybir as mybir
from concourse.bass_utils import run_bass_kernel_spmd

F32 = mybir.dt.float32
BF16 = mybir.dt.bfloat16
AF = mybir.ActivationFunctionType
ALU = mybir.AluOpType
AX = mybir.AxisListType
BF = ml_dtypes.bfloat16

D = 2048
AD = 512
KD = 16
ADT = 4
C = 128
E = 1024
ET = 8
NU = 8
HT = 16
TB = 512
EPS = 1e-5


class Sems:
    def __init__(self, nc, stack, names):
        self.h = {}
        self.c = {}
        for n in names:
            self.h[n] = stack.enter_context(nc.semaphore(n))
            self.c[n] = 0

    def dma(self, eng, dst, src, name):
        eng.dma_start(dst, src).then_inc(self.h[name], 16)
        self.c[name] += 16
        return self.c[name]

    def inc(self, inst, name):
        inst.then_inc(self.h[name], 1)
        self.c[name] += 1
        return self.c[name]

    def flush(self, eng, inst, name="fl"):
        inst.then_inc(self.h[name], 1)
        self.c[name] += 1
        eng.wait_ge(self.h[name], self.c[name])


def rep_ap(t, offset_elems, pstride, reps, inner):
    h = t.tensor if hasattr(t, "tensor") else t
    return bass.AP(h, offset_elems, [[pstride, 128], [0, reps], [1, inner]])


def build_nc(T, dbg=False, extk=()):
    def KINDF(name):
        if dbg or name in extk:
            return dict(kind="ExternalOutput")
        return {}
    NCH = T // C
    NB = T // TB
    nc = bass.Bass("TRN2", target_bir_lowering=False)

    xT = nc.dram_tensor("xT", [D, T], F32, kind="ExternalInput")
    wq = nc.dram_tensor("wq", [D, AD], F32, kind="ExternalInput")
    wk = nc.dram_tensor("wk", [3, D, AD], F32, kind="ExternalInput")
    wv = nc.dram_tensor("wv", [3, D, AD], F32, kind="ExternalInput")
    wo = nc.dram_tensor("wo", [AD, D], F32, kind="ExternalInput")
    wg3h = nc.dram_tensor("wg3h", [128, 48], F32, kind="ExternalInput")
    dlog = nc.dram_tensor("dlog", [1, 3], F32, kind="ExternalInput")
    oscale = nc.dram_tensor("oscale", [1, 1], F32, kind="ExternalInput")
    fgh = nc.dram_tensor("fgh", [2, 128, 128], F32, kind="ExternalInput")
    bgh = nc.dram_tensor("bgh", [1, 16], F32, kind="ExternalInput")
    fwup = nc.dram_tensor("fwup", [6, D, E], F32, kind="ExternalInput")
    fwsw = nc.dram_tensor("fwsw", [6, E, E], F32, kind="ExternalInput")
    fwdn = nc.dram_tensor("fwdn", [6, E, D], F32, kind="ExternalInput")
    fwsu = nc.dram_tensor("fwsu", [D, D], F32, kind="ExternalInput")
    fwsa = nc.dram_tensor("fwsa", [E, E], F32, kind="ExternalInput")
    fwsd = nc.dram_tensor("fwsd", [D, D], F32, kind="ExternalInput")
    biascols = nc.dram_tensor("biascols", [128, 120], F32, kind="ExternalInput")
    bexth = nc.dram_tensor("bexth", [7, D], F32, kind="ExternalInput")
    eye_bf_d = nc.dram_tensor("eye_bfd", [128, 128], BF16, kind="ExternalInput")
    eye_f_d = nc.dram_tensor("eye_fd", [128, 128], F32, kind="ExternalInput")
    ones1f_d = nc.dram_tensor("ones1fd", [1, 128], F32, kind="ExternalInput")
    onescol_d = nc.dram_tensor("onescold", [128, 1], BF16, kind="ExternalInput")
    selbf_d = nc.dram_tensor("selbfd", [8, 768], BF16, kind="ExternalInput")
    ubits_d = nc.dram_tensor("ubitsd", [128, 7], F32, kind="ExternalInput")
    c1bits_d = nc.dram_tensor("c1bitsd", [128, 7], F32, kind="ExternalInput")
    expnbits_d = nc.dram_tensor("expnbitsd", [128, 896], BF16, kind="ExternalInput")
    causT_d = nc.dram_tensor("causTd", [128, 128], BF16, kind="ExternalInput")

    y_s = nc.dram_tensor("y_s", [D, T], F32, **KINDF("y_s"))
    r0_s = nc.dram_tensor("r0_s", [D, T], F32, **KINDF("r0_s"))
    r0hi_s = nc.dram_tensor("r0hi_s", [D, T], BF16, **KINDF("r0hi_s"))
    r0lo_s = nc.dram_tensor("r0lo_s", [D, T], BF16, **KINDF("r0lo_s"))
    xt1hi_s = nc.dram_tensor("xt1hi_s", [D, T], BF16, **KINDF("xt1hi_s"))
    xt1lo_s = nc.dram_tensor("xt1lo_s", [D, T], BF16, **KINDF("xt1lo_s"))
    wup_s = nc.dram_tensor("wup_s", [NU, D, E], BF16)
    wsw_s = nc.dram_tensor("wsw_s", [NU, E, E], BF16)
    wdn_s = nc.dram_tensor("wdn_s", [HT, 128, NU, ET, 128], BF16)
    x2T = nc.dram_tensor("x2T", [D, T], F32, kind="ExternalOutput")
    dbgout = nc.dram_tensor("dbgout", [128, 32768], F32, kind="ExternalOutput") if dbg else None

    def fm(dr):
        return dr.rearrange("(k p) t -> p k t", p=128)

    with contextlib.ExitStack() as top:
        S = Sems(nc, top, ["ld", "st", "xld", "wld", "dld", "mm", "ev", "evv", "prog", "fl"])
        sb = lambda name, shape, dt: top.enter_context(nc.sbuf_tensor(name, shape, dt))

        eye_bf = sb("eye_bf", [128, 128], BF16)
        eye_f = sb("eye_f", [128, 128], F32)
        ones1f = sb("ones1f", [1, 128], F32)
        onescol = sb("onescol", [128, 1], BF16)
        selbf = sb("selbf", [8, 768], BF16)
        wghi = sb("wghi", [128, 2, 16, 8], BF16)
        wglo = sb("wglo", [128, 2, 16, 8], BF16)
        bgb = sb("bgb", [128, 16], F32)
        bcols = sb("bcols", [128, 120], F32)
        bext_bf = sb("bext_bf", [8, D], BF16)
        osc_col = sb("osc_col", [128, 1], F32)

        with contextlib.ExitStack() as pa:
            sa = lambda name, shape, dt: pa.enter_context(nc.sbuf_tensor(name, shape, dt))

            wo_sb = sa("wo_sb", [128, ADT, D], BF16)
            wloc = sa("wloc", [128, 3, 128], BF16)
            cmb = sa("cmb", [128, 3, 128], BF16)
            Isc = sa("Isc", [128, 3, 128], BF16)
            wtmp = sa("wtmp", [128, 128], BF16)
            dvec = sa("dvec", [128, 12], F32)
            dvecs = [dvec[:, i : i + 1] for i in range(12)]
            decu_sb = sa("decu_sb", [128, 8], F32)
            rowd = sa("rowd", [1, 4], F32)
            paA1w = pa.enter_context(contextlib.ExitStack())
            sw_ = lambda name, shape, dt: paA1w.enter_context(nc.sbuf_tensor(name, shape, dt))
            wq_sb = sw_("wq_sb", [128, KD, AD], BF16)
            wk_sb = sw_("wk_sb", [128, 3, KD, AD], BF16)
            wv_sb = sw_("wv_sb", [128, 3, KD, AD], BF16)
            wg3_sb = sw_("wg3_sb", [128, KD, 3], BF16)

            with contextlib.ExitStack() as ws:
                psA_misc = ws.enter_context(nc.psum_tensor("psW_misc", [128, 512], F32))
                scr_f = ws.enter_context(nc.sbuf_tensor("scr_f", [128, 160], F32))
                ubits = ws.enter_context(nc.sbuf_tensor("ubits", [128, 7], F32))
                c1bits = ws.enter_context(nc.sbuf_tensor("c1bits", [128, 7], F32))
                expnbits = ws.enter_context(nc.sbuf_tensor("expnbits", [128, 896], BF16))
                causT = ws.enter_context(nc.sbuf_tensor("causT", [128, 128], BF16))

                with nc.Block() as blk:
                    @blk.vector
                    def _(v):
                        v.memset(bext_bf[:], 0.0)
                with nc.Block() as blk:
                    @blk.sync
                    def _(e):
                        for dst, src in [
                            (eye_bf[:], eye_bf_d), (eye_f[:], eye_f_d), (ones1f[:], ones1f_d),
                            (onescol[:], onescol_d), (selbf[:], selbf_d), (ubits[:], ubits_d),
                            (c1bits[:], c1bits_d), (expnbits[:], expnbits_d), (causT[:], causT_d),
                            (bcols[:], biascols), (rowd[0:1, 0:3], dlog),
                            (scr_f[0:1, 0:1], oscale), (scr_f[0:1, 1:17], bgh),
                        ]:
                            S.dma(e, dst, src[:], "ld")

                    @blk.gpsimd
                    def _(g):
                        S.dma(g, bext_bf[0:7, :], bexth[:], "st")
                        S.dma(g, wq_sb[:], wq.rearrange("(k p) d -> p k d", p=128), "st")
                        for i in range(3):
                            S.dma(g, wk_sb[:, i], wk[i].rearrange("(k p) d -> p k d", p=128), "st")
                            S.dma(g, wv_sb[:, i], wv[i].rearrange("(k p) d -> p k d", p=128), "st")
                        S.dma(g, wo_sb[:], wo.rearrange("(k p) d -> p k d", p=128), "st")
                        S.dma(g, wg3_sb[:], wg3h.rearrange("p (k d) -> p k d", d=3), "st")
                        for u in range(NU):
                            if u < 6:
                                S.dma(g, wup_s[u].rearrange("(k p) e -> p k e", p=128),
                                      fwup[u].rearrange("(k p) e -> p k e", p=128), "st")
                                S.dma(g, wsw_s[u].rearrange("(k p) e -> p k e", p=128),
                                      fwsw[u].rearrange("(k p) e -> p k e", p=128), "st")
                                dnsrc = fwdn[u].rearrange("(k p) (m j) -> p k m j", p=128, j=128)
                            else:
                                S.dma(g, wup_s[u].rearrange("(k p) e -> p k e", p=128),
                                      fwsu.rearrange("(k p) e -> p k e", p=128)[:, :, (u - 6) * E : (u - 5) * E], "st")
                                S.dma(g, wsw_s[u].rearrange("(k p) e -> p k e", p=128),
                                      fwsa.rearrange("(k p) e -> p k e", p=128), "st")
                                dnsrc = fwsd[(u - 6) * E : (u - 5) * E, :].rearrange("(k p) (m j) -> p k m j", p=128, j=128)
                            for kk in range(ET):
                                S.dma(g, wdn_s[:, :, u, kk, :].rearrange("m p j -> p m j"),
                                      dnsrc[:, kk], "st")

                with nc.Block() as blk:
                    @blk.tensor
                    def _(t):
                        t.wait_ge(S.h["ld"], S.c["ld"])
                        t.matmul(psA_misc[:, 0:1], ones1f[:], scr_f[0:1, 0:1], start=True, stop=True)
                        t.matmul(psA_misc[:, 16:32], ones1f[:], scr_f[0:1, 1:17], start=True, stop=True)
                with nc.Block() as blk:
                    @blk.vector
                    def _(v):
                        v.tensor_copy(osc_col[:], psA_misc[:, 0:1])
                        v.tensor_copy(bgb[:], psA_misc[:, 16:32])
                with nc.Block() as blk:
                    @blk.vector
                    def _(v):
                        v.wait_ge(S.h["st"], S.c["st"])

                for j in range(2):
                    with nc.Block() as blk:
                        @blk.sync
                        def _(e, j=j):
                            S.dma(e, scr_f[:, 32:160], fgh[j], "ld")
                    with nc.Block() as blk:
                        @blk.vector
                        def _(v, j=j):
                            v.wait_ge(S.h["ld"], S.c["ld"])
                            hi = wghi[:, j].rearrange("p k d -> p (k d)")
                            lo = wglo[:, j].rearrange("p k d -> p (k d)")
                            v.tensor_copy(hi, scr_f[:, 32:160])
                            v.tensor_tensor(lo, scr_f[:, 32:160], hi, ALU.subtract)

                for i in range(3):
                    with nc.Block() as blk:
                        @blk.tensor
                        def _(t, i=i):
                            t.matmul(psA_misc[:, 32:33], ones1f[:], rowd[0:1, i : i + 1], start=True, stop=True)
                    with nc.Block() as blk:
                        @blk.scalar
                        def _(sc):
                            sc.activation(dvecs[0], psA_misc[:, 32:33], AF.Sigmoid)
                    with nc.Block() as blk:
                        @blk.vector
                        def _(v, i=i):
                            F = lambda inst: S.flush(v, inst)
                            for b in range(1, 7):
                                F(v.tensor_mul(dvecs[b], dvecs[b - 1], dvecs[b - 1]))
                            F(v.tensor_mul(dvecs[7], dvecs[6], dvecs[6]))
                            for tgt, bits in ((i, ubits), (3 + i, c1bits)):
                                cur = 8
                                for b in range(7):
                                    F(v.tensor_scalar_add(dvecs[10], dvecs[b], -1.0))
                                    F(v.tensor_scalar(dvecs[11], bits[:, b : b + 1], dvecs[10], 1.0, ALU.mult, ALU.add))
                                    if b == 0:
                                        F(v.tensor_copy(dvecs[cur], dvecs[11]))
                                    else:
                                        F(v.tensor_mul(dvecs[cur ^ 1], dvecs[cur], dvecs[11]))
                                        cur ^= 1
                                F(v.tensor_copy(decu_sb[:, tgt : tgt + 1], dvecs[cur]))
                            F(v.tensor_copy(wloc[:, i], causT[:]))
                            cur = 0
                            for b in range(7):
                                F(v.tensor_scalar_add(dvecs[10], dvecs[b], -1.0))
                                F(v.tensor_scalar(wtmp[:], expnbits[:, b * 128 : (b + 1) * 128], dvecs[10], 1.0, ALU.mult, ALU.add))
                                if cur == 0:
                                    F(v.tensor_mul(cmb[:, i], wloc[:, i], wtmp[:]))
                                else:
                                    F(v.tensor_mul(wloc[:, i], cmb[:, i], wtmp[:]))
                                cur ^= 1
                            if cur == 1:
                                F(v.tensor_copy(wloc[:, i], cmb[:, i]))
                            F(v.tensor_scalar(Isc[:, i], eye_bf[:], dvecs[7], None, ALU.mult))

            RQ, RKF, RKT, RV, RGB, RW = 0, 512, 2048, 3584, 5120, 5888
            qkv_s = nc.dram_tensor("qkv_s", [NB, 128, 4, RW], BF16)
            ao_s = nc.dram_tensor("ao_s", [D, T], F32)

            def emit_rows512(v_or_none, ps_sum, ps_sq, rows, RSW):
                with nc.Block() as blk:
                    @blk.vector
                    def _(v):
                        F = lambda inst: S.flush(v, inst)
                        F(v.tensor_scalar(RSW[0], ps_sum[0:1, 0:512], 1.0 / D, None, ALU.mult))
                        F(v.tensor_scalar(RSW[1], ps_sq[0:1, 0:512], 1.0 / D, None, ALU.mult))
                        F(v.tensor_mul(RSW[2], RSW[0], RSW[0]))
                        F(v.tensor_sub(RSW[4], RSW[1], RSW[2]))
                        F(v.tensor_scalar_add(RSW[3], RSW[4], EPS))
                with nc.Block() as blk:
                    @blk.scalar
                    def _(sc):
                        S.flush(sc, sc.activation(RSW[2], RSW[3], AF.Ln))
                        S.flush(sc, sc.activation(RSW[4], RSW[2], AF.Exp, bias=0.0, scale=-0.5))
                with nc.Block() as blk:
                    @blk.vector
                    def _(v):
                        F = lambda inst: S.flush(v, inst)
                        F(v.tensor_mul(RSW[1], RSW[4], RSW[4]))
                        F(v.tensor_mul(RSW[2], RSW[1], RSW[3]))
                        F(v.tensor_scalar(RSW[1], RSW[2], -0.5, 1.5, ALU.mult, ALU.add))
                        F(v.tensor_mul(RSW[6], RSW[4], RSW[1]))
                        F(v.tensor_mul(RSW[5], RSW[0], RSW[6]))

            with contextlib.ExitStack() as pa1:
                s1 = lambda name, shape, dt: pa1.enter_context(nc.sbuf_tensor(name, shape, dt))
                ps_proj = pa1.enter_context(nc.psum_tensor("ps_proj", [128, 2048], F32))
                ps_tp = pa1.enter_context(nc.psum_tensor("ps_tp", [128, 1024], BF16))
                ps_s1 = pa1.enter_context(nc.psum_tensor("ps_s1", [128, 512], F32))
                ps_s2 = pa1.enter_context(nc.psum_tensor("ps_s2", [128, 512], F32))
                ps_b1 = pa1.enter_context(nc.psum_tensor("ps_b1", [128, 512], F32))

                xq = s1("xq", [128, 2, 1024], F32)
                st1 = s1("st1", [128, 1024], BF16)
                st2 = s1("st2", [128, 1024], BF16)
                ta = s1("ta", [128, 1024], F32)
                xn_blk = s1("xn_blk", [128, KD, 512], BF16)
                rows = s1("rows", [1, 2560], F32)
                bc_blk = s1("bc_blk", [128, 1024], F32)
                qtok = s1("qtok", [128, AD], BF16)
                ktok3 = s1("ktok3", [128, 3, AD], BF16)
                qfm_s1 = s1("qfm_s1", [128, 512], BF16)
                kfm_s1 = s1("kfm_s1", [128, 1536], BF16)
                kt_s1 = s1("kt_s1", [128, 1536], BF16)
                v_s1 = s1("v_s1", [128, 1536], BF16)
                gb_s1 = s1("gb_s1", [128, 768], BF16)
                g3b = s1("g3b", [128, 12], F32)
                e_blk = s1("e_blk", [128, 12], F32)
                gsum = s1("gsum", [128, 8], F32)
                gd_blk = s1("gd_blk", [128, 4, 6], F32)
                gdT_blk = s1("gdT_blk", [8, 512], BF16)
                RSA = [rows[0:1, i * 512 : (i + 1) * 512] for i in range(5)]
                bc_h = bc_blk.tensor if hasattr(bc_blk, "tensor") else bc_blk
                a1_st = {}

                def rows5_and_bcast(psum1, psum2):
                    with nc.Block() as blk:
                        @blk.vector
                        def _(v):
                            F = lambda inst: S.flush(v, inst)
                            F(v.tensor_scalar(RSA[0], psum1[0:1, 0:512], 1.0 / D, None, ALU.mult))
                            F(v.tensor_scalar(RSA[1], psum2[0:1, 0:512], 1.0 / D, None, ALU.mult))
                            F(v.tensor_mul(RSA[2], RSA[0], RSA[0]))
                            F(v.tensor_sub(RSA[3], RSA[1], RSA[2]))
                            F(v.tensor_scalar_add(RSA[1], RSA[3], EPS))
                    with nc.Block() as blk:
                        @blk.scalar
                        def _(sc):
                            S.flush(sc, sc.activation(RSA[2], RSA[1], AF.Ln))
                            S.flush(sc, sc.activation(RSA[3], RSA[2], AF.Exp, bias=0.0, scale=-0.5))
                    with nc.Block() as blk:
                        @blk.vector
                        def _(v):
                            F = lambda inst: S.flush(v, inst)
                            F(v.tensor_mul(RSA[2], RSA[3], RSA[3]))
                            F(v.tensor_mul(RSA[4], RSA[2], RSA[1]))
                            F(v.tensor_scalar(RSA[2], RSA[4], -0.5, 1.5, ALU.mult, ALU.add))
                            F(v.tensor_mul(RSA[4], RSA[3], RSA[2]))
                            F(v.tensor_mul(RSA[3], RSA[0], RSA[4]))
                    with nc.Block() as blk:
                        @blk.tensor
                        def _(t):
                            t.matmul(ps_b1[:], ones1f[:], RSA[3], start=True, stop=True)
                            t.matmul(psum1[:, 0:512], ones1f[:], RSA[4], start=True, stop=True)
                    with nc.Block() as blk:
                        @blk.vector
                        def _(v):
                            S.flush(v, v.tensor_copy(bc_blk[:, 0:512], ps_b1[:]))
                            S.flush(v, v.tensor_copy(bc_blk[:, 512:1024], psum1[:, 0:512]))

                for g in range(NB):
                    for q8 in range(8):
                        with nc.Block() as blk:
                            @blk.sync
                            def _(e, g=g, q8=q8):
                                S.dma(e, xq[:, q8 % 2].rearrange("p (k t) -> p k t", t=64),
                                      fm(xT)[:, :, g * TB + q8 * 64 : g * TB + (q8 + 1) * 64], "xld")
                        with nc.Block() as blk:
                            @blk.scalar
                            def _(sc, q8=q8):
                                sc.wait_ge(S.h["xld"], S.c["xld"])
                                sc.activation(st1[:], xq[:, q8 % 2], AF.Identity)
                                sc.activation(st2[:], xq[:, q8 % 2], AF.Square)
                        with nc.Block() as blk:
                            @blk.tensor
                            def _(t, q8=q8):
                                for k in range(KD):
                                    t.matmul(ps_s1[0:1, q8 * 64 : (q8 + 1) * 64], onescol[:], st1[:, k * 64 : (k + 1) * 64],
                                             start=(k == 0), stop=(k == KD - 1))
                                for k in range(KD):
                                    t.matmul(ps_s2[0:1, q8 * 64 : (q8 + 1) * 64], onescol[:], st2[:, k * 64 : (k + 1) * 64],
                                             start=(k == 0), stop=(k == KD - 1))
                    rows5_and_bcast(ps_s1, ps_s2)
                    for q8 in range(8):
                        with nc.Block() as blk:
                            @blk.sync
                            def _(e, g=g, q8=q8):
                                S.dma(e, xq[:, q8 % 2].rearrange("p (k t) -> p k t", t=64),
                                      fm(xT)[:, :, g * TB + q8 * 64 : g * TB + (q8 + 1) * 64], "xld")
                        with nc.Block() as blk:
                            @blk.vector
                            def _(v, q8=q8):
                                v.wait_ge(S.h["xld"], S.c["xld"])
                                rrep = bass.AP(bc_h, 512 + q8 * 64, [[1024, 128], [0, KD], [1, 64]])
                                mrep = bass.AP(bc_h, q8 * 64, [[1024, 128], [0, KD], [1, 64]])
                                v3 = lambda buf: buf.rearrange("p (k t) -> p k t", t=64)
                                S.flush(v, v.tensor_tensor(v3(ta[:]), v3(xq[:, q8 % 2]), rrep, ALU.mult))
                                v.tensor_tensor(xn_blk[:, :, q8 * 64 : (q8 + 1) * 64], v3(ta[:]), mrep, ALU.subtract)

                    for sub in range(4):
                        projs = [("q", None), ("k", 0), ("v", 0), ("k", 1), ("v", 1), ("k", 2), ("v", 2)]
                        base_mm = S.c["mm"]
                        base_ev = S.c["ev"]
                        with nc.Block() as blk:
                            @blk.tensor
                            def _(t, sub=sub, base_ev=base_ev):
                                for gi, (kind, i) in enumerate(projs):
                                    if gi >= 4:
                                        t.wait_ge(S.h["ev"], base_ev + gi - 3)
                                    sl = ps_proj[:, (gi % 4) * 512 : (gi % 4 + 1) * 512]
                                    w = wq_sb[:] if kind == "q" else (wk_sb[:, i] if kind == "k" else wv_sb[:, i])
                                    for k in range(KD):
                                        mmi = t.matmul(sl, xn_blk[:, k, sub * 128 : (sub + 1) * 128], w[:, k, :],
                                                       start=(k == 0), stop=(k == KD - 1))
                                    S.inc(mmi, "mm")
                                for k in range(KD):
                                    mmi = t.matmul(ps_s1[:, 0:3], xn_blk[:, k, sub * 128 : (sub + 1) * 128], wg3_sb[:, k, :],
                                                   start=(k == 0), stop=(k == KD - 1))
                                S.inc(mmi, "mm")

                            @blk.vector
                            def _(v, g=g, sub=sub, base_mm=base_mm):
                                prev = a1_st.get((g, sub - 1)) if sub > 0 else (a1_st.get((g - 1, 3)) if g > 0 else None)
                                if prev:
                                    v.wait_ge(S.h["st"], prev)
                                for gi, (kind, i) in enumerate(projs):
                                    v.wait_ge(S.h["mm"], base_mm + gi + 1)
                                    sl = ps_proj[:, (gi % 4) * 512 : (gi % 4 + 1) * 512]
                                    if kind == "q":
                                        ev = v.tensor_copy(qtok[:], sl)
                                    elif kind == "k":
                                        v.tensor_copy(ktok3[:, i], sl)
                                        ev = v.tensor_scalar(kt_s1[:, i * 512 : (i + 1) * 512], sl, decu_sb[:, i : i + 1], None, ALU.mult)
                                    else:
                                        ev = v.tensor_copy(v_s1[:, i * 512 : (i + 1) * 512], sl)
                                    S.inc(ev, "ev")
                                v.wait_ge(S.h["mm"], base_mm + 8)
                                v.tensor_copy(g3b[:, sub * 3 : sub * 3 + 3], ps_s1[:, 0:3])

                        base_mm = S.c["mm"]
                        base_ev = S.c["ev"]
                        with nc.Block() as blk:
                            @blk.tensor
                            def _(t, base_ev=base_ev):
                                for bi2, srcs in enumerate([[("q", None), ("k", 0)], [("k", 1), ("k", 2)]]):
                                    if bi2 == 1:
                                        t.wait_ge(S.h["ev"], base_ev + 1)
                                    for si, (kind, i) in enumerate(srcs):
                                        src2 = qtok[:] if kind == "q" else ktok3[:, i]
                                        for jj in range(ADT):
                                            mmi = t.transpose(ps_tp[:, (si * 4 + jj) * 128 : (si * 4 + jj + 1) * 128],
                                                              src2[:, jj * 128 : (jj + 1) * 128], eye_bf[:])
                                    S.inc(mmi, "mm")

                            @blk.vector
                            def _(v, base_mm=base_mm):
                                v.wait_ge(S.h["mm"], base_mm + 1)
                                v.tensor_copy(qfm_s1[:], ps_tp[:, 0:512])
                                ev = v.tensor_copy(kfm_s1[:, 0:512], ps_tp[:, 512:1024])
                                S.inc(ev, "ev")
                                v.wait_ge(S.h["mm"], base_mm + 2)
                                v.tensor_copy(kfm_s1[:, 512:1024], ps_tp[:, 0:512])
                                v.tensor_copy(kfm_s1[:, 1024:1536], ps_tp[:, 512:1024])

                        with nc.Block() as blk:
                            @blk.gpsimd
                            def _(gp, g=g, sub=sub):
                                S.dma(gp, qkv_s[g][:, sub, RQ : RQ + 512], qfm_s1[:], "st")
                                S.dma(gp, qkv_s[g][:, sub, RKF : RKF + 1536], kfm_s1[:], "st")
                                S.dma(gp, qkv_s[g][:, sub, RKT : RKT + 1536], kt_s1[:], "st")
                                S.dma(gp, qkv_s[g][:, sub, RV : RV + 1536], v_s1[:], "st")
                                a1_st[(g, sub)] = S.c["st"]

                    with nc.Block() as blk:
                        @blk.scalar
                        def _(sc):
                            sc.activation(e_blk[:], g3b[:], AF.Exp)
                    with nc.Block() as blk:
                        @blk.vector
                        def _(v):
                            F = lambda inst: S.flush(v, inst)
                            for sub in range(4):
                                F(v.reduce_sum(gsum[:, sub : sub + 1], e_blk[:, sub * 3 : sub * 3 + 3], axis=AX.X))
                            F(v.reciprocal(gsum[:, 4:8], gsum[:, 0:4]))
                            for sub in range(4):
                                F(v.tensor_scalar(gd_blk[:, sub, 0:3], e_blk[:, sub * 3 : sub * 3 + 3], gsum[:, 4 + sub : 5 + sub], None, ALU.mult))
                            for i in range(3):
                                F(v.tensor_scalar(gd_blk[:, :, 3 + i : 4 + i], gd_blk[:, :, i : i + 1], decu_sb[:, 3 + i : 4 + i], None, ALU.mult))
                    base_mm = S.c["mm"]
                    with nc.Block() as blk:
                        @blk.tensor
                        def _(t):
                            for sub in range(4):
                                mmi = t.transpose(ps_s2[0:6, sub * 128 : (sub + 1) * 128], gd_blk[:, sub, 0:6], eye_f[:])
                            S.inc(mmi, "mm")
                        @blk.vector
                        def _(v, base_mm=base_mm):
                            v.wait_ge(S.h["mm"], base_mm + 1)
                            v.tensor_copy(gdT_blk[0:6, :], ps_s2[0:6, 0:512])
                    for sub in range(4):
                        base_mm = S.c["mm"]
                        base_ev = S.c["ev"]
                        with nc.Block() as blk:
                            @blk.tensor
                            def _(t, sub=sub, base_ev=base_ev):
                                for n in range(6):
                                    if n >= 2:
                                        t.wait_ge(S.h["ev"], base_ev + n - 1)
                                    ps = ps_b1[:, 0:128] if n % 2 == 0 else ps_s1[:, 0:128]
                                    mmi = t.matmul(ps, selbf[0:6, n * 128 : (n + 1) * 128],
                                                   gdT_blk[0:6, sub * 128 : (sub + 1) * 128], start=True, stop=True)
                                    S.inc(mmi, "mm")
                            @blk.vector
                            def _(v, g=g, sub=sub, base_mm=base_mm):
                                prev = a1_st.get((g, sub - 1)) if sub > 0 else None
                                if prev:
                                    v.wait_ge(S.h["st"], prev)
                                for n in range(6):
                                    v.wait_ge(S.h["mm"], base_mm + n + 1)
                                    ps = ps_b1[:, 0:128] if n % 2 == 0 else ps_s1[:, 0:128]
                                    i = n % 3
                                    off = i * 128 if n < 3 else 384 + i * 128
                                    ev = v.tensor_copy(gb_s1[:, off : off + 128], ps)
                                    S.inc(ev, "ev")
                        with nc.Block() as blk:
                            @blk.gpsimd
                            def _(gp, g=g, sub=sub):
                                S.dma(gp, qkv_s[g][:, sub, RGB : RGB + 768], gb_s1[:], "st")
                                a1_st[(g, sub)] = S.c["st"]

            paA1w.close()

            with contextlib.ExitStack() as pa2:
                s2 = lambda name, shape, dt: pa2.enter_context(nc.sbuf_tensor(name, shape, dt))
                ps_big = pa2.enter_context(nc.psum_tensor("psA2_big", [128, 2048], F32))
                ps_sc = pa2.enter_context(nc.psum_tensor("psA2_sc", [128, 512], F32))
                ps_retr = pa2.enter_context(nc.psum_tensor("psA2_retr", [128, 512], F32))

                S_sb = s2("S_sb", [128, 3, ADT, AD], BF16)
                rec = s2("rec", [128, 2, RW], BF16)
                qt = s2("qt", [128, 3, AD], BF16)
                cmbA = s2("cmbA", [128, 3, 128], BF16)
                scb = s2("scb", [128, 3, 128], BF16)
                retrbf = s2("retrbf", [128, AD], BF16)
                aobuf = s2("aobuf", [128, 2048], F32)
                rec_h = rec.tensor if hasattr(rec, "tensor") else rec
                a2_store_at = {}
                a2_prog_after = {}
                a2_ld_val = {}
                with nc.Block() as blk:
                    @blk.vector
                    def _(v):
                        for i in range(3):
                            v.memset(S_sb[:, i].rearrange("p a b -> p (a b)"), 0.0)

                for ci in range(NCH):
                    c = NCH - 1 - ci
                    bi = ci % 2
                    roff = bi * RW

                    with nc.Block() as blk:
                        @blk.sync
                        def _(e, ci=ci, c=c):
                            if ci == 0:
                                e.wait_ge(S.h["st"], S.c["st"])
                                a2_ld_val[0] = S.dma(e, rec[:, 0], qkv_s[c // 4][:, c % 4, :], "ld")
                            if ci + 1 < NCH:
                                if ci >= 1:
                                    e.wait_ge(S.h["prog"], a2_prog_after[ci - 1])
                                cn = NCH - 1 - (ci + 1)
                                a2_ld_val[ci + 1] = S.dma(e, rec[:, (ci + 1) % 2], qkv_s[cn // 4][:, cn % 4, :], "ld")
                    with nc.Block() as blk:
                        @blk.vector
                        def _(v, ci=ci, bi=bi, roff=roff):
                            v.wait_ge(S.h["ld"], a2_ld_val[ci])
                            if ci >= 1:
                                v.wait_ge(S.h["st"], a2_store_at[ci - 1])
                            for i in range(3):
                                S.flush(v, v.tensor_tensor(cmbA[:, i], wloc[:, i], rec[:, bi, RGB + i * 128 : RGB + (i + 1) * 128], ALU.mult))
                                brep = bass.AP(rec_h, roff + RGB + 384 + i * 128, [[2 * RW, 128], [0, ADT], [1, 128]])
                                S.flush(v, v.tensor_tensor(qt[:, i], rec[:, bi, RQ : RQ + 512].rearrange("p (a b) -> p a b", b=128), brep, ALU.mult))

                    first_retr = [True]
                    for i in range(3):
                        base_mm = S.c["mm"]
                        with nc.Block() as blk:
                            @blk.tensor
                            def _(t, i=i, bi=bi):
                                for kd in range(ADT):
                                    mmi = t.matmul(ps_sc[:, 128:256], rec[:, bi, RKF + i * 512 + kd * 128 : RKF + i * 512 + (kd + 1) * 128],
                                                   rec[:, bi, RQ + kd * 128 : RQ + (kd + 1) * 128], start=(kd == 0), stop=(kd == ADT - 1))
                                S.inc(mmi, "mm")
                        with nc.Block() as blk:
                            @blk.vector
                            def _(v, i=i, base_mm=base_mm):
                                v.wait_ge(S.h["mm"], base_mm + 1)
                                v.tensor_tensor(scb[:, i], ps_sc[:, 128:256], cmbA[:, i], ALU.mult)
                        with nc.Block() as blk:
                            @blk.tensor
                            def _(t, i=i, bi=bi):
                                for m in range(ADT):
                                    t.matmul(ps_retr[:, m * 128 : (m + 1) * 128],
                                             rec[:, bi, RV + i * 512 + m * 128 : RV + i * 512 + (m + 1) * 128],
                                             scb[:, i], start=first_retr[0], stop=False)
                                    first_retr[0] = False
                                for m in range(ADT):
                                    for kd in range(ADT):
                                        last = (i == 2 and m == ADT - 1 and kd == ADT - 1)
                                        t.matmul(ps_retr[:, m * 128 : (m + 1) * 128],
                                                 S_sb[:, i, kd, m * 128 : (m + 1) * 128],
                                                 qt[:, i, kd * 128 : (kd + 1) * 128], start=False, stop=last)
                    with nc.Block() as blk:
                        @blk.vector
                        def _(v):
                            v.tensor_scalar(retrbf[:], ps_retr[:], osc_col[:], None, ALU.mult)

                    base_mm = S.c["mm"]
                    base_ev = S.c["ev"]
                    with nc.Block() as blk:
                        @blk.tensor
                        def _(t, ci=ci, bi=bi, base_ev=base_ev):
                            for i in range(3):
                                if i >= 1:
                                    t.wait_ge(S.h["ev"], base_ev + i)
                                for m in range(ADT):
                                    t.matmul(ps_big[:, m * 512 : (m + 1) * 512],
                                             rec[:, bi, RKT + i * 512 + m * 128 : RKT + i * 512 + (m + 1) * 128],
                                             rec[:, bi, RV + i * 512 : RV + (i + 1) * 512], start=True, stop=False)
                                    mmi = t.matmul(ps_big[:, m * 512 : (m + 1) * 512], Isc[:, i],
                                                   S_sb[:, i, m, :], start=False, stop=True)
                                S.inc(mmi, "mm")
                            t.nop().then_inc(S.h["prog"], 1)
                            S.c["prog"] += 1
                            a2_prog_after[ci] = S.c["prog"]

                        @blk.vector
                        def _(v, base_mm=base_mm):
                            for i in range(3):
                                v.wait_ge(S.h["mm"], base_mm + i + 1)
                                ev = v.tensor_copy(S_sb[:, i].rearrange("p a b -> p (a b)"), ps_big[:])
                                S.inc(ev, "ev")

                    with nc.Block() as blk:
                        @blk.tensor
                        def _(t):
                            for m in range(HT):
                                for kd in range(ADT):
                                    t.matmul(ps_big[:, m * 128 : (m + 1) * 128], wo_sb[:, kd, m * 128 : (m + 1) * 128],
                                             retrbf[:, kd * 128 : (kd + 1) * 128], start=(kd == 0), stop=(kd == ADT - 1))
                    with nc.Block() as blk:
                        @blk.vector
                        def _(v):
                            v.tensor_copy(aobuf[:], ps_big[:])
                        @blk.gpsimd
                        def _(gp, c=c, ci=ci):
                            S.dma(gp, fm(ao_s)[:, :, c * C : (c + 1) * C], aobuf[:].rearrange("p (k t) -> p k t", t=C), "st")
                            a2_store_at[ci] = S.c["st"]

            with contextlib.ExitStack() as pa3:
                s3 = lambda name, shape, dt: pa3.enter_context(nc.sbuf_tensor(name, shape, dt))
                ps_s1 = pa3.enter_context(nc.psum_tensor("psA3_s1", [128, 512], F32))
                ps_s2 = pa3.enter_context(nc.psum_tensor("psA3_s2", [128, 512], F32))
                ps_b1 = pa3.enter_context(nc.psum_tensor("psA3_b1", [128, 512], F32))
                ps_b2 = pa3.enter_context(nc.psum_tensor("psA3_b2", [128, 512], F32))

                aq = s3("aq", [128, 2, 2048], F32)
                xq3 = s3("xq3", [128, 2, 2048], F32)
                yq = s3("yq", [128, 2, 2048], F32)
                ta3 = s3("ta3", [128, 2048], F32)
                r0q = s3("r0q", [128, 2, 2048], F32)
                hib = s3("hib", [128, 2, 2048], BF16)
                lob = s3("lob", [128, 2, 2048], BF16)
                st1b = s3("st1b", [128, D], BF16)
                st2b = s3("st2b", [128, D], BF16)
                rows3 = s3("rows3", [1, 3584], F32)
                bc3 = s3("bc3", [128, 1024], F32)
                RSW3 = [rows3[0:1, i * 512 : (i + 1) * 512] for i in range(7)]
                bc3_h = bc3.tensor if hasattr(bc3, "tensor") else bc3
                a3st = {}
                y_store_after = {}

                for g in range(NB):
                    for q in range(4):
                        with nc.Block() as blk:
                            @blk.sync
                            def _(e, g=g, q=q):
                                if g == 0 and q == 0:
                                    e.wait_ge(S.h["st"], S.c["st"])
                                S.dma(e, aq[:, q % 2].rearrange("p (k t) -> p k t", t=C),
                                      fm(ao_s)[:, :, g * TB + q * C : g * TB + (q + 1) * C], "ld")
                        with nc.Block() as blk:
                            @blk.scalar
                            def _(sc, q=q):
                                sc.wait_ge(S.h["ld"], S.c["ld"])
                                sc.activation(st1b[:], aq[:, q % 2], AF.Identity)
                                sc.activation(st2b[:], aq[:, q % 2], AF.Square)
                        with nc.Block() as blk:
                            @blk.tensor
                            def _(t, q=q):
                                for k in range(KD):
                                    t.matmul(ps_s1[0:1, q * 128 : (q + 1) * 128], onescol[:], st1b[:, k * 128 : (k + 1) * 128],
                                             start=(k == 0), stop=(k == KD - 1))
                                for k in range(KD):
                                    t.matmul(ps_s2[0:1, q * 128 : (q + 1) * 128], onescol[:], st2b[:, k * 128 : (k + 1) * 128],
                                             start=(k == 0), stop=(k == KD - 1))
                    emit_rows512(None, ps_s1, ps_s2, rows3, RSW3)
                    with nc.Block() as blk:
                        @blk.tensor
                        def _(t):
                            t.matmul(ps_b1[:], ones1f[:], RSW3[5], start=True, stop=True)
                            t.matmul(ps_b2[:], ones1f[:], RSW3[6], start=True, stop=True)
                    with nc.Block() as blk:
                        @blk.vector
                        def _(v):
                            S.flush(v, v.tensor_copy(bc3[:, 0:512], ps_b1[:]))
                            S.flush(v, v.tensor_copy(bc3[:, 512:1024], ps_b2[:]))
                    for q in range(4):
                        with nc.Block() as blk:
                            @blk.sync
                            def _(e, g=g, q=q):
                                S.dma(e, aq[:, q % 2].rearrange("p (k t) -> p k t", t=C),
                                      fm(ao_s)[:, :, g * TB + q * C : g * TB + (q + 1) * C], "ld")
                                S.dma(e, xq3[:, q % 2].rearrange("p (k t) -> p k t", t=C),
                                      fm(xT)[:, :, g * TB + q * C : g * TB + (q + 1) * C], "ld")
                        with nc.Block() as blk:
                            @blk.vector
                            def _(v, q=q):
                                v.wait_ge(S.h["ld"], S.c["ld"])
                                rrep = bass.AP(bc3_h, 512 + q * 128, [[1024, 128], [0, KD], [1, 128]])
                                mrep = bass.AP(bc3_h, q * 128, [[1024, 128], [0, KD], [1, 128]])
                                a3 = lambda buf: buf.rearrange("p (k t) -> p k t", t=C)
                                S.flush(v, v.tensor_tensor(a3(ta3[:]), a3(aq[:, q % 2]), rrep, ALU.mult))
                                S.flush(v, v.tensor_tensor(a3(aq[:, q % 2]), a3(ta3[:]), mrep, ALU.subtract))
                                v.tensor_tensor(yq[:, q % 2], aq[:, q % 2], xq3[:, q % 2], ALU.add)
                        with nc.Block() as blk:
                            @blk.scalar
                            def _(sc, q=q):
                                sc.activation(st1b[:], yq[:, q % 2], AF.Identity)
                                sc.activation(st2b[:], yq[:, q % 2], AF.Square)
                            @blk.gpsimd
                            def _(gp, g=g, q=q):
                                S.dma(gp, fm(y_s)[:, :, g * TB + q * C : g * TB + (q + 1) * C],
                                      yq[:, q % 2].rearrange("p (k t) -> p k t", t=C), "st")
                                y_store_after[g] = S.c["st"]
                        with nc.Block() as blk:
                            @blk.tensor
                            def _(t, q=q):
                                for k in range(KD):
                                    t.matmul(ps_s1[0:1, q * 128 : (q + 1) * 128], onescol[:], st1b[:, k * 128 : (k + 1) * 128],
                                             start=(k == 0), stop=(k == KD - 1))
                                for k in range(KD):
                                    t.matmul(ps_s2[0:1, q * 128 : (q + 1) * 128], onescol[:], st2b[:, k * 128 : (k + 1) * 128],
                                             start=(k == 0), stop=(k == KD - 1))
                    emit_rows512(None, ps_s1, ps_s2, rows3, RSW3)
                    with nc.Block() as blk:
                        @blk.tensor
                        def _(t):
                            t.matmul(ps_b1[:], ones1f[:], RSW3[5], start=True, stop=True)
                            t.matmul(ps_b2[:], ones1f[:], RSW3[6], start=True, stop=True)
                    with nc.Block() as blk:
                        @blk.vector
                        def _(v):
                            S.flush(v, v.tensor_copy(bc3[:, 0:512], ps_b1[:]))
                            S.flush(v, v.tensor_copy(bc3[:, 512:1024], ps_b2[:]))
                    for q in range(4):
                        with nc.Block() as blk:
                            @blk.sync
                            def _(e, g=g, q=q):
                                e.wait_ge(S.h["st"], y_store_after[g])
                                S.dma(e, yq[:, q % 2].rearrange("p (k t) -> p k t", t=C),
                                      fm(y_s)[:, :, g * TB + q * C : g * TB + (q + 1) * C], "ld")
                        with nc.Block() as blk:
                            @blk.vector
                            def _(v, g=g, q=q):
                                v.wait_ge(S.h["ld"], S.c["ld"])
                                n = g * 4 + q
                                if n >= 2 and (n - 2) in a3st:
                                    v.wait_ge(S.h["st"], a3st[n - 2])
                                rrep = bass.AP(bc3_h, 512 + q * 128, [[1024, 128], [0, KD], [1, 128]])
                                mrep = bass.AP(bc3_h, q * 128, [[1024, 128], [0, KD], [1, 128]])
                                a3 = lambda buf: buf.rearrange("p (k t) -> p k t", t=C)
                                S.flush(v, v.tensor_tensor(a3(ta3[:]), a3(yq[:, q % 2]), rrep, ALU.mult))
                                S.flush(v, v.tensor_tensor(a3(r0q[:, q % 2]), a3(ta3[:]), mrep, ALU.subtract))
                                S.flush(v, v.tensor_copy(hib[:, q % 2], r0q[:, q % 2]))
                                v.tensor_tensor(lob[:, q % 2], r0q[:, q % 2], hib[:, q % 2], ALU.subtract)
                        with nc.Block() as blk:
                            @blk.gpsimd
                            def _(gp, g=g, q=q):
                                sl = lambda dr: fm(dr)[:, :, g * TB + q * C : g * TB + (q + 1) * C]
                                v3 = lambda buf: buf.rearrange("p (k t) -> p k t", t=C)
                                S.dma(gp, sl(r0_s), v3(r0q[:, q % 2]), "st")
                                S.dma(gp, sl(r0hi_s), v3(hib[:, q % 2]), "st")
                                S.dma(gp, sl(r0lo_s), v3(lob[:, q % 2]), "st")
                                a3st[g * 4 + q] = S.c["st"]

        TT = 1024
        NT = T // TT
        with contextlib.ExitStack() as pb:
            sbb = lambda name, shape, dt: pb.enter_context(nc.sbuf_tensor(name, shape, dt))
            psA = pb.enter_context(nc.psum_tensor("psA", [128, 2048], F32))
            psW = pb.enter_context(nc.psum_tensor("psW", [128, 1024], F32))
            psG = pb.enter_context(nc.psum_tensor("psG", [128, 512], F32))
            psS = pb.enter_context(nc.psum_tensor("psS", [128, 512], F32))

            xtu = sbb("xtu", [128, KD, TT], BF16)
            ph = sbb("ph", [128, KD, TT], BF16)
            acc = sbb("acc", [128, HT, TT], F32)
            upb = sbb("upb", [128, 3, KD, 128], BF16)
            swb = sbb("swb", [128, 4, ET, 128], BF16)
            dnb = sbb("dnb", [128, 4, ET, 128], BF16)
            wbn = sbb("wbn", [128, TT], BF16)
            wTx = sbb("wTx", [8, TT], BF16)
            xgl = sbb("xgl", [128, 2, KD, 128], BF16)
            rrow = sbb("rrow", [128, 2, TT], F32)
            zrow = sbb("zrow", [128, 2, 2, TT], BF16)
            stg = sbb("stg", [128, 2, TT], F32)
            zsqb = sbb("zsqb", [128, 2, TT], BF16)
            gg = sbb("gg", [128, 56], F32)

            Q = {"tensor": [], "scalar": [], "vector": [], "sync": [], "gpsimd": []}

            def em(eng, fn):
                Q[eng].append(fn)

            def wv(eng, sem, cnt):
                assert cnt <= S.c[sem], (eng, sem, cnt, S.c[sem])
                em(eng, lambda e, sem=sem, cnt=cnt: e.wait_ge(S.h[sem], cnt))

            def wr(eng, rel):
                if rel is not None:
                    wv(eng, rel[0], rel[1])

            def dmaQ(eng, sem, dst, src):
                S.c[sem] += 16
                em(eng, lambda e, dst=dst, src=src, sem=sem: e.dma_start(dst, src).then_inc(S.h[sem], 16))
                return S.c[sem]

            def incQ(eng, sem, fn):
                S.c[sem] += 1
                em(eng, lambda e, fn=fn, sem=sem: fn(e).then_inc(S.h[sem], 1))
                return S.c[sem]

            def flQ(eng, fn):
                S.c["fl"] += 1
                cnt = S.c["fl"]
                em(eng, lambda e, fn=fn, cnt=cnt: (fn(e).then_inc(S.h["fl"], 1),
                                                   e.wait_ge(S.h["fl"], cnt)))

            TOT_TILES = 2 * NT
            TOT_UP = TOT_TILES * NU * 8
            TOT_SW = TOT_TILES * NU * 8
            TOT_DN = TOT_TILES * NU * HT
            up_ld = {}; up_cons = {}
            sw_ld = {}; sw_cons = {}
            dn_ld = {}; dn_cons = {}

            def issue_up(g):
                if g >= TOT_UP:
                    return
                n, c = divmod(g % (NU * 8), 8)
                r = up_cons.get(g - 3)
                if r is not None:
                    wv("sync", "mm", r)
                up_ld[g] = dmaQ("sync", "wld", upb[:, g % 3],
                                wup_s[n].rearrange("(k p) e -> p k e", p=128)[:, :, c * 128:(c + 1) * 128])

            def issue_sw(g):
                if g >= TOT_SW:
                    return
                n, c = divmod(g % (NU * 8), 8)
                r = sw_cons.get(g - 4)
                if r is not None:
                    wv("sync", "mm", r)
                sw_ld[g] = dmaQ("sync", "wld", swb[:, g % 4],
                                wsw_s[n].rearrange("(k p) e -> p k e", p=128)[:, :, c * 128:(c + 1) * 128])

            def issue_dn(g):
                if g >= TOT_DN:
                    return
                n, m = divmod(g % (NU * HT), HT)
                r = dn_cons.get(g - 4)
                if r is not None:
                    wv("gpsimd", "mm", r)
                dn_ld[g] = dmaQ("gpsimd", "dld", dnb[:, g % 4], wdn_s[m, :, n, :, :])

            bankrel = [None, None, None, None]
            bank_i = [0]

            def acquire_bank():
                b = bank_i[0] % 4
                bank_i[0] += 1
                wr("tensor", bankrel[b])
                return b

            carry = {"xtu": None, "ph": None, "psG": None, "psW": None, "wTx": None,
                     "stg": None, "rrow": {0: None, 1: None}, "zrow": {0: None, 1: None},
                     "xgl": {0: None, 1: None}}
            zstore = {}
            tile_iter = 0
            for j in range(2):
                hi_src = r0hi_s if j == 0 else xt1hi_s
                lo_src = r0lo_s if j == 0 else xt1lo_s
                for t in range(NT):
                    base = t * TT
                    if tile_iter == 0:
                        st_all = S.c["st"]
                        wv("sync", "st", st_all)
                        wv("scalar", "st", st_all)
                        wv("gpsimd", "st", st_all)
                    if j == 1:
                        wv("sync", "st", zstore[t])
                        wv("scalar", "st", zstore[t])
                    wr("sync", carry["xtu"])
                    xtu_ldv = dmaQ("sync", "ld", xtu[:], fm(hi_src)[:, :, base:base + TT])
                    if tile_iter == 0:
                        for g0 in range(3):
                            issue_up(g0)
                        for g0 in range(4):
                            issue_sw(g0)
                        for g0 in range(4):
                            issue_dn(g0)

                    wr("vector", carry["wTx"])
                    flQ("vector", lambda e: e.memset(wTx[:], 1.0))
                    G8 = gg[:, 0:8]
                    g_mm = {}; gate_rel = {}; gtr = {}; wTxrel = {}
                    for c in range(8):
                        if c >= 2:
                            wv("scalar", "mm", g_mm[c - 2])
                        else:
                            wr("scalar", carry["xgl"][c])
                        xldc = dmaQ("scalar", "xld", xgl[:, c % 2],
                                    fm(lo_src)[:, :, base + c * 128:base + (c + 1) * 128])
                        wv("tensor", "xld", xldc)
                        if c == 0:
                            wv("tensor", "ld", xtu_ldv)
                            wr("tensor", carry["psG"])
                        else:
                            wv("tensor", "evv", gate_rel[c - 1])

                        def g_mms(e, c=c, j=j):
                            for k in range(KD):
                                e.matmul(psG[:, 0:8], xtu[:, k, c * 128:(c + 1) * 128],
                                         wghi[:, j, k, :], start=(k == 0), stop=False)
                            for k in range(KD):
                                e.matmul(psG[:, 0:8], xtu[:, k, c * 128:(c + 1) * 128],
                                         wglo[:, j, k, :], start=False, stop=False)
                            for k in range(KD):
                                mi = e.matmul(psG[:, 0:8], xgl[:, c % 2, k, :],
                                              wghi[:, j, k, :], start=False, stop=(k == KD - 1))
                            return mi
                        g_mm[c] = incQ("tensor", "mm", g_mms)

                        wv("vector", "mm", g_mm[c])
                        if c > 0:
                            wv("vector", "mm", gtr[c - 1])
                        gate_rel[c] = incQ("vector", "evv",
                                           lambda e, j=j: e.tensor_tensor(G8, psG[:, 0:8], bgb[:, j * 8:(j + 1) * 8], ALU.add))
                        wv("vector", "evv", gate_rel[c])
                        flQ("vector", lambda e: e.tensor_single_scalar(gg[:, 8:16], G8, G8[:, 0:1], ALU.is_lt))
                        cur = 8
                        for mI in range(1, 8):
                            flQ("vector", lambda e, mI=mI: e.tensor_single_scalar(gg[:, 24:32], G8, G8[:, mI:mI + 1], ALU.is_lt))
                            nxt = 24 - cur
                            flQ("vector", lambda e, cur=cur, nxt=nxt: e.tensor_add(gg[:, nxt:nxt + 8], gg[:, cur:cur + 8], gg[:, 24:32]))
                            cur = nxt
                        flQ("vector", lambda e, cur=cur: e.tensor_single_scalar(gg[:, 32:40], gg[:, cur:cur + 8], 5.5, ALU.is_lt))
                        ggv = incQ("vector", "evv", lambda e: e.reduce_max(gg[:, 48:49], G8, axis=AX.X, negate=True))
                        wv("scalar", "evv", ggv)
                        gexp = incQ("scalar", "ev", lambda e: e.activation(gg[:, 40:48], gg[:, 0:8], AF.Exp, bias=gg[:, 48:49], scale=1.0))
                        wv("vector", "ev", gexp)
                        flQ("vector", lambda e: e.tensor_mul(gg[:, 24:32], gg[:, 40:48], gg[:, 32:40]))
                        flQ("vector", lambda e: e.reduce_sum(gg[:, 49:50], gg[:, 24:32], axis=AX.X))
                        flQ("vector", lambda e: e.reciprocal(gg[:, 50:51], gg[:, 49:50]))
                        gn = incQ("vector", "evv", lambda e: e.tensor_scalar(gg[:, 0:8], gg[:, 24:32], gg[:, 50:51], None, ALU.mult))
                        wv("tensor", "evv", gn)
                        if c > 0:
                            wv("tensor", "evv", wTxrel[c - 1])
                        gtr[c] = incQ("tensor", "mm", lambda e: e.transpose(psG[0:8, 128:256], gg[:, 0:8], eye_f[:]))
                        wv("vector", "mm", gtr[c])
                        wTxrel[c] = incQ("vector", "evv",
                                         lambda e, c=c: e.tensor_copy(wTx[0:6, c * 128:(c + 1) * 128], psG[0:6, 128:256]))
                    wtx_done = wTxrel[7]
                    carry["xgl"] = {0: ("mm", g_mm[6]), 1: ("mm", g_mm[7])}

                    selmm = {}; wbnrel = {}
                    updone = {}; swdone = {}; dndone = {}
                    for n in range(NU):
                        if n < 6:
                            wv("tensor", "evv", wtx_done if n == 0 else wbnrel[n - 1])
                            if n == 0:
                                wr("tensor", carry["psW"])

                            def sel_mms(e, n=n):
                                e.matmul(psW[:, 0:512], selbf[0:8, n * 128:(n + 1) * 128], wTx[0:8, 0:512], start=True, stop=True)
                                return e.matmul(psW[:, 512:1024], selbf[0:8, n * 128:(n + 1) * 128], wTx[0:8, 512:1024], start=True, stop=True)
                            selmm[n] = incQ("tensor", "mm", sel_mms)
                            wv("vector", "mm", selmm[n])
                            wbnrel[n] = incQ("vector", "evv", lambda e: e.tensor_copy(wbn[:], psW[:]))

                        last_up_ev = {}
                        for c in range(8):
                            gidx = tile_iter * 64 + n * 8 + c
                            wv("tensor", "wld", up_ld[gidx])
                            for sub in range(2):
                                b = acquire_bank()
                                if tile_iter == 0 and n == 0 and c == 0 and sub == 0:
                                    wv("tensor", "ld", xtu_ldv)

                                def up_mms(e, c=c, sub=sub, b=b, slot=gidx % 3):
                                    for k in range(KD):
                                        mi = e.matmul(psA[:, b * 512:(b + 1) * 512], upb[:, slot, k, :],
                                                      xtu[:, k, sub * 512:(sub + 1) * 512],
                                                      start=(k == 0), stop=(k == KD - 1))
                                    return mi
                                gm = incQ("tensor", "mm", up_mms)
                                wv("scalar", "mm", gm)
                                if c == 0 and sub == 0:
                                    if n == 0:
                                        wr("scalar", carry["ph"])
                                    else:
                                        wv("scalar", "mm", swdone[n - 1])
                                bcol = (bcols[:, n * 8 + c:n * 8 + c + 1] if n < 6
                                        else bcols[:, 96 + (n - 6) * 8 + c:96 + (n - 6) * 8 + c + 1])
                                eva = incQ("scalar", "ev",
                                           lambda e, c=c, sub=sub, b=b, bcol=bcol: e.activation(
                                               ph[:, c, sub * 512:(sub + 1) * 512], psA[:, b * 512:(b + 1) * 512],
                                               AF.Identity, bias=bcol, scale=1.0))
                                bankrel[b] = ("ev", eva)
                                last_up_ev[sub] = eva
                            up_cons[gidx] = S.c["mm"]
                            issue_up(gidx + 3)
                        updone[n] = S.c["mm"]

                        h2done = {}
                        for c in range(8):
                            gidx = tile_iter * 64 + n * 8 + c
                            wv("tensor", "wld", sw_ld[gidx])
                            for sub in range(2):
                                b = acquire_bank()
                                if c == 0:
                                    wv("tensor", "ev", last_up_ev[sub])

                                def sw_mms(e, c=c, sub=sub, b=b, slot=gidx % 4):
                                    for k in range(ET):
                                        mi = e.matmul(psA[:, b * 512:(b + 1) * 512], swb[:, slot, k, :],
                                                      ph[:, k, sub * 512:(sub + 1) * 512],
                                                      start=(k == 0), stop=(k == ET - 1))
                                    return mi
                                gm = incQ("tensor", "mm", sw_mms)
                                wv("scalar", "mm", gm)
                                if c == 0 and sub == 0 and n > 0:
                                    wv("scalar", "mm", dndone[n - 1])
                                bcol = (bcols[:, 48 + n * 8 + c:48 + n * 8 + c + 1] if n < 6
                                        else bcols[:, 112 + c:112 + c + 1])
                                eva = incQ("scalar", "ev",
                                           lambda e, c=c, sub=sub, b=b, bcol=bcol: e.activation(
                                               ph[:, ET + c, sub * 512:(sub + 1) * 512], psA[:, b * 512:(b + 1) * 512],
                                               AF.Silu, bias=bcol, scale=1.0))
                                bankrel[b] = ("ev", eva)
                                wv("vector", "ev", eva)
                                h2s = ph[:, ET + c, sub * 512:(sub + 1) * 512]
                                if n < 6:
                                    flQ("vector", lambda e, h2s=h2s, c=c, sub=sub: e.tensor_tensor(
                                        h2s, h2s, ph[:, c, sub * 512:(sub + 1) * 512], ALU.mult))
                                    h2done[sub] = incQ("vector", "evv",
                                                       lambda e, h2s=h2s, sub=sub: e.tensor_tensor(
                                                           h2s, h2s, wbn[:, sub * 512:(sub + 1) * 512], ALU.mult))
                                else:
                                    h2done[sub] = incQ("vector", "evv",
                                                       lambda e, h2s=h2s, c=c, sub=sub: e.tensor_tensor(
                                                           h2s, h2s, ph[:, c, sub * 512:(sub + 1) * 512], ALU.mult))
                            sw_cons[gidx] = S.c["mm"]
                            issue_sw(gidx + 4)
                        swdone[n] = S.c["mm"]

                        for m in range(HT):
                            gidx = tile_iter * 128 + n * 16 + m
                            wv("tensor", "dld", dn_ld[gidx])
                            for sub in range(2):
                                b = acquire_bank()
                                if m == 0:
                                    wv("tensor", "evv", h2done[sub])

                                def dn_mms(e, n=n, m=m, sub=sub, b=b, slot=gidx % 4):
                                    for k in range(ET):
                                        mi = e.matmul(psA[:, b * 512:(b + 1) * 512], dnb[:, slot, k, :],
                                                      ph[:, ET + k, sub * 512:(sub + 1) * 512],
                                                      start=(k == 0), stop=(k == ET - 1 and n != 0))
                                    if n == 0:
                                        mi = e.matmul(psA[:, b * 512:(b + 1) * 512],
                                                      bext_bf[0:7, m * 128:(m + 1) * 128],
                                                      wTx[0:7, sub * 512:(sub + 1) * 512], start=False, stop=True)
                                    return mi
                                gm = incQ("tensor", "mm", dn_mms)
                                wv("vector", "mm", gm)
                                accsl = acc[:, m, sub * 512:(sub + 1) * 512]
                                if n == 0:
                                    av = incQ("vector", "evv",
                                              lambda e, accsl=accsl, b=b: e.tensor_copy(accsl, psA[:, b * 512:(b + 1) * 512]))
                                else:
                                    av = incQ("vector", "evv",
                                              lambda e, accsl=accsl, b=b: e.tensor_tensor(accsl, accsl, psA[:, b * 512:(b + 1) * 512], ALU.add))
                                bankrel[b] = ("evv", av)
                            dn_cons[gidx] = S.c["mm"]
                            issue_dn(gidx + 4)
                        dndone[n] = S.c["mm"]

                    if j == 0:
                        rrel = {}; zst = {}
                        for m in range(HT):
                            if m >= 2:
                                wv("scalar", "evv", rrel[m - 2])
                            else:
                                wr("scalar", carry["rrow"][m % 2])
                            rld = dmaQ("scalar", "xld", rrow[:, m % 2], fm(r0_s)[:, m, base:base + TT])
                            wv("vector", "xld", rld)
                            if m >= 2:
                                wv("vector", "st", zst[m - 2])
                            else:
                                wr("vector", carry["zrow"][m % 2])
                            flQ("vector", lambda e, m=m: e.tensor_tensor(acc[:, m, :], acc[:, m, :], rrow[:, m % 2], ALU.add))
                            flQ("vector", lambda e, m=m: e.tensor_copy(zrow[:, m % 2, 0], acc[:, m, :]))
                            zev = incQ("vector", "evv",
                                       lambda e, m=m: e.tensor_tensor(zrow[:, m % 2, 1], acc[:, m, :], zrow[:, m % 2, 0], ALU.subtract))
                            rrel[m] = zev
                            wv("gpsimd", "evv", zev)
                            dmaQ("gpsimd", "st", fm(xt1hi_s)[:, m, base:base + TT], zrow[:, m % 2, 0])
                            zst[m] = dmaQ("gpsimd", "st", fm(xt1lo_s)[:, m, base:base + TT], zrow[:, m % 2, 1])
                        zstore[t] = S.c["st"]
                        carry["xtu"] = ("mm", updone[NU - 1])
                        carry["ph"] = None
                        carry["rrow"] = {0: ("evv", rrel[14]), 1: ("evv", rrel[15])}
                        carry["zrow"] = {0: ("st", zst[14]), 1: ("st", zst[15])}
                        carry["psW"] = ("evv", wbnrel[5])
                        carry["psG"] = ("evv", wtx_done)
                        carry["wTx"] = ("mm", selmm[5])
                    else:
                        zrel = {}; smm = {}
                        for m in range(HT):
                            if m >= 2:
                                wv("scalar", "evv", zrel[m - 2])
                            else:
                                wr("scalar", carry["rrow"][m % 2])
                            rld = dmaQ("scalar", "xld", rrow[:, m % 2], fm(r0_s)[:, m, base:base + TT])
                            wv("vector", "xld", rld)
                            flQ("vector", lambda e, m=m: e.tensor_tensor(acc[:, m, :], acc[:, m, :], rrow[:, m % 2], ALU.add))
                            flQ("vector", lambda e, m=m: e.tensor_copy(xtu[:, m, :], acc[:, m, :]))
                            zev = incQ("vector", "evv",
                                       lambda e, m=m: e.tensor_tensor(ph[:, m, :], acc[:, m, :], xtu[:, m, :], ALU.subtract))
                            zrel[m] = zev
                            wv("scalar", "evv", zev)
                            if m >= 2:
                                wv("scalar", "mm", smm[m - 2])
                            sqe = incQ("scalar", "ev", lambda e, m=m: e.activation(zsqb[:, m % 2], xtu[:, m, :], AF.Square))
                            wv("tensor", "evv", zev)
                            wv("tensor", "ev", sqe)
                            if m == 0:
                                wv("tensor", "evv", wbnrel[5])

                            def stat_mms(e, m=m):
                                for sub in range(2):
                                    tgt = psG if sub == 0 else psS
                                    e.matmul(tgt[0:1, 0:512], onescol[:], xtu[:, m, sub * 512:(sub + 1) * 512],
                                             start=(m == 0), stop=False)
                                    e.matmul(tgt[0:1, 0:512], onescol[:], ph[:, m, sub * 512:(sub + 1) * 512],
                                             start=False, stop=(m == HT - 1))
                                e.matmul(psW[0:1, 0:512], onescol[:], zsqb[:, m % 2, 0:512],
                                         start=(m == 0), stop=(m == HT - 1))
                                return e.matmul(psW[0:1, 512:1024], onescol[:], zsqb[:, m % 2, 512:1024],
                                                start=(m == 0), stop=(m == HT - 1))
                            smm[m] = incQ("tensor", "mm", stat_mms)

                        Asl = [rrow[0:1, i // 2, (i % 2) * 512:(i % 2) * 512 + 512] for i in range(4)]
                        Bsl = [stg[0:1, i // 2, (i % 2) * 512:(i % 2) * 512 + 512] for i in range(4)]
                        wv("vector", "mm", smm[HT - 1])
                        wr("vector", carry["stg"])
                        rows_done = None
                        for sub in range(2):
                            s1ap = (psG if sub == 0 else psS)[0:1, 0:512]
                            s2ap = psW[0:1, sub * 512:(sub + 1) * 512]
                            MR = Bsl[0] if sub == 0 else Bsl[2]
                            R = Bsl[1] if sub == 0 else Bsl[3]
                            s0, s1_, s2_, s3_, s4_ = Asl[0], Asl[1], Asl[2], Asl[3], Bsl[3]
                            flQ("vector", lambda e, s0=s0, s1ap=s1ap: e.tensor_scalar(s0, s1ap, 1.0 / D, None, ALU.mult))
                            flQ("vector", lambda e, s1_=s1_, s2ap=s2ap: e.tensor_scalar(s1_, s2ap, 1.0 / D, None, ALU.mult))
                            flQ("vector", lambda e, s2_=s2_, s0=s0: e.tensor_mul(s2_, s0, s0))
                            flQ("vector", lambda e, s4_=s4_, s1_=s1_, s2_=s2_: e.tensor_sub(s4_, s1_, s2_))
                            ve = incQ("vector", "evv", lambda e, s3_=s3_, s4_=s4_: e.tensor_scalar_add(s3_, s4_, EPS))
                            wv("scalar", "evv", ve)
                            flQ("scalar", lambda e, s2_=s2_, s3_=s3_: e.activation(s2_, s3_, AF.Ln))
                            se = incQ("scalar", "ev", lambda e, s4_=s4_, s2_=s2_: e.activation(s4_, s2_, AF.Exp, bias=0.0, scale=-0.5))
                            wv("vector", "ev", se)
                            flQ("vector", lambda e, s1_=s1_, s4_=s4_: e.tensor_mul(s1_, s4_, s4_))
                            flQ("vector", lambda e, s2_=s2_, s1_=s1_, s3_=s3_: e.tensor_mul(s2_, s1_, s3_))
                            flQ("vector", lambda e, s1_=s1_, s2_=s2_: e.tensor_scalar(s1_, s2_, -0.5, 1.5, ALU.mult, ALU.add))
                            flQ("vector", lambda e, R=R, s4_=s4_, s1_=s1_: e.tensor_mul(R, s4_, s1_))
                            rows_done = incQ("vector", "evv", lambda e, MR=MR, s0=s0, R=R: e.tensor_mul(MR, s0, R))
                        wv("tensor", "evv", rows_done)

                        def bc_mms(e):
                            e.matmul(psW[:, 0:512], ones1f[:], Bsl[0], start=True, stop=True)
                            e.matmul(psW[:, 512:1024], ones1f[:], Bsl[2], start=True, stop=True)
                            e.matmul(psG[:, 0:512], ones1f[:], Bsl[1], start=True, stop=True)
                            return e.matmul(psS[:, 0:512], ones1f[:], Bsl[3], start=True, stop=True)
                        bc_mm = incQ("tensor", "mm", bc_mms)

                        yrel = {}; ost = {}
                        for m in range(HT):
                            if m < 2:
                                wv("scalar", "evv", rows_done)
                                wv("scalar", "evv", zrel[14 + (m % 2)])
                            else:
                                wv("scalar", "evv", yrel[m - 2])
                            yld = dmaQ("scalar", "xld", rrow[:, m % 2], fm(y_s)[:, m, base:base + TT])
                            wv("vector", "xld", yld)
                            if m == 0:
                                wv("vector", "mm", bc_mm)
                            if m >= 2:
                                wv("vector", "st", ost[m - 2])
                            orv = None
                            for sub in range(2):
                                Rps = psG if sub == 0 else psS
                                ss = stg[:, m % 2, sub * 512:(sub + 1) * 512]
                                flQ("vector", lambda e, ss=ss, m=m, sub=sub, Rps=Rps: e.tensor_tensor(
                                    ss, acc[:, m, sub * 512:(sub + 1) * 512], Rps[:, 0:512], ALU.mult))
                                flQ("vector", lambda e, ss=ss, sub=sub: e.tensor_tensor(
                                    ss, ss, psW[:, sub * 512:(sub + 1) * 512], ALU.subtract))
                                opl = lambda e, ss=ss, m=m, sub=sub: e.tensor_tensor(
                                    ss, ss, rrow[:, m % 2, sub * 512:(sub + 1) * 512], ALU.add)
                                if sub == 1:
                                    orv = incQ("vector", "evv", opl)
                                else:
                                    flQ("vector", opl)
                            yrel[m] = orv
                            wv("gpsimd", "evv", orv)
                            ost[m] = dmaQ("gpsimd", "st", fm(x2T)[:, m, base:base + TT], stg[:, m % 2])
                        carry["xtu"] = ("mm", smm[HT - 1])
                        carry["ph"] = ("mm", smm[HT - 1])
                        carry["psG"] = ("evv", yrel[15])
                        carry["psW"] = ("evv", yrel[15])
                        carry["rrow"] = {0: ("evv", yrel[14]), 1: ("evv", yrel[15])}
                        carry["stg"] = ("st", ost[15])
                        carry["wTx"] = ("mm", selmm[5])
                    tile_iter += 1

            wv("gpsimd", "st", S.c["st"])

            with nc.Block() as blk:
                @blk.tensor
                def _(te):
                    for fn in Q["tensor"]:
                        fn(te)

                @blk.scalar
                def _(sc):
                    for fn in Q["scalar"]:
                        fn(sc)

                @blk.vector
                def _(v):
                    for fn in Q["vector"]:
                        fn(v)

                @blk.sync
                def _(e):
                    for fn in Q["sync"]:
                        fn(e)

                @blk.gpsimd
                def _(gp):
                    for fn in Q["gpsimd"]:
                        fn(gp)
    return nc


def _consts():
    u = np.arange(128)
    ubits = ((u[:, None] >> np.arange(7)[None, :]) & 1).astype(np.float32)
    c1 = 127 - u
    c1bits = ((c1[:, None] >> np.arange(7)[None, :]) & 1).astype(np.float32)
    s = np.arange(128)[:, None]
    t = np.arange(128)[None, :]
    expn = np.maximum(s - t - 1, 0)
    expnbits = np.concatenate([((expn >> b) & 1).astype(np.float32) for b in range(7)], axis=1).astype(BF)
    causT = (s > t).astype(np.float32).astype(BF)
    sel = np.zeros((8, 768), np.float32)
    for n in range(6):
        sel[n, n * 128 : (n + 1) * 128] = 1.0
    return dict(
        eye_bfd=np.eye(128, dtype=np.float32).astype(BF),
        eye_fd=np.eye(128, dtype=np.float32),
        ones1fd=np.ones((1, 128), np.float32),
        onescold=np.ones((128, 1), np.float32).astype(BF),
        selbfd=sel.astype(BF),
        ubitsd=ubits, c1bitsd=c1bits, expnbitsd=expnbits, causTd=causT,
    )


def _prep_inputs(inputs, T):
    f = lambda k: np.ascontiguousarray(np.asarray(inputs[k], np.float32))
    wg3 = f("L0_attn_Wg")
    wg3h = np.ascontiguousarray(wg3.reshape(16, 128, 3).transpose(1, 0, 2).reshape(128, 48))
    fg = f("L0_ffn_Wg")
    fgh = np.ascontiguousarray(fg.reshape(2, 16, 128, 8).transpose(0, 2, 1, 3).reshape(2, 128, 128))
    bup = f("L0_ffn_bup").reshape(6, 8, 128).transpose(2, 0, 1).reshape(128, 48)
    bsw = f("L0_ffn_bsw").reshape(6, 8, 128).transpose(2, 0, 1).reshape(128, 48)
    bsu = f("L0_ffn_bsu").reshape(2, 8, 128).transpose(2, 0, 1).reshape(128, 16)
    bsa = f("L0_ffn_bsa").reshape(8, 128).transpose(1, 0).reshape(128, 8)
    biascols = np.ascontiguousarray(np.concatenate([bup, bsw, bsu, bsa], axis=1))
    bext = np.ascontiguousarray(np.concatenate([f("L0_ffn_bdn"), f("L0_ffn_bsd").reshape(1, -1)], 0))
    common = dict(
        wq=f("L0_attn_Wq"), wk=f("L0_attn_Wk"), wv=f("L0_attn_Wv"), wo=f("L0_attn_Wo"),
        wg3h=wg3h, dlog=f("L0_attn_decay_logits").reshape(1, 3),
        oscale=np.asarray(inputs["L0_attn_out_scale"], np.float32).reshape(1, 1),
        fgh=fgh, bgh=f("L0_ffn_bg").reshape(1, 16),
        fwup=f("L0_ffn_Wup"), fwsw=f("L0_ffn_Wsw"), fwdn=f("L0_ffn_Wdn"),
        fwsu=f("L0_ffn_Wsu"), fwsa=f("L0_ffn_Wsa"), fwsd=f("L0_ffn_Wsd"),
        biascols=biascols, bexth=bext,
    )
    common.update(_consts())
    x = np.asarray(inputs["L0_attn_x"], np.float32)
    maps = []
    for b in range(x.shape[0]):
        m = dict(common)
        m["xT"] = np.ascontiguousarray(x[b, :T].T)
        maps.append(m)
    return maps


def kernel(**inputs):
    T = 4096
    nc = build_nc(T)
    in_maps = _prep_inputs(inputs, T)
    res = run_bass_kernel_spmd(nc, in_maps, core_ids=list(range(8)))
    out = np.stack([np.ascontiguousarray(np.asarray(r["x2T"]).T) for r in res.results], 0)
    return out.astype(np.float32)


# version 07
# speedup vs baseline: 1.2520x; 1.0493x over previous
import contextlib
import numpy as np
import ml_dtypes
import concourse.bass as bass
import concourse.mybir as mybir
from concourse.bass_utils import run_bass_kernel_spmd

F32 = mybir.dt.float32
BF16 = mybir.dt.bfloat16
AF = mybir.ActivationFunctionType
ALU = mybir.AluOpType
AX = mybir.AxisListType
BF = ml_dtypes.bfloat16

D = 2048
AD = 512
KD = 16
ADT = 4
C = 128
E = 1024
ET = 8
NU = 8
HT = 16
TB = 512
EPS = 1e-5


class Sems:
    def __init__(self, nc, stack, names):
        self.h = {}
        self.c = {}
        for n in names:
            self.h[n] = stack.enter_context(nc.semaphore(n))
            self.c[n] = 0

    def dma(self, eng, dst, src, name):
        eng.dma_start(dst, src).then_inc(self.h[name], 16)
        self.c[name] += 16
        return self.c[name]

    def inc(self, inst, name):
        inst.then_inc(self.h[name], 1)
        self.c[name] += 1
        return self.c[name]

    def flush(self, eng, inst, name="fl"):
        inst.then_inc(self.h[name], 1)
        self.c[name] += 1
        eng.wait_ge(self.h[name], self.c[name])


def rep_ap(t, offset_elems, pstride, reps, inner):
    h = t.tensor if hasattr(t, "tensor") else t
    return bass.AP(h, offset_elems, [[pstride, 128], [0, reps], [1, inner]])


def build_nc(T, dbg=False, extk=()):
    def KINDF(name):
        if dbg or name in extk:
            return dict(kind="ExternalOutput")
        return {}
    NCH = T // C
    NB = T // TB
    nc = bass.Bass("TRN2", target_bir_lowering=False)

    xT = nc.dram_tensor("xT", [D, T], F32, kind="ExternalInput")
    wq = nc.dram_tensor("wq", [D, AD], F32, kind="ExternalInput")
    wk = nc.dram_tensor("wk", [3, D, AD], F32, kind="ExternalInput")
    wv = nc.dram_tensor("wv", [3, D, AD], F32, kind="ExternalInput")
    wo = nc.dram_tensor("wo", [AD, D], F32, kind="ExternalInput")
    wg3h = nc.dram_tensor("wg3h", [128, 48], F32, kind="ExternalInput")
    dlog = nc.dram_tensor("dlog", [1, 3], F32, kind="ExternalInput")
    oscale = nc.dram_tensor("oscale", [1, 1], F32, kind="ExternalInput")
    fgh = nc.dram_tensor("fgh", [2, 128, 128], F32, kind="ExternalInput")
    bgh = nc.dram_tensor("bgh", [1, 16], F32, kind="ExternalInput")
    fwup = nc.dram_tensor("fwup", [6, D, E], F32, kind="ExternalInput")
    fwsw = nc.dram_tensor("fwsw", [6, E, E], F32, kind="ExternalInput")
    fwdn = nc.dram_tensor("fwdn", [6, E, D], F32, kind="ExternalInput")
    fwsu = nc.dram_tensor("fwsu", [D, D], F32, kind="ExternalInput")
    fwsa = nc.dram_tensor("fwsa", [E, E], F32, kind="ExternalInput")
    fwsd = nc.dram_tensor("fwsd", [D, D], F32, kind="ExternalInput")
    biascols = nc.dram_tensor("biascols", [128, 120], F32, kind="ExternalInput")
    bexth = nc.dram_tensor("bexth", [7, D], F32, kind="ExternalInput")
    eye_bf_d = nc.dram_tensor("eye_bfd", [128, 128], BF16, kind="ExternalInput")
    eye_f_d = nc.dram_tensor("eye_fd", [128, 128], F32, kind="ExternalInput")
    ones1f_d = nc.dram_tensor("ones1fd", [1, 128], F32, kind="ExternalInput")
    onescol_d = nc.dram_tensor("onescold", [128, 1], BF16, kind="ExternalInput")
    selbf_d = nc.dram_tensor("selbfd", [8, 768], BF16, kind="ExternalInput")
    ubits_d = nc.dram_tensor("ubitsd", [128, 7], F32, kind="ExternalInput")
    c1bits_d = nc.dram_tensor("c1bitsd", [128, 7], F32, kind="ExternalInput")
    expnbits_d = nc.dram_tensor("expnbitsd", [128, 896], BF16, kind="ExternalInput")
    causT_d = nc.dram_tensor("causTd", [128, 128], BF16, kind="ExternalInput")

    y_s = nc.dram_tensor("y_s", [D, T], F32, **KINDF("y_s"))
    r0_s = nc.dram_tensor("r0_s", [D, T], F32, **KINDF("r0_s"))
    r0hi_s = nc.dram_tensor("r0hi_s", [D, T], BF16, **KINDF("r0hi_s"))
    r0lo_s = nc.dram_tensor("r0lo_s", [D, T], BF16, **KINDF("r0lo_s"))
    xt1hi_s = nc.dram_tensor("xt1hi_s", [D, T], BF16, **KINDF("xt1hi_s"))
    xt1lo_s = nc.dram_tensor("xt1lo_s", [D, T], BF16, **KINDF("xt1lo_s"))
    wup_s = nc.dram_tensor("wup_s", [NU, 8, 128, KD, 128], BF16)
    wsw_s = nc.dram_tensor("wsw_s", [NU, 8, 128, ET, 128], BF16)
    wdn_s = nc.dram_tensor("wdn_s", [HT, 128, NU, ET, 128], BF16)
    x2T = nc.dram_tensor("x2T", [D, T], F32, kind="ExternalOutput")
    dbgout = nc.dram_tensor("dbgout", [128, 32768], F32, kind="ExternalOutput") if dbg else None

    def fm(dr):
        return dr.rearrange("(k p) t -> p k t", p=128)

    with contextlib.ExitStack() as top:
        S = Sems(nc, top, ["ld", "st", "xld", "wld", "dld", "mm", "ev", "evv", "prog", "fl"])
        sb = lambda name, shape, dt: top.enter_context(nc.sbuf_tensor(name, shape, dt))

        eye_bf = sb("eye_bf", [128, 128], BF16)
        eye_f = sb("eye_f", [128, 128], F32)
        ones1f = sb("ones1f", [1, 128], F32)
        onescol = sb("onescol", [128, 1], BF16)
        selbf = sb("selbf", [8, 768], BF16)
        wghi = sb("wghi", [128, 2, 16, 8], BF16)
        wglo = sb("wglo", [128, 2, 16, 8], BF16)
        bgb = sb("bgb", [128, 16], F32)
        bcols = sb("bcols", [128, 120], F32)
        bext_bf = sb("bext_bf", [8, D], BF16)
        osc_col = sb("osc_col", [128, 1], F32)

        with contextlib.ExitStack() as pa:
            sa = lambda name, shape, dt: pa.enter_context(nc.sbuf_tensor(name, shape, dt))

            wo_sb = sa("wo_sb", [128, ADT, D], BF16)
            wloc = sa("wloc", [128, 3, 128], BF16)
            cmb = sa("cmb", [128, 3, 128], BF16)
            Isc = sa("Isc", [128, 3, 128], BF16)
            wtmp = sa("wtmp", [128, 128], BF16)
            dvec = sa("dvec", [128, 12], F32)
            dvecs = [dvec[:, i : i + 1] for i in range(12)]
            decu_sb = sa("decu_sb", [128, 8], F32)
            rowd = sa("rowd", [1, 4], F32)
            paA1w = pa.enter_context(contextlib.ExitStack())
            sw_ = lambda name, shape, dt: paA1w.enter_context(nc.sbuf_tensor(name, shape, dt))
            wq_sb = sw_("wq_sb", [128, KD, AD], BF16)
            wk_sb = sw_("wk_sb", [128, 3, KD, AD], BF16)
            wv_sb = sw_("wv_sb", [128, 3, KD, AD], BF16)
            wg3_sb = sw_("wg3_sb", [128, KD, 3], BF16)

            with contextlib.ExitStack() as ws:
                psA_misc = ws.enter_context(nc.psum_tensor("psW_misc", [128, 512], F32))
                scr_f = ws.enter_context(nc.sbuf_tensor("scr_f", [128, 160], F32))
                ubits = ws.enter_context(nc.sbuf_tensor("ubits", [128, 7], F32))
                c1bits = ws.enter_context(nc.sbuf_tensor("c1bits", [128, 7], F32))
                expnbits = ws.enter_context(nc.sbuf_tensor("expnbits", [128, 896], BF16))
                causT = ws.enter_context(nc.sbuf_tensor("causT", [128, 128], BF16))

                with nc.Block() as blk:
                    @blk.vector
                    def _(v):
                        v.memset(bext_bf[:], 0.0)
                with nc.Block() as blk:
                    @blk.sync
                    def _(e):
                        for dst, src in [
                            (eye_bf[:], eye_bf_d), (eye_f[:], eye_f_d), (ones1f[:], ones1f_d),
                            (onescol[:], onescol_d), (selbf[:], selbf_d), (ubits[:], ubits_d),
                            (c1bits[:], c1bits_d), (expnbits[:], expnbits_d), (causT[:], causT_d),
                            (bcols[:], biascols), (rowd[0:1, 0:3], dlog),
                            (scr_f[0:1, 0:1], oscale), (scr_f[0:1, 1:17], bgh),
                        ]:
                            S.dma(e, dst, src[:], "ld")

                    @blk.gpsimd
                    def _(g):
                        S.dma(g, bext_bf[0:7, :], bexth[:], "st")
                        S.dma(g, wq_sb[:], wq.rearrange("(k p) d -> p k d", p=128), "st")
                        for i in range(3):
                            S.dma(g, wk_sb[:, i], wk[i].rearrange("(k p) d -> p k d", p=128), "st")
                            S.dma(g, wv_sb[:, i], wv[i].rearrange("(k p) d -> p k d", p=128), "st")
                        S.dma(g, wo_sb[:], wo.rearrange("(k p) d -> p k d", p=128), "st")
                        S.dma(g, wg3_sb[:], wg3h.rearrange("p (k d) -> p k d", d=3), "st")
                        for u in range(NU):
                            if u < 6:
                                upsrc = fwup[u].rearrange("(k p) e -> p k e", p=128)
                                swsrc = fwsw[u].rearrange("(k p) e -> p k e", p=128)
                                dnsrc = fwdn[u].rearrange("(k p) (m j) -> p k m j", p=128, j=128)
                            else:
                                upsrc = fwsu.rearrange("(k p) e -> p k e", p=128)[:, :, (u - 6) * E : (u - 5) * E]
                                swsrc = fwsa.rearrange("(k p) e -> p k e", p=128)
                                dnsrc = fwsd[(u - 6) * E : (u - 5) * E, :].rearrange("(k p) (m j) -> p k m j", p=128, j=128)
                            for cc in range(8):
                                S.dma(g, wup_s[u, cc], upsrc[:, :, cc * 128 : (cc + 1) * 128], "st")
                                S.dma(g, wsw_s[u, cc], swsrc[:, :, cc * 128 : (cc + 1) * 128], "st")
                            for kk in range(ET):
                                S.dma(g, wdn_s[:, :, u, kk, :].rearrange("m p j -> p m j"),
                                      dnsrc[:, kk], "st")

                with nc.Block() as blk:
                    @blk.tensor
                    def _(t):
                        t.wait_ge(S.h["ld"], S.c["ld"])
                        t.matmul(psA_misc[:, 0:1], ones1f[:], scr_f[0:1, 0:1], start=True, stop=True)
                        t.matmul(psA_misc[:, 16:32], ones1f[:], scr_f[0:1, 1:17], start=True, stop=True)
                with nc.Block() as blk:
                    @blk.vector
                    def _(v):
                        v.tensor_copy(osc_col[:], psA_misc[:, 0:1])
                        v.tensor_copy(bgb[:], psA_misc[:, 16:32])
                with nc.Block() as blk:
                    @blk.vector
                    def _(v):
                        v.wait_ge(S.h["st"], S.c["st"])

                for j in range(2):
                    with nc.Block() as blk:
                        @blk.sync
                        def _(e, j=j):
                            S.dma(e, scr_f[:, 32:160], fgh[j], "ld")
                    with nc.Block() as blk:
                        @blk.vector
                        def _(v, j=j):
                            v.wait_ge(S.h["ld"], S.c["ld"])
                            hi = wghi[:, j].rearrange("p k d -> p (k d)")
                            lo = wglo[:, j].rearrange("p k d -> p (k d)")
                            v.tensor_copy(hi, scr_f[:, 32:160])
                            v.tensor_tensor(lo, scr_f[:, 32:160], hi, ALU.subtract)

                for i in range(3):
                    with nc.Block() as blk:
                        @blk.tensor
                        def _(t, i=i):
                            t.matmul(psA_misc[:, 32:33], ones1f[:], rowd[0:1, i : i + 1], start=True, stop=True)
                    with nc.Block() as blk:
                        @blk.scalar
                        def _(sc):
                            sc.activation(dvecs[0], psA_misc[:, 32:33], AF.Sigmoid)
                    with nc.Block() as blk:
                        @blk.vector
                        def _(v, i=i):
                            F = lambda inst: S.flush(v, inst)
                            for b in range(1, 7):
                                F(v.tensor_mul(dvecs[b], dvecs[b - 1], dvecs[b - 1]))
                            F(v.tensor_mul(dvecs[7], dvecs[6], dvecs[6]))
                            for tgt, bits in ((i, ubits), (3 + i, c1bits)):
                                cur = 8
                                for b in range(7):
                                    F(v.tensor_scalar_add(dvecs[10], dvecs[b], -1.0))
                                    F(v.tensor_scalar(dvecs[11], bits[:, b : b + 1], dvecs[10], 1.0, ALU.mult, ALU.add))
                                    if b == 0:
                                        F(v.tensor_copy(dvecs[cur], dvecs[11]))
                                    else:
                                        F(v.tensor_mul(dvecs[cur ^ 1], dvecs[cur], dvecs[11]))
                                        cur ^= 1
                                F(v.tensor_copy(decu_sb[:, tgt : tgt + 1], dvecs[cur]))
                            F(v.tensor_copy(wloc[:, i], causT[:]))
                            cur = 0
                            for b in range(7):
                                F(v.tensor_scalar_add(dvecs[10], dvecs[b], -1.0))
                                F(v.tensor_scalar(wtmp[:], expnbits[:, b * 128 : (b + 1) * 128], dvecs[10], 1.0, ALU.mult, ALU.add))
                                if cur == 0:
                                    F(v.tensor_mul(cmb[:, i], wloc[:, i], wtmp[:]))
                                else:
                                    F(v.tensor_mul(wloc[:, i], cmb[:, i], wtmp[:]))
                                cur ^= 1
                            if cur == 1:
                                F(v.tensor_copy(wloc[:, i], cmb[:, i]))
                            F(v.tensor_scalar(Isc[:, i], eye_bf[:], dvecs[7], None, ALU.mult))

            RQ, RKF, RKT, RV, RGB, RW = 0, 512, 2048, 3584, 5120, 5888
            qkv_s = nc.dram_tensor("qkv_s", [NB, 128, 4, RW], BF16)
            ao_s = nc.dram_tensor("ao_s", [D, T], F32)

            def emit_rows512(v_or_none, ps_sum, ps_sq, rows, RSW):
                with nc.Block() as blk:
                    @blk.vector
                    def _(v):
                        F = lambda inst: S.flush(v, inst)
                        F(v.tensor_scalar(RSW[0], ps_sum[0:1, 0:512], 1.0 / D, None, ALU.mult))
                        F(v.tensor_scalar(RSW[1], ps_sq[0:1, 0:512], 1.0 / D, None, ALU.mult))
                        F(v.tensor_mul(RSW[2], RSW[0], RSW[0]))
                        F(v.tensor_sub(RSW[4], RSW[1], RSW[2]))
                        F(v.tensor_scalar_add(RSW[3], RSW[4], EPS))
                with nc.Block() as blk:
                    @blk.scalar
                    def _(sc):
                        S.flush(sc, sc.activation(RSW[2], RSW[3], AF.Ln))
                        S.flush(sc, sc.activation(RSW[4], RSW[2], AF.Exp, bias=0.0, scale=-0.5))
                with nc.Block() as blk:
                    @blk.vector
                    def _(v):
                        F = lambda inst: S.flush(v, inst)
                        F(v.tensor_mul(RSW[1], RSW[4], RSW[4]))
                        F(v.tensor_mul(RSW[2], RSW[1], RSW[3]))
                        F(v.tensor_scalar(RSW[1], RSW[2], -0.5, 1.5, ALU.mult, ALU.add))
                        F(v.tensor_mul(RSW[6], RSW[4], RSW[1]))
                        F(v.tensor_mul(RSW[5], RSW[0], RSW[6]))

            with contextlib.ExitStack() as pa1:
                s1 = lambda name, shape, dt: pa1.enter_context(nc.sbuf_tensor(name, shape, dt))
                ps_proj = pa1.enter_context(nc.psum_tensor("ps_proj", [128, 2048], F32))
                ps_tp = pa1.enter_context(nc.psum_tensor("ps_tp", [128, 1024], BF16))
                ps_s1 = pa1.enter_context(nc.psum_tensor("ps_s1", [128, 512], F32))
                ps_s2 = pa1.enter_context(nc.psum_tensor("ps_s2", [128, 512], F32))
                ps_b1 = pa1.enter_context(nc.psum_tensor("ps_b1", [128, 512], F32))

                xq = s1("xq", [128, 2, 1024], F32)
                st1 = s1("st1", [128, 1024], BF16)
                st2 = s1("st2", [128, 1024], BF16)
                ta = s1("ta", [128, 1024], F32)
                xn_blk = s1("xn_blk", [128, KD, 512], BF16)
                rows = s1("rows", [1, 2560], F32)
                bc_blk = s1("bc_blk", [128, 1024], F32)
                qtok = s1("qtok", [128, AD], BF16)
                ktok3 = s1("ktok3", [128, 3, AD], BF16)
                qfm_s1 = s1("qfm_s1", [128, 512], BF16)
                kfm_s1 = s1("kfm_s1", [128, 1536], BF16)
                kt_s1 = s1("kt_s1", [128, 1536], BF16)
                v_s1 = s1("v_s1", [128, 1536], BF16)
                gb_s1 = s1("gb_s1", [128, 768], BF16)
                g3b = s1("g3b", [128, 12], F32)
                e_blk = s1("e_blk", [128, 12], F32)
                gsum = s1("gsum", [128, 8], F32)
                gd_blk = s1("gd_blk", [128, 4, 6], F32)
                gdT_blk = s1("gdT_blk", [8, 512], BF16)
                RSA = [rows[0:1, i * 512 : (i + 1) * 512] for i in range(5)]
                bc_h = bc_blk.tensor if hasattr(bc_blk, "tensor") else bc_blk
                a1_st = {}

                def rows5_and_bcast(psum1, psum2):
                    with nc.Block() as blk:
                        @blk.vector
                        def _(v):
                            F = lambda inst: S.flush(v, inst)
                            F(v.tensor_scalar(RSA[0], psum1[0:1, 0:512], 1.0 / D, None, ALU.mult))
                            F(v.tensor_scalar(RSA[1], psum2[0:1, 0:512], 1.0 / D, None, ALU.mult))
                            F(v.tensor_mul(RSA[2], RSA[0], RSA[0]))
                            F(v.tensor_sub(RSA[3], RSA[1], RSA[2]))
                            F(v.tensor_scalar_add(RSA[1], RSA[3], EPS))
                    with nc.Block() as blk:
                        @blk.scalar
                        def _(sc):
                            S.flush(sc, sc.activation(RSA[2], RSA[1], AF.Ln))
                            S.flush(sc, sc.activation(RSA[3], RSA[2], AF.Exp, bias=0.0, scale=-0.5))
                    with nc.Block() as blk:
                        @blk.vector
                        def _(v):
                            F = lambda inst: S.flush(v, inst)
                            F(v.tensor_mul(RSA[2], RSA[3], RSA[3]))
                            F(v.tensor_mul(RSA[4], RSA[2], RSA[1]))
                            F(v.tensor_scalar(RSA[2], RSA[4], -0.5, 1.5, ALU.mult, ALU.add))
                            F(v.tensor_mul(RSA[4], RSA[3], RSA[2]))
                            F(v.tensor_mul(RSA[3], RSA[0], RSA[4]))
                    with nc.Block() as blk:
                        @blk.tensor
                        def _(t):
                            t.matmul(ps_b1[:], ones1f[:], RSA[3], start=True, stop=True)
                            t.matmul(psum1[:, 0:512], ones1f[:], RSA[4], start=True, stop=True)
                    with nc.Block() as blk:
                        @blk.vector
                        def _(v):
                            S.flush(v, v.tensor_copy(bc_blk[:, 0:512], ps_b1[:]))
                            S.flush(v, v.tensor_copy(bc_blk[:, 512:1024], psum1[:, 0:512]))

                for g in range(NB):
                    for q8 in range(8):
                        with nc.Block() as blk:
                            @blk.sync
                            def _(e, g=g, q8=q8):
                                S.dma(e, xq[:, q8 % 2].rearrange("p (k t) -> p k t", t=64),
                                      fm(xT)[:, :, g * TB + q8 * 64 : g * TB + (q8 + 1) * 64], "xld")
                        with nc.Block() as blk:
                            @blk.scalar
                            def _(sc, q8=q8):
                                sc.wait_ge(S.h["xld"], S.c["xld"])
                                sc.activation(st1[:], xq[:, q8 % 2], AF.Identity)
                                sc.activation(st2[:], xq[:, q8 % 2], AF.Square)
                        with nc.Block() as blk:
                            @blk.tensor
                            def _(t, q8=q8):
                                for k in range(KD):
                                    t.matmul(ps_s1[0:1, q8 * 64 : (q8 + 1) * 64], onescol[:], st1[:, k * 64 : (k + 1) * 64],
                                             start=(k == 0), stop=(k == KD - 1))
                                for k in range(KD):
                                    t.matmul(ps_s2[0:1, q8 * 64 : (q8 + 1) * 64], onescol[:], st2[:, k * 64 : (k + 1) * 64],
                                             start=(k == 0), stop=(k == KD - 1))
                    rows5_and_bcast(ps_s1, ps_s2)
                    for q8 in range(8):
                        with nc.Block() as blk:
                            @blk.sync
                            def _(e, g=g, q8=q8):
                                S.dma(e, xq[:, q8 % 2].rearrange("p (k t) -> p k t", t=64),
                                      fm(xT)[:, :, g * TB + q8 * 64 : g * TB + (q8 + 1) * 64], "xld")
                        with nc.Block() as blk:
                            @blk.vector
                            def _(v, q8=q8):
                                v.wait_ge(S.h["xld"], S.c["xld"])
                                rrep = bass.AP(bc_h, 512 + q8 * 64, [[1024, 128], [0, KD], [1, 64]])
                                mrep = bass.AP(bc_h, q8 * 64, [[1024, 128], [0, KD], [1, 64]])
                                v3 = lambda buf: buf.rearrange("p (k t) -> p k t", t=64)
                                S.flush(v, v.tensor_tensor(v3(ta[:]), v3(xq[:, q8 % 2]), rrep, ALU.mult))
                                v.tensor_tensor(xn_blk[:, :, q8 * 64 : (q8 + 1) * 64], v3(ta[:]), mrep, ALU.subtract)

                    for sub in range(4):
                        projs = [("q", None), ("k", 0), ("v", 0), ("k", 1), ("v", 1), ("k", 2), ("v", 2)]
                        base_mm = S.c["mm"]
                        base_ev = S.c["ev"]
                        with nc.Block() as blk:
                            @blk.tensor
                            def _(t, sub=sub, base_ev=base_ev):
                                for gi, (kind, i) in enumerate(projs):
                                    if gi >= 4:
                                        t.wait_ge(S.h["ev"], base_ev + gi - 3)
                                    sl = ps_proj[:, (gi % 4) * 512 : (gi % 4 + 1) * 512]
                                    w = wq_sb[:] if kind == "q" else (wk_sb[:, i] if kind == "k" else wv_sb[:, i])
                                    for k in range(KD):
                                        mmi = t.matmul(sl, xn_blk[:, k, sub * 128 : (sub + 1) * 128], w[:, k, :],
                                                       start=(k == 0), stop=(k == KD - 1))
                                    S.inc(mmi, "mm")
                                for k in range(KD):
                                    mmi = t.matmul(ps_s1[:, 0:3], xn_blk[:, k, sub * 128 : (sub + 1) * 128], wg3_sb[:, k, :],
                                                   start=(k == 0), stop=(k == KD - 1))
                                S.inc(mmi, "mm")

                            @blk.vector
                            def _(v, g=g, sub=sub, base_mm=base_mm):
                                prev = a1_st.get((g, sub - 1)) if sub > 0 else (a1_st.get((g - 1, 3)) if g > 0 else None)
                                if prev:
                                    v.wait_ge(S.h["st"], prev)
                                for gi, (kind, i) in enumerate(projs):
                                    v.wait_ge(S.h["mm"], base_mm + gi + 1)
                                    sl = ps_proj[:, (gi % 4) * 512 : (gi % 4 + 1) * 512]
                                    if kind == "q":
                                        ev = v.tensor_copy(qtok[:], sl)
                                    elif kind == "k":
                                        v.tensor_copy(ktok3[:, i], sl)
                                        ev = v.tensor_scalar(kt_s1[:, i * 512 : (i + 1) * 512], sl, decu_sb[:, i : i + 1], None, ALU.mult)
                                    else:
                                        ev = v.tensor_copy(v_s1[:, i * 512 : (i + 1) * 512], sl)
                                    S.inc(ev, "ev")
                                v.wait_ge(S.h["mm"], base_mm + 8)
                                v.tensor_copy(g3b[:, sub * 3 : sub * 3 + 3], ps_s1[:, 0:3])

                        base_mm = S.c["mm"]
                        base_ev = S.c["ev"]
                        with nc.Block() as blk:
                            @blk.tensor
                            def _(t, base_ev=base_ev):
                                for bi2, srcs in enumerate([[("q", None), ("k", 0)], [("k", 1), ("k", 2)]]):
                                    if bi2 == 1:
                                        t.wait_ge(S.h["ev"], base_ev + 1)
                                    for si, (kind, i) in enumerate(srcs):
                                        src2 = qtok[:] if kind == "q" else ktok3[:, i]
                                        for jj in range(ADT):
                                            mmi = t.transpose(ps_tp[:, (si * 4 + jj) * 128 : (si * 4 + jj + 1) * 128],
                                                              src2[:, jj * 128 : (jj + 1) * 128], eye_bf[:])
                                    S.inc(mmi, "mm")

                            @blk.vector
                            def _(v, base_mm=base_mm):
                                v.wait_ge(S.h["mm"], base_mm + 1)
                                v.tensor_copy(qfm_s1[:], ps_tp[:, 0:512])
                                ev = v.tensor_copy(kfm_s1[:, 0:512], ps_tp[:, 512:1024])
                                S.inc(ev, "ev")
                                v.wait_ge(S.h["mm"], base_mm + 2)
                                v.tensor_copy(kfm_s1[:, 512:1024], ps_tp[:, 0:512])
                                v.tensor_copy(kfm_s1[:, 1024:1536], ps_tp[:, 512:1024])

                        with nc.Block() as blk:
                            @blk.gpsimd
                            def _(gp, g=g, sub=sub):
                                S.dma(gp, qkv_s[g][:, sub, RQ : RQ + 512], qfm_s1[:], "st")
                                S.dma(gp, qkv_s[g][:, sub, RKF : RKF + 1536], kfm_s1[:], "st")
                                S.dma(gp, qkv_s[g][:, sub, RKT : RKT + 1536], kt_s1[:], "st")
                                S.dma(gp, qkv_s[g][:, sub, RV : RV + 1536], v_s1[:], "st")
                                a1_st[(g, sub)] = S.c["st"]

                    with nc.Block() as blk:
                        @blk.scalar
                        def _(sc):
                            sc.activation(e_blk[:], g3b[:], AF.Exp)
                    with nc.Block() as blk:
                        @blk.vector
                        def _(v):
                            F = lambda inst: S.flush(v, inst)
                            for sub in range(4):
                                F(v.reduce_sum(gsum[:, sub : sub + 1], e_blk[:, sub * 3 : sub * 3 + 3], axis=AX.X))
                            F(v.reciprocal(gsum[:, 4:8], gsum[:, 0:4]))
                            for sub in range(4):
                                F(v.tensor_scalar(gd_blk[:, sub, 0:3], e_blk[:, sub * 3 : sub * 3 + 3], gsum[:, 4 + sub : 5 + sub], None, ALU.mult))
                            for i in range(3):
                                F(v.tensor_scalar(gd_blk[:, :, 3 + i : 4 + i], gd_blk[:, :, i : i + 1], decu_sb[:, 3 + i : 4 + i], None, ALU.mult))
                    base_mm = S.c["mm"]
                    with nc.Block() as blk:
                        @blk.tensor
                        def _(t):
                            for sub in range(4):
                                mmi = t.transpose(ps_s2[0:6, sub * 128 : (sub + 1) * 128], gd_blk[:, sub, 0:6], eye_f[:])
                            S.inc(mmi, "mm")
                        @blk.vector
                        def _(v, base_mm=base_mm):
                            v.wait_ge(S.h["mm"], base_mm + 1)
                            v.tensor_copy(gdT_blk[0:6, :], ps_s2[0:6, 0:512])
                    for sub in range(4):
                        base_mm = S.c["mm"]
                        base_ev = S.c["ev"]
                        with nc.Block() as blk:
                            @blk.tensor
                            def _(t, sub=sub, base_ev=base_ev):
                                for n in range(6):
                                    if n >= 2:
                                        t.wait_ge(S.h["ev"], base_ev + n - 1)
                                    ps = ps_b1[:, 0:128] if n % 2 == 0 else ps_s1[:, 0:128]
                                    mmi = t.matmul(ps, selbf[0:6, n * 128 : (n + 1) * 128],
                                                   gdT_blk[0:6, sub * 128 : (sub + 1) * 128], start=True, stop=True)
                                    S.inc(mmi, "mm")
                            @blk.vector
                            def _(v, g=g, sub=sub, base_mm=base_mm):
                                prev = a1_st.get((g, sub - 1)) if sub > 0 else None
                                if prev:
                                    v.wait_ge(S.h["st"], prev)
                                for n in range(6):
                                    v.wait_ge(S.h["mm"], base_mm + n + 1)
                                    ps = ps_b1[:, 0:128] if n % 2 == 0 else ps_s1[:, 0:128]
                                    i = n % 3
                                    off = i * 128 if n < 3 else 384 + i * 128
                                    ev = v.tensor_copy(gb_s1[:, off : off + 128], ps)
                                    S.inc(ev, "ev")
                        with nc.Block() as blk:
                            @blk.gpsimd
                            def _(gp, g=g, sub=sub):
                                S.dma(gp, qkv_s[g][:, sub, RGB : RGB + 768], gb_s1[:], "st")
                                a1_st[(g, sub)] = S.c["st"]

            paA1w.close()

            with contextlib.ExitStack() as pa2:
                s2 = lambda name, shape, dt: pa2.enter_context(nc.sbuf_tensor(name, shape, dt))
                ps_big = pa2.enter_context(nc.psum_tensor("psA2_big", [128, 2048], F32))
                ps_sc = pa2.enter_context(nc.psum_tensor("psA2_sc", [128, 512], F32))
                ps_retr = pa2.enter_context(nc.psum_tensor("psA2_retr", [128, 512], F32))

                S_sb = s2("S_sb", [128, 3, ADT, AD], BF16)
                rec = s2("rec", [128, 2, RW], BF16)
                qt = s2("qt", [128, 3, AD], BF16)
                cmbA = s2("cmbA", [128, 3, 128], BF16)
                scb = s2("scb", [128, 3, 128], BF16)
                retrbf = s2("retrbf", [128, AD], BF16)
                aobuf = s2("aobuf", [128, 2048], F32)
                rec_h = rec.tensor if hasattr(rec, "tensor") else rec
                a2_store_at = {}
                a2_prog_after = {}
                a2_ld_val = {}
                with nc.Block() as blk:
                    @blk.vector
                    def _(v):
                        for i in range(3):
                            v.memset(S_sb[:, i].rearrange("p a b -> p (a b)"), 0.0)

                for ci in range(NCH):
                    c = NCH - 1 - ci
                    bi = ci % 2
                    roff = bi * RW

                    with nc.Block() as blk:
                        @blk.sync
                        def _(e, ci=ci, c=c):
                            if ci == 0:
                                e.wait_ge(S.h["st"], S.c["st"])
                                a2_ld_val[0] = S.dma(e, rec[:, 0], qkv_s[c // 4][:, c % 4, :], "ld")
                            if ci + 1 < NCH:
                                if ci >= 1:
                                    e.wait_ge(S.h["prog"], a2_prog_after[ci - 1])
                                cn = NCH - 1 - (ci + 1)
                                a2_ld_val[ci + 1] = S.dma(e, rec[:, (ci + 1) % 2], qkv_s[cn // 4][:, cn % 4, :], "ld")
                    with nc.Block() as blk:
                        @blk.vector
                        def _(v, ci=ci, bi=bi, roff=roff):
                            v.wait_ge(S.h["ld"], a2_ld_val[ci])
                            if ci >= 1:
                                v.wait_ge(S.h["st"], a2_store_at[ci - 1])
                            for i in range(3):
                                S.flush(v, v.tensor_tensor(cmbA[:, i], wloc[:, i], rec[:, bi, RGB + i * 128 : RGB + (i + 1) * 128], ALU.mult))
                                brep = bass.AP(rec_h, roff + RGB + 384 + i * 128, [[2 * RW, 128], [0, ADT], [1, 128]])
                                S.flush(v, v.tensor_tensor(qt[:, i], rec[:, bi, RQ : RQ + 512].rearrange("p (a b) -> p a b", b=128), brep, ALU.mult))

                    first_retr = [True]
                    for i in range(3):
                        base_mm = S.c["mm"]
                        with nc.Block() as blk:
                            @blk.tensor
                            def _(t, i=i, bi=bi):
                                for kd in range(ADT):
                                    mmi = t.matmul(ps_sc[:, 128:256], rec[:, bi, RKF + i * 512 + kd * 128 : RKF + i * 512 + (kd + 1) * 128],
                                                   rec[:, bi, RQ + kd * 128 : RQ + (kd + 1) * 128], start=(kd == 0), stop=(kd == ADT - 1))
                                S.inc(mmi, "mm")
                        with nc.Block() as blk:
                            @blk.vector
                            def _(v, i=i, base_mm=base_mm):
                                v.wait_ge(S.h["mm"], base_mm + 1)
                                v.tensor_tensor(scb[:, i], ps_sc[:, 128:256], cmbA[:, i], ALU.mult)
                        with nc.Block() as blk:
                            @blk.tensor
                            def _(t, i=i, bi=bi):
                                for m in range(ADT):
                                    t.matmul(ps_retr[:, m * 128 : (m + 1) * 128],
                                             rec[:, bi, RV + i * 512 + m * 128 : RV + i * 512 + (m + 1) * 128],
                                             scb[:, i], start=first_retr[0], stop=False)
                                    first_retr[0] = False
                                for m in range(ADT):
                                    for kd in range(ADT):
                                        last = (i == 2 and m == ADT - 1 and kd == ADT - 1)
                                        t.matmul(ps_retr[:, m * 128 : (m + 1) * 128],
                                                 S_sb[:, i, kd, m * 128 : (m + 1) * 128],
                                                 qt[:, i, kd * 128 : (kd + 1) * 128], start=False, stop=last)
                    with nc.Block() as blk:
                        @blk.vector
                        def _(v):
                            v.tensor_scalar(retrbf[:], ps_retr[:], osc_col[:], None, ALU.mult)

                    base_mm = S.c["mm"]
                    base_ev = S.c["ev"]
                    with nc.Block() as blk:
                        @blk.tensor
                        def _(t, ci=ci, bi=bi, base_ev=base_ev):
                            for i in range(3):
                                if i >= 1:
                                    t.wait_ge(S.h["ev"], base_ev + i)
                                for m in range(ADT):
                                    t.matmul(ps_big[:, m * 512 : (m + 1) * 512],
                                             rec[:, bi, RKT + i * 512 + m * 128 : RKT + i * 512 + (m + 1) * 128],
                                             rec[:, bi, RV + i * 512 : RV + (i + 1) * 512], start=True, stop=False)
                                    mmi = t.matmul(ps_big[:, m * 512 : (m + 1) * 512], Isc[:, i],
                                                   S_sb[:, i, m, :], start=False, stop=True)
                                S.inc(mmi, "mm")
                            t.nop().then_inc(S.h["prog"], 1)
                            S.c["prog"] += 1
                            a2_prog_after[ci] = S.c["prog"]

                        @blk.vector
                        def _(v, base_mm=base_mm):
                            for i in range(3):
                                v.wait_ge(S.h["mm"], base_mm + i + 1)
                                ev = v.tensor_copy(S_sb[:, i].rearrange("p a b -> p (a b)"), ps_big[:])
                                S.inc(ev, "ev")

                    with nc.Block() as blk:
                        @blk.tensor
                        def _(t):
                            for m in range(HT):
                                for kd in range(ADT):
                                    t.matmul(ps_big[:, m * 128 : (m + 1) * 128], wo_sb[:, kd, m * 128 : (m + 1) * 128],
                                             retrbf[:, kd * 128 : (kd + 1) * 128], start=(kd == 0), stop=(kd == ADT - 1))
                    with nc.Block() as blk:
                        @blk.vector
                        def _(v):
                            v.tensor_copy(aobuf[:], ps_big[:])
                        @blk.gpsimd
                        def _(gp, c=c, ci=ci):
                            S.dma(gp, fm(ao_s)[:, :, c * C : (c + 1) * C], aobuf[:].rearrange("p (k t) -> p k t", t=C), "st")
                            a2_store_at[ci] = S.c["st"]

            with contextlib.ExitStack() as pa3:
                s3 = lambda name, shape, dt: pa3.enter_context(nc.sbuf_tensor(name, shape, dt))
                ps_s1 = pa3.enter_context(nc.psum_tensor("psA3_s1", [128, 512], F32))
                ps_s2 = pa3.enter_context(nc.psum_tensor("psA3_s2", [128, 512], F32))
                ps_b1 = pa3.enter_context(nc.psum_tensor("psA3_b1", [128, 512], F32))
                ps_b2 = pa3.enter_context(nc.psum_tensor("psA3_b2", [128, 512], F32))

                aq = s3("aq", [128, 2, 2048], F32)
                xq3 = s3("xq3", [128, 2, 2048], F32)
                yq = s3("yq", [128, 2, 2048], F32)
                ta3 = s3("ta3", [128, 2048], F32)
                r0q = s3("r0q", [128, 2, 2048], F32)
                hib = s3("hib", [128, 2, 2048], BF16)
                lob = s3("lob", [128, 2, 2048], BF16)
                st1b = s3("st1b", [128, D], BF16)
                st2b = s3("st2b", [128, D], BF16)
                rows3 = s3("rows3", [1, 3584], F32)
                bc3 = s3("bc3", [128, 1024], F32)
                RSW3 = [rows3[0:1, i * 512 : (i + 1) * 512] for i in range(7)]
                bc3_h = bc3.tensor if hasattr(bc3, "tensor") else bc3
                a3st = {}
                y_store_after = {}

                for g in range(NB):
                    for q in range(4):
                        with nc.Block() as blk:
                            @blk.sync
                            def _(e, g=g, q=q):
                                if g == 0 and q == 0:
                                    e.wait_ge(S.h["st"], S.c["st"])
                                S.dma(e, aq[:, q % 2].rearrange("p (k t) -> p k t", t=C),
                                      fm(ao_s)[:, :, g * TB + q * C : g * TB + (q + 1) * C], "ld")
                        with nc.Block() as blk:
                            @blk.scalar
                            def _(sc, q=q):
                                sc.wait_ge(S.h["ld"], S.c["ld"])
                                sc.activation(st1b[:], aq[:, q % 2], AF.Identity)
                                sc.activation(st2b[:], aq[:, q % 2], AF.Square)
                        with nc.Block() as blk:
                            @blk.tensor
                            def _(t, q=q):
                                for k in range(KD):
                                    t.matmul(ps_s1[0:1, q * 128 : (q + 1) * 128], onescol[:], st1b[:, k * 128 : (k + 1) * 128],
                                             start=(k == 0), stop=(k == KD - 1))
                                for k in range(KD):
                                    t.matmul(ps_s2[0:1, q * 128 : (q + 1) * 128], onescol[:], st2b[:, k * 128 : (k + 1) * 128],
                                             start=(k == 0), stop=(k == KD - 1))
                    emit_rows512(None, ps_s1, ps_s2, rows3, RSW3)
                    with nc.Block() as blk:
                        @blk.tensor
                        def _(t):
                            t.matmul(ps_b1[:], ones1f[:], RSW3[5], start=True, stop=True)
                            t.matmul(ps_b2[:], ones1f[:], RSW3[6], start=True, stop=True)
                    with nc.Block() as blk:
                        @blk.vector
                        def _(v):
                            S.flush(v, v.tensor_copy(bc3[:, 0:512], ps_b1[:]))
                            S.flush(v, v.tensor_copy(bc3[:, 512:1024], ps_b2[:]))
                    for q in range(4):
                        with nc.Block() as blk:
                            @blk.sync
                            def _(e, g=g, q=q):
                                S.dma(e, aq[:, q % 2].rearrange("p (k t) -> p k t", t=C),
                                      fm(ao_s)[:, :, g * TB + q * C : g * TB + (q + 1) * C], "ld")
                                S.dma(e, xq3[:, q % 2].rearrange("p (k t) -> p k t", t=C),
                                      fm(xT)[:, :, g * TB + q * C : g * TB + (q + 1) * C], "ld")
                        with nc.Block() as blk:
                            @blk.vector
                            def _(v, q=q):
                                v.wait_ge(S.h["ld"], S.c["ld"])
                                rrep = bass.AP(bc3_h, 512 + q * 128, [[1024, 128], [0, KD], [1, 128]])
                                mrep = bass.AP(bc3_h, q * 128, [[1024, 128], [0, KD], [1, 128]])
                                a3 = lambda buf: buf.rearrange("p (k t) -> p k t", t=C)
                                S.flush(v, v.tensor_tensor(a3(ta3[:]), a3(aq[:, q % 2]), rrep, ALU.mult))
                                S.flush(v, v.tensor_tensor(a3(aq[:, q % 2]), a3(ta3[:]), mrep, ALU.subtract))
                                v.tensor_tensor(yq[:, q % 2], aq[:, q % 2], xq3[:, q % 2], ALU.add)
                        with nc.Block() as blk:
                            @blk.scalar
                            def _(sc, q=q):
                                sc.activation(st1b[:], yq[:, q % 2], AF.Identity)
                                sc.activation(st2b[:], yq[:, q % 2], AF.Square)
                            @blk.gpsimd
                            def _(gp, g=g, q=q):
                                S.dma(gp, fm(y_s)[:, :, g * TB + q * C : g * TB + (q + 1) * C],
                                      yq[:, q % 2].rearrange("p (k t) -> p k t", t=C), "st")
                                y_store_after[g] = S.c["st"]
                        with nc.Block() as blk:
                            @blk.tensor
                            def _(t, q=q):
                                for k in range(KD):
                                    t.matmul(ps_s1[0:1, q * 128 : (q + 1) * 128], onescol[:], st1b[:, k * 128 : (k + 1) * 128],
                                             start=(k == 0), stop=(k == KD - 1))
                                for k in range(KD):
                                    t.matmul(ps_s2[0:1, q * 128 : (q + 1) * 128], onescol[:], st2b[:, k * 128 : (k + 1) * 128],
                                             start=(k == 0), stop=(k == KD - 1))
                    emit_rows512(None, ps_s1, ps_s2, rows3, RSW3)
                    with nc.Block() as blk:
                        @blk.tensor
                        def _(t):
                            t.matmul(ps_b1[:], ones1f[:], RSW3[5], start=True, stop=True)
                            t.matmul(ps_b2[:], ones1f[:], RSW3[6], start=True, stop=True)
                    with nc.Block() as blk:
                        @blk.vector
                        def _(v):
                            S.flush(v, v.tensor_copy(bc3[:, 0:512], ps_b1[:]))
                            S.flush(v, v.tensor_copy(bc3[:, 512:1024], ps_b2[:]))
                    for q in range(4):
                        with nc.Block() as blk:
                            @blk.sync
                            def _(e, g=g, q=q):
                                e.wait_ge(S.h["st"], y_store_after[g])
                                S.dma(e, yq[:, q % 2].rearrange("p (k t) -> p k t", t=C),
                                      fm(y_s)[:, :, g * TB + q * C : g * TB + (q + 1) * C], "ld")
                        with nc.Block() as blk:
                            @blk.vector
                            def _(v, g=g, q=q):
                                v.wait_ge(S.h["ld"], S.c["ld"])
                                n = g * 4 + q
                                if n >= 2 and (n - 2) in a3st:
                                    v.wait_ge(S.h["st"], a3st[n - 2])
                                rrep = bass.AP(bc3_h, 512 + q * 128, [[1024, 128], [0, KD], [1, 128]])
                                mrep = bass.AP(bc3_h, q * 128, [[1024, 128], [0, KD], [1, 128]])
                                a3 = lambda buf: buf.rearrange("p (k t) -> p k t", t=C)
                                S.flush(v, v.tensor_tensor(a3(ta3[:]), a3(yq[:, q % 2]), rrep, ALU.mult))
                                S.flush(v, v.tensor_tensor(a3(r0q[:, q % 2]), a3(ta3[:]), mrep, ALU.subtract))
                                S.flush(v, v.tensor_copy(hib[:, q % 2], r0q[:, q % 2]))
                                v.tensor_tensor(lob[:, q % 2], r0q[:, q % 2], hib[:, q % 2], ALU.subtract)
                        with nc.Block() as blk:
                            @blk.gpsimd
                            def _(gp, g=g, q=q):
                                sl = lambda dr: fm(dr)[:, :, g * TB + q * C : g * TB + (q + 1) * C]
                                v3 = lambda buf: buf.rearrange("p (k t) -> p k t", t=C)
                                S.dma(gp, sl(r0_s), v3(r0q[:, q % 2]), "st")
                                S.dma(gp, sl(r0hi_s), v3(hib[:, q % 2]), "st")
                                S.dma(gp, sl(r0lo_s), v3(lob[:, q % 2]), "st")
                                a3st[g * 4 + q] = S.c["st"]

        TT = 1024
        NT = T // TT
        with contextlib.ExitStack() as pb:
            sbb = lambda name, shape, dt: pb.enter_context(nc.sbuf_tensor(name, shape, dt))
            psA = pb.enter_context(nc.psum_tensor("psA", [128, 2048], F32))
            psW = pb.enter_context(nc.psum_tensor("psW", [128, 1024], F32))
            psG = pb.enter_context(nc.psum_tensor("psG", [128, 512], F32))
            psS = pb.enter_context(nc.psum_tensor("psS", [128, 512], F32))

            xtu = sbb("xtu", [128, KD, TT], BF16)
            ph = sbb("ph", [128, KD, TT], BF16)
            acc = sbb("acc", [128, HT, TT], F32)
            upb = sbb("upb", [128, 3, KD, 128], BF16)
            swb = sbb("swb", [128, 4, ET, 128], BF16)
            dnb = sbb("dnb", [128, 4, ET, 128], BF16)
            wbn = sbb("wbn", [128, TT], BF16)
            wTx = sbb("wTx", [8, TT], BF16)
            xgl = sbb("xgl", [128, 2, KD, 128], BF16)
            rrow = sbb("rrow", [128, 2, TT], F32)
            zrow = sbb("zrow", [128, 2, 2, 512], BF16)
            stg = sbb("stg", [128, 2, TT], F32)
            zsqb = sbb("zsqb", [128, TT], BF16)
            gg = sbb("gg", [128, 56], F32)

            Q = {"tensor": [], "scalar": [], "vector": [], "sync": [], "gpsimd": []}
            TRACE = {k: [] for k in Q}
            nc._dbg_trace = (dict(S.c), TRACE)

            def em(eng, fn):
                Q[eng].append(fn)

            def wv(eng, sem, cnt):
                assert cnt <= S.c[sem], (eng, sem, cnt, S.c[sem])
                TRACE[eng].append(("wait", sem, cnt))
                em(eng, lambda e, sem=sem, cnt=cnt: e.wait_ge(S.h[sem], cnt))

            def wr(eng, rel):
                if rel is not None:
                    wv(eng, rel[0], rel[1])

            def dmaQ(eng, sem, dst, src):
                S.c[sem] += 16
                TRACE[eng].append(("inc", sem, 16))
                em(eng, lambda e, dst=dst, src=src, sem=sem: e.dma_start(dst, src).then_inc(S.h[sem], 16))
                return S.c[sem]

            def incQ(eng, sem, fn):
                S.c[sem] += 1
                TRACE[eng].append(("inc", sem, 1))
                em(eng, lambda e, fn=fn, sem=sem: fn(e).then_inc(S.h[sem], 1))
                return S.c[sem]

            def flQ(eng, fn):
                S.c["fl"] += 1
                cnt = S.c["fl"]
                TRACE[eng].append(("inc", "fl", 1))
                TRACE[eng].append(("wait", "fl", cnt))
                em(eng, lambda e, fn=fn, cnt=cnt: (fn(e).then_inc(S.h["fl"], 1),
                                                   e.wait_ge(S.h["fl"], cnt)))

            TOT_TILES = 2 * NT
            TOT_UP = TOT_TILES * NU * 8
            TOT_SW = TOT_TILES * NU * 8
            TOT_DN = TOT_TILES * NU * HT
            up_ld = {}; up_cons = {}
            sw_ld = {}; sw_cons = {}
            dn_ld = {}; dn_cons = {}

            def issue_up(g):
                if g >= TOT_UP:
                    return
                n, c = divmod(g % (NU * 8), 8)
                r = up_cons.get(g - 3)
                if r is not None:
                    wv("sync", "mm", r)
                up_ld[g] = dmaQ("sync", "wld", upb[:, g % 3], wup_s[n, c])

            def issue_sw(g):
                if g >= TOT_SW:
                    return
                n, c = divmod(g % (NU * 8), 8)
                r = sw_cons.get(g - 4)
                if r is not None:
                    wv("sync", "mm", r)
                sw_ld[g] = dmaQ("sync", "wld", swb[:, g % 4], wsw_s[n, c])

            def issue_dn(g):
                if g >= TOT_DN:
                    return
                n, m = divmod(g % (NU * HT), HT)
                r = dn_cons.get(g - 4)
                if r is not None:
                    wv("gpsimd", "mm", r)
                dn_ld[g] = dmaQ("gpsimd", "dld", dnb[:, g % 4], wdn_s[m, :, n, :, :])

            bankrel = [None, None, None, None]
            bank_i = [0]

            def acquire_bank():
                b = bank_i[0] % 4
                bank_i[0] += 1
                wr("tensor", bankrel[b])
                return b

            carry = {"xtu": None, "ph": None, "psG": None, "psW": None, "wTx": None,
                     "stg": None, "rrow": {0: None, 1: None}, "zrow": {0: None, 1: None},
                     "xgl": {0: None, 1: None}}
            zstore = {}
            tile_iter = 0
            for j in range(2):
                hi_src = r0hi_s if j == 0 else xt1hi_s
                lo_src = r0lo_s if j == 0 else xt1lo_s
                for t in range(NT):
                    base = t * TT
                    if tile_iter == 0:
                        st_all = S.c["st"]
                        wv("sync", "st", st_all)
                        wv("scalar", "st", st_all)
                        wv("gpsimd", "st", st_all)
                    if j == 1:
                        wv("sync", "st", zstore[t])
                        wv("scalar", "st", zstore[t])
                    wr("sync", carry["xtu"])
                    xtu_ldv = dmaQ("sync", "ld", xtu[:], fm(hi_src)[:, :, base:base + TT])
                    if tile_iter == 0:
                        for g0 in range(3):
                            issue_up(g0)
                        for g0 in range(4):
                            issue_sw(g0)
                        for g0 in range(4):
                            issue_dn(g0)

                    wr("vector", carry["wTx"])
                    flQ("vector", lambda e: e.memset(wTx[:], 1.0))
                    G8 = gg[:, 0:8]
                    g_mm = {}; gate_rel = {}; gtr = {}; wTxrel = {}
                    for c in range(8):
                        if c >= 2:
                            wv("scalar", "mm", g_mm[c - 2])
                        else:
                            wr("scalar", carry["xgl"][c])
                        xldc = dmaQ("scalar", "xld", xgl[:, c % 2],
                                    fm(lo_src)[:, :, base + c * 128:base + (c + 1) * 128])
                        wv("tensor", "xld", xldc)
                        if c == 0:
                            wv("tensor", "ld", xtu_ldv)
                            wr("tensor", carry["psG"])
                        else:
                            wv("tensor", "evv", gate_rel[c - 1])

                        def g_mms(e, c=c, j=j):
                            for k in range(KD):
                                e.matmul(psG[:, 0:8], xtu[:, k, c * 128:(c + 1) * 128],
                                         wghi[:, j, k, :], start=(k == 0), stop=False)
                            for k in range(KD):
                                e.matmul(psG[:, 0:8], xtu[:, k, c * 128:(c + 1) * 128],
                                         wglo[:, j, k, :], start=False, stop=False)
                            for k in range(KD):
                                mi = e.matmul(psG[:, 0:8], xgl[:, c % 2, k, :],
                                              wghi[:, j, k, :], start=False, stop=(k == KD - 1))
                            return mi
                        g_mm[c] = incQ("tensor", "mm", g_mms)

                        wv("vector", "mm", g_mm[c])
                        if c > 0:
                            wv("vector", "mm", gtr[c - 1])
                        gate_rel[c] = incQ("vector", "evv",
                                           lambda e, j=j: e.tensor_tensor(G8, psG[:, 0:8], bgb[:, j * 8:(j + 1) * 8], ALU.add))
                        wv("vector", "evv", gate_rel[c])
                        flQ("vector", lambda e: e.tensor_single_scalar(gg[:, 8:16], G8, G8[:, 0:1], ALU.is_lt))
                        cur = 8
                        for mI in range(1, 8):
                            flQ("vector", lambda e, mI=mI: e.tensor_single_scalar(gg[:, 24:32], G8, G8[:, mI:mI + 1], ALU.is_lt))
                            nxt = 24 - cur
                            flQ("vector", lambda e, cur=cur, nxt=nxt: e.tensor_add(gg[:, nxt:nxt + 8], gg[:, cur:cur + 8], gg[:, 24:32]))
                            cur = nxt
                        flQ("vector", lambda e, cur=cur: e.tensor_single_scalar(gg[:, 32:40], gg[:, cur:cur + 8], 5.5, ALU.is_lt))
                        ggv = incQ("vector", "evv", lambda e: e.reduce_max(gg[:, 48:49], G8, axis=AX.X, negate=True))
                        wv("scalar", "evv", ggv)
                        gexp = incQ("scalar", "ev", lambda e: e.activation(gg[:, 40:48], gg[:, 0:8], AF.Exp, bias=gg[:, 48:49], scale=1.0))
                        wv("vector", "ev", gexp)
                        flQ("vector", lambda e: e.tensor_mul(gg[:, 24:32], gg[:, 40:48], gg[:, 32:40]))
                        flQ("vector", lambda e: e.reduce_sum(gg[:, 49:50], gg[:, 24:32], axis=AX.X))
                        flQ("vector", lambda e: e.reciprocal(gg[:, 50:51], gg[:, 49:50]))
                        gn = incQ("vector", "evv", lambda e: e.tensor_scalar(gg[:, 0:8], gg[:, 24:32], gg[:, 50:51], None, ALU.mult))
                        wv("tensor", "evv", gn)
                        if c > 0:
                            wv("tensor", "evv", wTxrel[c - 1])
                        gtr[c] = incQ("tensor", "mm", lambda e: e.transpose(psG[0:8, 128:256], gg[:, 0:8], eye_f[:]))
                        wv("vector", "mm", gtr[c])
                        wTxrel[c] = incQ("vector", "evv",
                                         lambda e, c=c: e.tensor_copy(wTx[0:6, c * 128:(c + 1) * 128], psG[0:6, 128:256]))
                    wtx_done = wTxrel[7]
                    carry["xgl"] = {0: ("mm", g_mm[6]), 1: ("mm", g_mm[7])}

                    selmm = {}; wbnrel = {}
                    updone = {}; swdone = {}; dndone = {}
                    for n in range(NU):
                        if n < 6:
                            wv("tensor", "evv", wtx_done if n == 0 else wbnrel[n - 1])
                            if n == 0:
                                wr("tensor", carry["psW"])

                            def sel_mms(e, n=n):
                                e.matmul(psW[:, 0:512], selbf[0:8, n * 128:(n + 1) * 128], wTx[0:8, 0:512], start=True, stop=True)
                                return e.matmul(psW[:, 512:1024], selbf[0:8, n * 128:(n + 1) * 128], wTx[0:8, 512:1024], start=True, stop=True)
                            selmm[n] = incQ("tensor", "mm", sel_mms)
                            wv("vector", "mm", selmm[n])
                            wbnrel[n] = incQ("vector", "evv", lambda e: e.tensor_copy(wbn[:], psW[:]))

                        last_up_ev = {}
                        for c in range(8):
                            gidx = tile_iter * 64 + n * 8 + c
                            wv("tensor", "wld", up_ld[gidx])
                            for sub in range(2):
                                b = acquire_bank()
                                if tile_iter == 0 and n == 0 and c == 0 and sub == 0:
                                    wv("tensor", "ld", xtu_ldv)

                                def up_mms(e, c=c, sub=sub, b=b, slot=gidx % 3):
                                    for k in range(KD):
                                        mi = e.matmul(psA[:, b * 512:(b + 1) * 512], upb[:, slot, k, :],
                                                      xtu[:, k, sub * 512:(sub + 1) * 512],
                                                      start=(k == 0), stop=(k == KD - 1))
                                    return mi
                                gm = incQ("tensor", "mm", up_mms)
                                wv("scalar", "mm", gm)
                                if c == 0 and sub == 0:
                                    if n == 0:
                                        wr("scalar", carry["ph"])
                                    else:
                                        wv("scalar", "mm", swdone[n - 1])
                                bcol = (bcols[:, n * 8 + c:n * 8 + c + 1] if n < 6
                                        else bcols[:, 96 + (n - 6) * 8 + c:96 + (n - 6) * 8 + c + 1])
                                eva = incQ("scalar", "ev",
                                           lambda e, c=c, sub=sub, b=b, bcol=bcol: e.activation(
                                               ph[:, c, sub * 512:(sub + 1) * 512], psA[:, b * 512:(b + 1) * 512],
                                               AF.Identity, bias=bcol, scale=1.0))
                                bankrel[b] = ("ev", eva)
                                last_up_ev[sub] = eva
                            up_cons[gidx] = S.c["mm"]
                            issue_up(gidx + 3)
                        updone[n] = S.c["mm"]

                        h2done = {}
                        for c in range(8):
                            gidx = tile_iter * 64 + n * 8 + c
                            wv("tensor", "wld", sw_ld[gidx])
                            for sub in range(2):
                                b = acquire_bank()
                                if c == 0:
                                    wv("tensor", "ev", last_up_ev[sub])

                                def sw_mms(e, c=c, sub=sub, b=b, slot=gidx % 4):
                                    for k in range(ET):
                                        mi = e.matmul(psA[:, b * 512:(b + 1) * 512], swb[:, slot, k, :],
                                                      ph[:, k, sub * 512:(sub + 1) * 512],
                                                      start=(k == 0), stop=(k == ET - 1))
                                    return mi
                                gm = incQ("tensor", "mm", sw_mms)
                                wv("scalar", "mm", gm)
                                if c == 0 and sub == 0 and n > 0:
                                    wv("scalar", "mm", dndone[n - 1])
                                bcol = (bcols[:, 48 + n * 8 + c:48 + n * 8 + c + 1] if n < 6
                                        else bcols[:, 112 + c:112 + c + 1])
                                eva = incQ("scalar", "ev",
                                           lambda e, c=c, sub=sub, b=b, bcol=bcol: e.activation(
                                               ph[:, ET + c, sub * 512:(sub + 1) * 512], psA[:, b * 512:(b + 1) * 512],
                                               AF.Silu, bias=bcol, scale=1.0))
                                bankrel[b] = ("ev", eva)
                                wv("vector", "ev", eva)
                                h2s = ph[:, ET + c, sub * 512:(sub + 1) * 512]
                                if n < 6:
                                    flQ("vector", lambda e, h2s=h2s, c=c, sub=sub: e.tensor_tensor(
                                        h2s, h2s, ph[:, c, sub * 512:(sub + 1) * 512], ALU.mult))
                                    h2done[sub] = incQ("vector", "evv",
                                                       lambda e, h2s=h2s, sub=sub: e.tensor_tensor(
                                                           h2s, h2s, wbn[:, sub * 512:(sub + 1) * 512], ALU.mult))
                                else:
                                    h2done[sub] = incQ("vector", "evv",
                                                       lambda e, h2s=h2s, c=c, sub=sub: e.tensor_tensor(
                                                           h2s, h2s, ph[:, c, sub * 512:(sub + 1) * 512], ALU.mult))
                            sw_cons[gidx] = S.c["mm"]
                            issue_sw(gidx + 4)
                        swdone[n] = S.c["mm"]

                        for m in range(HT):
                            gidx = tile_iter * 128 + n * 16 + m
                            wv("tensor", "dld", dn_ld[gidx])
                            for sub in range(2):
                                b = acquire_bank()
                                if m == 0:
                                    wv("tensor", "evv", h2done[sub])

                                def dn_mms(e, n=n, m=m, sub=sub, b=b, slot=gidx % 4):
                                    for k in range(ET):
                                        mi = e.matmul(psA[:, b * 512:(b + 1) * 512], dnb[:, slot, k, :],
                                                      ph[:, ET + k, sub * 512:(sub + 1) * 512],
                                                      start=(k == 0), stop=(k == ET - 1 and n != 0))
                                    if n == 0:
                                        mi = e.matmul(psA[:, b * 512:(b + 1) * 512],
                                                      bext_bf[0:7, m * 128:(m + 1) * 128],
                                                      wTx[0:7, sub * 512:(sub + 1) * 512], start=False, stop=True)
                                    return mi
                                gm = incQ("tensor", "mm", dn_mms)
                                wv("vector", "mm", gm)
                                accsl = acc[:, m, sub * 512:(sub + 1) * 512]
                                if n == 0:
                                    av = incQ("vector", "evv",
                                              lambda e, accsl=accsl, b=b: e.tensor_copy(accsl, psA[:, b * 512:(b + 1) * 512]))
                                else:
                                    av = incQ("vector", "evv",
                                              lambda e, accsl=accsl, b=b: e.tensor_tensor(accsl, accsl, psA[:, b * 512:(b + 1) * 512], ALU.add))
                                bankrel[b] = ("evv", av)
                            dn_cons[gidx] = S.c["mm"]
                            issue_dn(gidx + 4)
                        dndone[n] = S.c["mm"]

                    if j == 0:
                        rrel = {}; zst = {}
                        for m in range(HT):
                            if m >= 2:
                                wv("scalar", "evv", rrel[m - 2])
                            else:
                                wr("scalar", carry["rrow"][m % 2])
                            rld = dmaQ("scalar", "xld", rrow[:, m % 2], fm(r0_s)[:, m, base:base + TT])
                            wv("vector", "xld", rld)
                            flQ("vector", lambda e, m=m: e.tensor_tensor(acc[:, m, :], acc[:, m, :], rrow[:, m % 2], ALU.add))
                            for sub in range(2):
                                if m >= 1:
                                    wv("vector", "st", zst[(m - 1, sub)])
                                elif carry["zrow"][sub] is not None:
                                    wr("vector", carry["zrow"][sub])
                                asub = acc[:, m, sub * 512:(sub + 1) * 512]
                                flQ("vector", lambda e, asub=asub, sub=sub: e.tensor_copy(zrow[:, sub, 0], asub))
                                zev = incQ("vector", "evv",
                                           lambda e, asub=asub, sub=sub: e.tensor_tensor(zrow[:, sub, 1], asub, zrow[:, sub, 0], ALU.subtract))
                                wv("gpsimd", "evv", zev)
                                dmaQ("gpsimd", "st", fm(xt1hi_s)[:, m, base + sub * 512:base + (sub + 1) * 512], zrow[:, sub, 0])
                                zst[(m, sub)] = dmaQ("gpsimd", "st", fm(xt1lo_s)[:, m, base + sub * 512:base + (sub + 1) * 512], zrow[:, sub, 1])
                            rrel[m] = zev
                        zstore[t] = S.c["st"]
                        carry["xtu"] = ("mm", updone[NU - 1])
                        carry["ph"] = None
                        carry["rrow"] = {0: ("evv", rrel[14]), 1: ("evv", rrel[15])}
                        carry["zrow"] = {0: ("st", zst[(15, 0)]), 1: ("st", zst[(15, 1)])}
                        carry["psW"] = ("evv", wbnrel[5])
                        carry["psG"] = ("evv", wtx_done)
                        carry["wTx"] = ("mm", selmm[5])
                    else:
                        zrel = {}; smm = {}
                        for m in range(HT):
                            if m >= 2:
                                wv("scalar", "evv", zrel[m - 2])
                            else:
                                wr("scalar", carry["rrow"][m % 2])
                            rld = dmaQ("scalar", "xld", rrow[:, m % 2], fm(r0_s)[:, m, base:base + TT])
                            wv("vector", "xld", rld)
                            flQ("vector", lambda e, m=m: e.tensor_tensor(acc[:, m, :], acc[:, m, :], rrow[:, m % 2], ALU.add))
                            flQ("vector", lambda e, m=m: e.tensor_copy(xtu[:, m, :], acc[:, m, :]))
                            zev = incQ("vector", "evv",
                                       lambda e, m=m: e.tensor_tensor(ph[:, m, :], acc[:, m, :], xtu[:, m, :], ALU.subtract))
                            zrel[m] = zev
                            wv("scalar", "evv", zev)
                            if m >= 1:
                                wv("scalar", "mm", smm[m - 1])
                            sqe = incQ("scalar", "ev", lambda e, m=m: e.activation(zsqb[:], xtu[:, m, :], AF.Square))
                            wv("tensor", "evv", zev)
                            wv("tensor", "ev", sqe)
                            if m == 0:
                                wv("tensor", "evv", wbnrel[5])

                            def stat_mms(e, m=m):
                                for sub in range(2):
                                    tgt = psG if sub == 0 else psS
                                    e.matmul(tgt[0:1, 0:512], onescol[:], xtu[:, m, sub * 512:(sub + 1) * 512],
                                             start=(m == 0), stop=False)
                                    e.matmul(tgt[0:1, 0:512], onescol[:], ph[:, m, sub * 512:(sub + 1) * 512],
                                             start=False, stop=(m == HT - 1))
                                e.matmul(psW[0:1, 0:512], onescol[:], zsqb[:, 0:512],
                                         start=(m == 0), stop=(m == HT - 1))
                                return e.matmul(psW[0:1, 512:1024], onescol[:], zsqb[:, 512:1024],
                                                start=(m == 0), stop=(m == HT - 1))
                            smm[m] = incQ("tensor", "mm", stat_mms)

                        Asl = [rrow[0:1, i // 2, (i % 2) * 512:(i % 2) * 512 + 512] for i in range(4)]
                        Bsl = [stg[0:1, i // 2, (i % 2) * 512:(i % 2) * 512 + 512] for i in range(4)]
                        wv("vector", "mm", smm[HT - 1])
                        wr("vector", carry["stg"])
                        rows_done = None
                        for sub in range(2):
                            s1ap = (psG if sub == 0 else psS)[0:1, 0:512]
                            s2ap = psW[0:1, sub * 512:(sub + 1) * 512]
                            MR = Bsl[0] if sub == 0 else Bsl[2]
                            R = Bsl[1] if sub == 0 else Bsl[3]
                            s0, s1_, s2_, s3_, s4_ = Asl[0], Asl[1], Asl[2], Asl[3], Bsl[3]
                            flQ("vector", lambda e, s0=s0, s1ap=s1ap: e.tensor_scalar(s0, s1ap, 1.0 / D, None, ALU.mult))
                            flQ("vector", lambda e, s1_=s1_, s2ap=s2ap: e.tensor_scalar(s1_, s2ap, 1.0 / D, None, ALU.mult))
                            flQ("vector", lambda e, s2_=s2_, s0=s0: e.tensor_mul(s2_, s0, s0))
                            flQ("vector", lambda e, s4_=s4_, s1_=s1_, s2_=s2_: e.tensor_sub(s4_, s1_, s2_))
                            ve = incQ("vector", "evv", lambda e, s3_=s3_, s4_=s4_: e.tensor_scalar_add(s3_, s4_, EPS))
                            wv("scalar", "evv", ve)
                            flQ("scalar", lambda e, s2_=s2_, s3_=s3_: e.activation(s2_, s3_, AF.Ln))
                            se = incQ("scalar", "ev", lambda e, s4_=s4_, s2_=s2_: e.activation(s4_, s2_, AF.Exp, bias=0.0, scale=-0.5))
                            wv("vector", "ev", se)
                            flQ("vector", lambda e, s1_=s1_, s4_=s4_: e.tensor_mul(s1_, s4_, s4_))
                            flQ("vector", lambda e, s2_=s2_, s1_=s1_, s3_=s3_: e.tensor_mul(s2_, s1_, s3_))
                            flQ("vector", lambda e, s1_=s1_, s2_=s2_: e.tensor_scalar(s1_, s2_, -0.5, 1.5, ALU.mult, ALU.add))
                            flQ("vector", lambda e, R=R, s4_=s4_, s1_=s1_: e.tensor_mul(R, s4_, s1_))
                            rows_done = incQ("vector", "evv", lambda e, MR=MR, s0=s0, R=R: e.tensor_mul(MR, s0, R))
                        wv("tensor", "evv", rows_done)

                        def bc_mms(e):
                            e.matmul(psW[:, 0:512], ones1f[:], Bsl[0], start=True, stop=True)
                            e.matmul(psW[:, 512:1024], ones1f[:], Bsl[2], start=True, stop=True)
                            e.matmul(psG[:, 0:512], ones1f[:], Bsl[1], start=True, stop=True)
                            return e.matmul(psS[:, 0:512], ones1f[:], Bsl[3], start=True, stop=True)
                        bc_mm = incQ("tensor", "mm", bc_mms)

                        yrel = {}; ost = {}
                        for m in range(HT):
                            if m < 2:
                                wv("scalar", "evv", rows_done)
                                wv("scalar", "evv", zrel[14 + (m % 2)])
                            else:
                                wv("scalar", "evv", yrel[m - 2])
                            yld = dmaQ("scalar", "xld", rrow[:, m % 2], fm(y_s)[:, m, base:base + TT])
                            wv("vector", "xld", yld)
                            if m == 0:
                                wv("vector", "mm", bc_mm)
                            if m >= 2:
                                wv("vector", "st", ost[m - 2])
                            orv = None
                            for sub in range(2):
                                Rps = psG if sub == 0 else psS
                                ss = stg[:, m % 2, sub * 512:(sub + 1) * 512]
                                flQ("vector", lambda e, ss=ss, m=m, sub=sub, Rps=Rps: e.tensor_tensor(
                                    ss, acc[:, m, sub * 512:(sub + 1) * 512], Rps[:, 0:512], ALU.mult))
                                flQ("vector", lambda e, ss=ss, sub=sub: e.tensor_tensor(
                                    ss, ss, psW[:, sub * 512:(sub + 1) * 512], ALU.subtract))
                                opl = lambda e, ss=ss, m=m, sub=sub: e.tensor_tensor(
                                    ss, ss, rrow[:, m % 2, sub * 512:(sub + 1) * 512], ALU.add)
                                if sub == 1:
                                    orv = incQ("vector", "evv", opl)
                                else:
                                    flQ("vector", opl)
                            yrel[m] = orv
                            wv("gpsimd", "evv", orv)
                            ost[m] = dmaQ("gpsimd", "st", fm(x2T)[:, m, base:base + TT], stg[:, m % 2])
                        carry["xtu"] = ("mm", smm[HT - 1])
                        carry["ph"] = ("mm", smm[HT - 1])
                        carry["psG"] = ("evv", yrel[15])
                        carry["psW"] = ("evv", yrel[15])
                        carry["rrow"] = {0: ("evv", yrel[14]), 1: ("evv", yrel[15])}
                        carry["stg"] = ("st", ost[15])
                        carry["wTx"] = ("mm", selmm[5])
                    tile_iter += 1

            wv("gpsimd", "st", S.c["st"])

            with nc.Block() as blk:
                @blk.tensor
                def _(te):
                    for fn in Q["tensor"]:
                        fn(te)

                @blk.scalar
                def _(sc):
                    for fn in Q["scalar"]:
                        fn(sc)

                @blk.vector
                def _(v):
                    for fn in Q["vector"]:
                        fn(v)

                @blk.sync
                def _(e):
                    for fn in Q["sync"]:
                        fn(e)

                @blk.gpsimd
                def _(gp):
                    for fn in Q["gpsimd"]:
                        fn(gp)
    return nc


def _consts():
    u = np.arange(128)
    ubits = ((u[:, None] >> np.arange(7)[None, :]) & 1).astype(np.float32)
    c1 = 127 - u
    c1bits = ((c1[:, None] >> np.arange(7)[None, :]) & 1).astype(np.float32)
    s = np.arange(128)[:, None]
    t = np.arange(128)[None, :]
    expn = np.maximum(s - t - 1, 0)
    expnbits = np.concatenate([((expn >> b) & 1).astype(np.float32) for b in range(7)], axis=1).astype(BF)
    causT = (s > t).astype(np.float32).astype(BF)
    sel = np.zeros((8, 768), np.float32)
    for n in range(6):
        sel[n, n * 128 : (n + 1) * 128] = 1.0
    return dict(
        eye_bfd=np.eye(128, dtype=np.float32).astype(BF),
        eye_fd=np.eye(128, dtype=np.float32),
        ones1fd=np.ones((1, 128), np.float32),
        onescold=np.ones((128, 1), np.float32).astype(BF),
        selbfd=sel.astype(BF),
        ubitsd=ubits, c1bitsd=c1bits, expnbitsd=expnbits, causTd=causT,
    )


def _prep_inputs(inputs, T):
    f = lambda k: np.ascontiguousarray(np.asarray(inputs[k], np.float32))
    wg3 = f("L0_attn_Wg")
    wg3h = np.ascontiguousarray(wg3.reshape(16, 128, 3).transpose(1, 0, 2).reshape(128, 48))
    fg = f("L0_ffn_Wg")
    fgh = np.ascontiguousarray(fg.reshape(2, 16, 128, 8).transpose(0, 2, 1, 3).reshape(2, 128, 128))
    bup = f("L0_ffn_bup").reshape(6, 8, 128).transpose(2, 0, 1).reshape(128, 48)
    bsw = f("L0_ffn_bsw").reshape(6, 8, 128).transpose(2, 0, 1).reshape(128, 48)
    bsu = f("L0_ffn_bsu").reshape(2, 8, 128).transpose(2, 0, 1).reshape(128, 16)
    bsa = f("L0_ffn_bsa").reshape(8, 128).transpose(1, 0).reshape(128, 8)
    biascols = np.ascontiguousarray(np.concatenate([bup, bsw, bsu, bsa], axis=1))
    bext = np.ascontiguousarray(np.concatenate([f("L0_ffn_bdn"), f("L0_ffn_bsd").reshape(1, -1)], 0))
    common = dict(
        wq=f("L0_attn_Wq"), wk=f("L0_attn_Wk"), wv=f("L0_attn_Wv"), wo=f("L0_attn_Wo"),
        wg3h=wg3h, dlog=f("L0_attn_decay_logits").reshape(1, 3),
        oscale=np.asarray(inputs["L0_attn_out_scale"], np.float32).reshape(1, 1),
        fgh=fgh, bgh=f("L0_ffn_bg").reshape(1, 16),
        fwup=f("L0_ffn_Wup"), fwsw=f("L0_ffn_Wsw"), fwdn=f("L0_ffn_Wdn"),
        fwsu=f("L0_ffn_Wsu"), fwsa=f("L0_ffn_Wsa"), fwsd=f("L0_ffn_Wsd"),
        biascols=biascols, bexth=bext,
    )
    common.update(_consts())
    x = np.asarray(inputs["L0_attn_x"], np.float32)
    maps = []
    for b in range(x.shape[0]):
        m = dict(common)
        m["xT"] = np.ascontiguousarray(x[b, :T].T)
        maps.append(m)
    return maps


def kernel(**inputs):
    T = 4096
    nc = build_nc(T)
    in_maps = _prep_inputs(inputs, T)
    res = run_bass_kernel_spmd(nc, in_maps, core_ids=list(range(8)))
    out = np.stack([np.ascontiguousarray(np.asarray(r["x2T"]).T) for r in res.results], 0)
    return out.astype(np.float32)


# version 09
# speedup vs baseline: 1.4191x; 1.1335x over previous
import contextlib
import numpy as np
import ml_dtypes
import concourse.bass as bass
import concourse.mybir as mybir
from concourse.bass_utils import run_bass_kernel_spmd

F32 = mybir.dt.float32
BF16 = mybir.dt.bfloat16
AF = mybir.ActivationFunctionType
ALU = mybir.AluOpType
AX = mybir.AxisListType
BF = ml_dtypes.bfloat16

D = 2048
AD = 512
KD = 16
ADT = 4
C = 128
E = 1024
ET = 8
NU = 8
HT = 16
TB = 512
EPS = 1e-5


class Sems:
    def __init__(self, nc, stack, names):
        self.h = {}
        self.c = {}
        for n in names:
            self.h[n] = stack.enter_context(nc.semaphore(n))
            self.c[n] = 0

    def dma(self, eng, dst, src, name):
        eng.dma_start(dst, src).then_inc(self.h[name], 16)
        self.c[name] += 16
        return self.c[name]

    def inc(self, inst, name):
        inst.then_inc(self.h[name], 1)
        self.c[name] += 1
        return self.c[name]

    def flush(self, eng, inst, name="fl"):
        inst.then_inc(self.h[name], 1)
        self.c[name] += 1
        eng.wait_ge(self.h[name], self.c[name])


def rep_ap(t, offset_elems, pstride, reps, inner):
    h = t.tensor if hasattr(t, "tensor") else t
    return bass.AP(h, offset_elems, [[pstride, 128], [0, reps], [1, inner]])


ABLOCKS = None

def build_nc(T, dbg=False, extk=()):
    def KINDF(name):
        if dbg or name in extk:
            return dict(kind="ExternalOutput")
        return {}
    NCH = T // C
    NB = T // TB
    nc = bass.Bass("TRN2", target_bir_lowering=False)

    xT = nc.dram_tensor("xT", [D, T], F32, kind="ExternalInput")
    wq = nc.dram_tensor("wq", [D, AD], F32, kind="ExternalInput")
    wk = nc.dram_tensor("wk", [3, D, AD], F32, kind="ExternalInput")
    wv = nc.dram_tensor("wv", [3, D, AD], F32, kind="ExternalInput")
    wo = nc.dram_tensor("wo", [AD, D], F32, kind="ExternalInput")
    wg3h = nc.dram_tensor("wg3h", [128, 48], F32, kind="ExternalInput")
    dlog = nc.dram_tensor("dlog", [1, 3], F32, kind="ExternalInput")
    oscale = nc.dram_tensor("oscale", [1, 1], F32, kind="ExternalInput")
    fgh = nc.dram_tensor("fgh", [2, 128, 128], F32, kind="ExternalInput")
    bgh = nc.dram_tensor("bgh", [1, 16], F32, kind="ExternalInput")
    fwup = nc.dram_tensor("fwup", [6, D, E], F32, kind="ExternalInput")
    fwsw = nc.dram_tensor("fwsw", [6, E, E], F32, kind="ExternalInput")
    fwdn = nc.dram_tensor("fwdn", [6, E, D], F32, kind="ExternalInput")
    fwsu = nc.dram_tensor("fwsu", [D, D], F32, kind="ExternalInput")
    fwsa = nc.dram_tensor("fwsa", [E, E], F32, kind="ExternalInput")
    fwsd = nc.dram_tensor("fwsd", [D, D], F32, kind="ExternalInput")
    biascols = nc.dram_tensor("biascols", [128, 120], F32, kind="ExternalInput")
    bexth = nc.dram_tensor("bexth", [7, D], F32, kind="ExternalInput")
    eye_bf_d = nc.dram_tensor("eye_bfd", [128, 128], BF16, kind="ExternalInput")
    eye_f_d = nc.dram_tensor("eye_fd", [128, 128], F32, kind="ExternalInput")
    ones1f_d = nc.dram_tensor("ones1fd", [1, 128], F32, kind="ExternalInput")
    onescol_d = nc.dram_tensor("onescold", [128, 1], BF16, kind="ExternalInput")
    selbf_d = nc.dram_tensor("selbfd", [8, 768], BF16, kind="ExternalInput")
    ubits_d = nc.dram_tensor("ubitsd", [128, 7], F32, kind="ExternalInput")
    c1bits_d = nc.dram_tensor("c1bitsd", [128, 7], F32, kind="ExternalInput")
    expnbits_d = nc.dram_tensor("expnbitsd", [128, 896], BF16, kind="ExternalInput")
    causT_d = nc.dram_tensor("causTd", [128, 128], BF16, kind="ExternalInput")

    y_s = nc.dram_tensor("y_s", [D, T], F32, **KINDF("y_s"))
    r0hi_s = nc.dram_tensor("r0hi_s", [D, T], BF16, **KINDF("r0hi_s"))
    r0lo_s = nc.dram_tensor("r0lo_s", [D, T], BF16, **KINDF("r0lo_s"))
    xt1hi_s = nc.dram_tensor("xt1hi_s", [D, T], BF16, **KINDF("xt1hi_s"))
    xt1lo_s = nc.dram_tensor("xt1lo_s", [D, T], BF16, **KINDF("xt1lo_s"))
    wqkv_s = nc.dram_tensor("wqkv_s", [7, 128, KD, AD], BF16)
    wup_s = nc.dram_tensor("wup_s", [NU, 8, 128, KD, 128], BF16)
    wsw_s = nc.dram_tensor("wsw_s", [NU, 8, 128, ET, 128], BF16)
    wdn_s = nc.dram_tensor("wdn_s", [HT, 128, NU, ET, 128], BF16)
    x2T = nc.dram_tensor("x2T", [D, T], F32, kind="ExternalOutput")
    dbgout = nc.dram_tensor("dbgout", [128, 32768], F32, kind="ExternalOutput") if dbg else None
    dbgb = nc.dram_tensor("dbgb", [128, 8192], BF16, kind="ExternalOutput") if dbg else None

    def fm(dr):
        return dr.rearrange("(k p) t -> p k t", p=128)

    with contextlib.ExitStack() as top:
        S = Sems(nc, top, ["ld", "st", "xld", "wld", "dld", "mm", "ev", "evv", "prog", "fl"])
        sb = lambda name, shape, dt: top.enter_context(nc.sbuf_tensor(name, shape, dt))

        def mk_rec():
            Q = {"tensor": [], "scalar": [], "vector": [], "sync": [], "gpsimd": []}
            TRACE = {k: [] for k in Q}
            if not hasattr(nc, "_dbg_traces"):
                nc._dbg_traces = []
            nc._dbg_traces.append((dict(S.c), TRACE))

            def em(eng, fn):
                Q[eng].append(fn)

            def wv(eng, sem, cnt):
                assert cnt <= S.c[sem], (eng, sem, cnt, S.c[sem])
                TRACE[eng].append(("wait", sem, cnt))
                em(eng, lambda e, sem=sem, cnt=cnt: e.wait_ge(S.h[sem], cnt))

            def wr(eng, rel):
                if rel is not None:
                    wv(eng, rel[0], rel[1])

            def dmaQ(eng, sem, dst, src):
                S.c[sem] += 16
                TRACE[eng].append(("inc", sem, 16))
                em(eng, lambda e, dst=dst, src=src, sem=sem: e.dma_start(dst, src).then_inc(S.h[sem], 16))
                return S.c[sem]

            def incQ(eng, sem, fn):
                S.c[sem] += 1
                TRACE[eng].append(("inc", sem, 1))
                em(eng, lambda e, fn=fn, sem=sem: fn(e).then_inc(S.h[sem], 1))
                return S.c[sem]

            def flQ(eng, fn):
                S.c["fl"] += 1
                cnt = S.c["fl"]
                TRACE[eng].append(("inc", "fl", 1))
                TRACE[eng].append(("wait", "fl", cnt))
                em(eng, lambda e, fn=fn, cnt=cnt: (fn(e).then_inc(S.h["fl"], 1),
                                                   e.wait_ge(S.h["fl"], cnt)))
            return Q, em, wv, wr, dmaQ, incQ, flQ

        eye_bf = sb("eye_bf", [128, 128], BF16)
        eye_f = sb("eye_f", [128, 128], F32)
        ones1f = sb("ones1f", [1, 128], F32)
        onescol = sb("onescol", [128, 1], BF16)
        selbf = sb("selbf", [8, 768], BF16)
        wghi = sb("wghi", [128, 2, 16, 8], BF16)
        wglo = sb("wglo", [128, 2, 16, 8], BF16)
        bgb = sb("bgb", [128, 16], F32)
        bcols = sb("bcols", [128, 120], F32)
        bext_bf = sb("bext_bf", [8, D], BF16)
        osc_col = sb("osc_col", [128, 1], F32)

        with contextlib.ExitStack() as pa:
            sa = lambda name, shape, dt: pa.enter_context(nc.sbuf_tensor(name, shape, dt))

            wo_sb = sa("wo_sb", [128, ADT, D], BF16)
            wloc = sa("wloc", [128, 3, 128], BF16)
            cmb = sa("cmb", [128, 3, 128], BF16)
            Isc = sa("Isc", [128, 3, 128], BF16)
            wtmp = sa("wtmp", [128, 128], BF16)
            dvec = sa("dvec", [128, 12], F32)
            dvecs = [dvec[:, i : i + 1] for i in range(12)]
            decu_sb = sa("decu_sb", [128, 8], F32)
            rowd = sa("rowd", [1, 4], F32)
            wg3_sb = sa("wg3_sb", [128, KD, 3], BF16)

            with contextlib.ExitStack() as ws:
                psA_misc = ws.enter_context(nc.psum_tensor("psW_misc", [128, 512], F32))
                scr_f = ws.enter_context(nc.sbuf_tensor("scr_f", [128, 160], F32))
                ubits = ws.enter_context(nc.sbuf_tensor("ubits", [128, 7], F32))
                c1bits = ws.enter_context(nc.sbuf_tensor("c1bits", [128, 7], F32))
                expnbits = ws.enter_context(nc.sbuf_tensor("expnbits", [128, 896], BF16))
                causT = ws.enter_context(nc.sbuf_tensor("causT", [128, 128], BF16))

                with nc.Block() as blk:
                    @blk.vector
                    def _(v):
                        v.memset(bext_bf[:], 0.0)
                with nc.Block() as blk:
                    @blk.sync
                    def _(e):
                        for dst, src in [
                            (eye_bf[:], eye_bf_d), (eye_f[:], eye_f_d), (ones1f[:], ones1f_d),
                            (onescol[:], onescol_d), (selbf[:], selbf_d), (ubits[:], ubits_d),
                            (c1bits[:], c1bits_d), (expnbits[:], expnbits_d), (causT[:], causT_d),
                            (bcols[:], biascols), (rowd[0:1, 0:3], dlog),
                            (scr_f[0:1, 0:1], oscale), (scr_f[0:1, 1:17], bgh),
                        ]:
                            S.dma(e, dst, src[:], "ld")

                    @blk.gpsimd
                    def _(g):
                        S.dma(g, bext_bf[0:7, :], bexth[:], "st")
                        S.dma(g, wqkv_s[0], wq.rearrange("(k p) d -> p k d", p=128), "st")
                        for i in range(3):
                            S.dma(g, wqkv_s[1 + 2 * i], wk[i].rearrange("(k p) d -> p k d", p=128), "st")
                            S.dma(g, wqkv_s[2 + 2 * i], wv[i].rearrange("(k p) d -> p k d", p=128), "st")
                        S.dma(g, wo_sb[:], wo.rearrange("(k p) d -> p k d", p=128), "st")
                        S.dma(g, wg3_sb[:], wg3h.rearrange("p (k d) -> p k d", d=3), "st")
                        for u in range(NU):
                            if u < 6:
                                upsrc = fwup[u].rearrange("(k p) e -> p k e", p=128)
                                swsrc = fwsw[u].rearrange("(k p) e -> p k e", p=128)
                                dnsrc = fwdn[u].rearrange("(k p) (m j) -> p k m j", p=128, j=128)
                            else:
                                upsrc = fwsu.rearrange("(k p) e -> p k e", p=128)[:, :, (u - 6) * E : (u - 5) * E]
                                swsrc = fwsa.rearrange("(k p) e -> p k e", p=128)
                                dnsrc = fwsd[(u - 6) * E : (u - 5) * E, :].rearrange("(k p) (m j) -> p k m j", p=128, j=128)
                            for cc in range(8):
                                S.dma(g, wup_s[u, cc], upsrc[:, :, cc * 128 : (cc + 1) * 128], "st")
                                S.dma(g, wsw_s[u, cc], swsrc[:, :, cc * 128 : (cc + 1) * 128], "st")
                            for kk in range(ET):
                                S.dma(g, wdn_s[:, :, u, kk, :].rearrange("m p j -> p m j"),
                                      dnsrc[:, kk], "st")

                with nc.Block() as blk:
                    @blk.tensor
                    def _(t):
                        t.wait_ge(S.h["ld"], S.c["ld"])
                        t.matmul(psA_misc[:, 0:1], ones1f[:], scr_f[0:1, 0:1], start=True, stop=True)
                        t.matmul(psA_misc[:, 16:32], ones1f[:], scr_f[0:1, 1:17], start=True, stop=True)
                with nc.Block() as blk:
                    @blk.vector
                    def _(v):
                        v.tensor_copy(osc_col[:], psA_misc[:, 0:1])
                        v.tensor_copy(bgb[:], psA_misc[:, 16:32])
                with nc.Block() as blk:
                    @blk.vector
                    def _(v):
                        v.wait_ge(S.h["st"], S.c["st"])

                for j in range(2):
                    with nc.Block() as blk:
                        @blk.sync
                        def _(e, j=j):
                            S.dma(e, scr_f[:, 32:160], fgh[j], "ld")
                    with nc.Block() as blk:
                        @blk.vector
                        def _(v, j=j):
                            v.wait_ge(S.h["ld"], S.c["ld"])
                            hi = wghi[:, j].rearrange("p k d -> p (k d)")
                            lo = wglo[:, j].rearrange("p k d -> p (k d)")
                            v.tensor_copy(hi, scr_f[:, 32:160])
                            v.tensor_tensor(lo, scr_f[:, 32:160], hi, ALU.subtract)

                for i in range(3):
                    with nc.Block() as blk:
                        @blk.tensor
                        def _(t, i=i):
                            t.matmul(psA_misc[:, 32:33], ones1f[:], rowd[0:1, i : i + 1], start=True, stop=True)
                    with nc.Block() as blk:
                        @blk.scalar
                        def _(sc):
                            sc.activation(dvecs[0], psA_misc[:, 32:33], AF.Sigmoid)
                    with nc.Block() as blk:
                        @blk.vector
                        def _(v, i=i):
                            F = lambda inst: S.flush(v, inst)
                            for b in range(1, 7):
                                F(v.tensor_mul(dvecs[b], dvecs[b - 1], dvecs[b - 1]))
                            F(v.tensor_mul(dvecs[7], dvecs[6], dvecs[6]))
                            for tgt, bits in ((i, ubits), (3 + i, c1bits)):
                                cur = 8
                                for b in range(7):
                                    F(v.tensor_scalar_add(dvecs[10], dvecs[b], -1.0))
                                    F(v.tensor_scalar(dvecs[11], bits[:, b : b + 1], dvecs[10], 1.0, ALU.mult, ALU.add))
                                    if b == 0:
                                        F(v.tensor_copy(dvecs[cur], dvecs[11]))
                                    else:
                                        F(v.tensor_mul(dvecs[cur ^ 1], dvecs[cur], dvecs[11]))
                                        cur ^= 1
                                F(v.tensor_copy(decu_sb[:, tgt : tgt + 1], dvecs[cur]))
                            F(v.tensor_copy(wloc[:, i], causT[:]))
                            cur = 0
                            for b in range(7):
                                F(v.tensor_scalar_add(dvecs[10], dvecs[b], -1.0))
                                F(v.tensor_scalar(wtmp[:], expnbits[:, b * 128 : (b + 1) * 128], dvecs[10], 1.0, ALU.mult, ALU.add))
                                if cur == 0:
                                    F(v.tensor_mul(cmb[:, i], wloc[:, i], wtmp[:]))
                                else:
                                    F(v.tensor_mul(wloc[:, i], cmb[:, i], wtmp[:]))
                                cur ^= 1
                            if cur == 1:
                                F(v.tensor_copy(wloc[:, i], cmb[:, i]))
                            F(v.tensor_scalar(Isc[:, i], eye_bf[:], dvecs[7], None, ALU.mult))

            RQ, RKF, RKT, RV, RGB, RW = 0, 512, 2048, 3584, 5120, 5888

            psR = pa.enter_context(nc.psum_tensor("psR", [128, 2048], F32))
            psBC = pa.enter_context(nc.psum_tensor("psBC", [128, 1024], F32))
            psTP = pa.enter_context(nc.psum_tensor("psTP", [128, 1024], BF16))
            psX = pa.enter_context(nc.psum_tensor("psX", [128, 512], F32))

            fb = sa("fb", [128, KD, TB], F32)
            st1 = sa("st1", [128, 2, TB], BF16)
            st2 = sa("st2", [128, 2, TB], BF16)
            xn = sa("xn", [128, KD, TB], BF16)
            recb = sa("recb", [128, 4, RW], BF16)
            wpj = sa("wpj", [128, 2, KD, AD], BF16)
            qkt = sa("qkt", [128, 2, TB], BF16)
            gwork = sa("gwork", [128, 16], F32)
            gdT_sb = sa("gdT_sb", [8, 128], BF16)
            S_sb = sa("S_sb", [128, 3, ADT, AD], BF16)
            qt = sa("qt", [128, 3, AD], BF16)
            cmbA = sa("cmbA", [128, 3, 128], BF16)
            scb = sa("scb", [128, 3, 128], BF16)
            retrbf = sa("retrbf", [128, AD], BF16)
            xs = sa("xs", [128, 2, TB], F32)
            ta0 = sa("ta0", [128, TB], F32)
            hl = sa("hl", [128, 2, 2, TB], BF16)
            rowsA = sa("rowsA", [1, 3584], F32)
            RS = [rowsA[0:1, i * 512:(i + 1) * 512] for i in range(7)]
            recb_h = recb.tensor if hasattr(recb, "tensor") else recb

            QA, emA, wvA, wrA, dmaA, incA, flA = mk_rec()

            def rows_mathA(s1ap, s2ap, mmcnt):
                wvA("vector", "mm", mmcnt)
                flA("vector", lambda e: e.tensor_scalar(RS[0], s1ap, 1.0 / D, None, ALU.mult))
                flA("vector", lambda e: e.tensor_scalar(RS[1], s2ap, 1.0 / D, None, ALU.mult))
                flA("vector", lambda e: e.tensor_mul(RS[2], RS[0], RS[0]))
                flA("vector", lambda e: e.tensor_sub(RS[4], RS[1], RS[2]))
                ve = incA("vector", "evv", lambda e: e.tensor_scalar_add(RS[3], RS[4], EPS))
                wvA("scalar", "evv", ve)
                flA("scalar", lambda e: e.activation(RS[2], RS[3], AF.Ln))
                se = incA("scalar", "ev", lambda e: e.activation(RS[4], RS[2], AF.Exp, bias=0.0, scale=-0.5))
                wvA("vector", "ev", se)
                flA("vector", lambda e: e.tensor_mul(RS[1], RS[4], RS[4]))
                flA("vector", lambda e: e.tensor_mul(RS[2], RS[1], RS[3]))
                flA("vector", lambda e: e.tensor_scalar(RS[1], RS[2], -0.5, 1.5, ALU.mult, ALU.add))
                flA("vector", lambda e: e.tensor_mul(RS[6], RS[4], RS[1]))
                return incA("vector", "evv", lambda e: e.tensor_mul(RS[5], RS[0], RS[6]))

            bankA = [None, None, None, None]
            bA_i = [0]

            def acqA():
                b = bA_i[0] % 4
                bA_i[0] += 1
                wrA("tensor", bankA[b])
                return b

            def bcs(e):
                e.matmul(psBC[:, 0:512], ones1f[:], RS[5], start=True, stop=True)
                return e.matmul(psBC[:, 512:1024], ones1f[:], RS[6], start=True, stop=True)

            wvA("sync", "st", S.c["st"])
            memev = incA("vector", "evv", lambda e: e.memset(S_sb[:].rearrange("p a b c -> p (a b c)"), 0.0))
            supd_done = memev
            wp_i = [0]
            wpj_ld = {}
            wpj_cons = {}
            PJS = [("q", None), ("k", 0), ("v", 0), ("k", 1), ("v", 1), ("k", 2), ("v", 2)]

            def issue_wpj():
                idx = wp_i[0]
                if idx >= 7 * NB:
                    return
                r = wpj_cons.get(idx - 2)
                if r is not None:
                    wvA("sync", "mm", r)
                wpj_ld[idx] = dmaA("sync", "wld", wpj[:, idx % 2], wqkv_s[idx % 7])
                wp_i[0] += 1

            issue_wpj()
            issue_wpj()

            fb_free = None
            fb_free_st = None
            bc_free = None
            psX_rel = None
            scan_done_mm = None
            prev_retr_mm = None
            sel_last_mm = None
            gate_tmm_prev = None
            hl_carry = {0: None, 1: None}
            tp_state = {"rel": None, "last_tmm": None}
            pend_tp = []

            def emit_tp():
                kind, i, slot, ev = pend_tp.pop(0)
                wvA("tensor", "evv", ev)
                if tp_state["rel"] is not None:
                    wrA("tensor", tp_state["rel"])

                def tmms(e, slot=slot):
                    for jj in range(ADT):
                        mi = e.transpose(psTP[:, jj * 128:(jj + 1) * 128],
                                         qkt[:, slot, jj * 128:(jj + 1) * 128], eye_bf[:])
                    return mi
                tmm = incA("tensor", "mm", tmms)
                tp_state["last_tmm"] = tmm
                wvA("vector", "mm", tmm)
                sub, off = kind
                tev = incA("vector", "evv",
                           lambda e, sub=sub, off=off: e.tensor_copy(recb[:, sub, off:off + 512], psTP[:, 0:512]))
                tp_state["rel"] = ("evv", tev)

            for gi in range(NB if ABLOCKS is None else ABLOCKS):
                g = NB - 1 - gi
                wrA("sync", fb_free)
                wrA("sync", fb_free_st)
                fb_ld = dmaA("sync", "ld", fb[:], fm(xT)[:, :, g * TB:(g + 1) * TB])

                stat_mm = {}
                for k in range(KD):
                    if k == 0:
                        wvA("scalar", "ld", fb_ld)
                    if k >= 2:
                        wvA("scalar", "mm", stat_mm[k - 2])
                    incA("scalar", "ev", lambda e, k=k: e.activation(st1[:, k % 2], fb[:, k], AF.Identity))
                    e2 = incA("scalar", "ev", lambda e, k=k: e.activation(st2[:, k % 2], fb[:, k], AF.Square))
                    wvA("tensor", "ev", e2)
                    if k == 0:
                        wrA("tensor", bc_free)

                    def smms(e, k=k):
                        e.matmul(psBC[0:1, 0:512], onescol[:], st1[:, k % 2], start=(k == 0), stop=(k == KD - 1))
                        return e.matmul(psBC[0:1, 512:1024], onescol[:], st2[:, k % 2], start=(k == 0), stop=(k == KD - 1))
                    stat_mm[k] = incA("tensor", "mm", smms)
                ln1_scalar_ev = S.c["ev"]
                rev = rows_mathA(psBC[0:1, 0:512], psBC[0:1, 512:1024], stat_mm[KD - 1])
                wvA("tensor", "evv", rev)
                bc_mm = incA("tensor", "mm", bcs)

                if dbg and gi == 0:
                    wvA("gpsimd", "mm", bc_mm)
                    dmaA("gpsimd", "st", dbgout[0:1, 8192:8192 + 3584], rowsA[:])
                aev = None
                for k in range(KD):
                    if k == 0:
                        wvA("vector", "mm", bc_mm)
                        wvA("vector", "ld", fb_ld)
                    flA("vector", lambda e, k=k: e.tensor_tensor(ta0[:], fb[:, k], psBC[:, 512:1024], ALU.mult))
                    aev = incA("vector", "evv", lambda e, k=k: e.tensor_tensor(xn[:, k, :], ta0[:], psBC[:, 0:512], ALU.subtract))
                apply_done = aev
                bc_free = ("evv", apply_done)

                qk_ctr = [0]
                for pj in range(7):
                    kind, i = PJS[pj]
                    widx = gi * 7 + pj
                    wslot = widx % 2
                    wvA("tensor", "wld", wpj_ld[widx])
                    for sub in range(4):
                        b = acqA()
                        if pj == 0 and sub == 0:
                            wvA("tensor", "evv", apply_done)

                        def pmms(e, sub=sub, b=b, wslot=wslot):
                            for kk in range(KD):
                                mi = e.matmul(psR[:, b * 512:(b + 1) * 512], xn[:, kk, sub * 128:(sub + 1) * 128],
                                              wpj[:, wslot, kk, :], start=(kk == 0), stop=(kk == KD - 1))
                            return mi
                        pmm = incA("tensor", "mm", pmms)
                        if pend_tp:
                            emit_tp()
                        wvA("vector", "mm", pmm)
                        if pj == 0 and sub == 0 and scan_done_mm is not None:
                            wvA("vector", "mm", scan_done_mm)
                        sl = psR[:, b * 512:(b + 1) * 512]
                        if kind == "v":
                            ev = incA("vector", "evv",
                                      lambda e, sl=sl, sub=sub, i=i: e.tensor_copy(
                                          recb[:, sub, RV + i * 512:RV + (i + 1) * 512], sl))
                        else:
                            tslot = qk_ctr[0] % 2
                            qk_ctr[0] += 1
                            if tp_state["last_tmm"] is not None:
                                wvA("vector", "mm", tp_state["last_tmm"])
                            if kind == "q":
                                ev = incA("vector", "evv",
                                          lambda e, sl=sl, tslot=tslot: e.tensor_copy(qkt[:, tslot], sl))
                                pend_tp.append(((sub, RQ), i, tslot, ev))
                            else:
                                flA("vector", lambda e, sl=sl, tslot=tslot: e.tensor_copy(qkt[:, tslot], sl))
                                ev = incA("vector", "evv",
                                          lambda e, sl=sl, sub=sub, i=i: e.tensor_scalar(
                                              recb[:, sub, RKT + i * 512:RKT + (i + 1) * 512], sl,
                                              decu_sb[:, i:i + 1], None, ALU.mult))
                                pend_tp.append(((sub, RKF + i * 512), i, tslot, ev))
                        bankA[b] = ("evv", ev)
                    wpj_cons[widx] = S.c["mm"]
                    issue_wpj()
                while pend_tp:
                    emit_tp()
                if dbg and gi == 0:
                    wvA("gpsimd", "evv", apply_done)
                    dmaA("gpsimd", "st", dbgb[:, 0:8192].rearrange("p (k t) -> p k t", t=TB), xn[:])

                rec_done = {}
                for sub in range(4):
                    wrA("tensor", psX_rel)

                    def gmms(e, sub=sub):
                        for kk in range(KD):
                            mi = e.matmul(psX[:, 256:259], xn[:, kk, sub * 128:(sub + 1) * 128],
                                          wg3_sb[:, kk, :], start=(kk == 0), stop=(kk == KD - 1))
                        return mi
                    gmm = incA("tensor", "mm", gmms)
                    wvA("vector", "mm", gmm)
                    ge = incA("vector", "evv", lambda e: e.tensor_copy(gwork[:, 0:3], psX[:, 256:259]))
                    psX_rel = ("evv", ge)
                    wvA("vector", "evv", ge)
                    wvA("scalar", "evv", ge)
                    se = incA("scalar", "ev", lambda e: e.activation(gwork[:, 3:6], gwork[:, 0:3], AF.Exp))
                    wvA("vector", "ev", se)
                    flA("vector", lambda e: e.reduce_sum(gwork[:, 6:7], gwork[:, 3:6], axis=AX.X))
                    flA("vector", lambda e: e.reciprocal(gwork[:, 7:8], gwork[:, 6:7]))
                    if gate_tmm_prev is not None:
                        wvA("vector", "mm", gate_tmm_prev)
                    flA("vector", lambda e: e.tensor_scalar(gwork[:, 8:11], gwork[:, 3:6], gwork[:, 7:8], None, ALU.mult))
                    for i3 in range(2):
                        emA("vector", lambda e, i3=i3: e.tensor_scalar(gwork[:, 11 + i3:12 + i3],
                                                                       gwork[:, 8 + i3:9 + i3],
                                                                       decu_sb[:, 3 + i3:4 + i3], None, ALU.mult))
                    gdev = incA("vector", "evv",
                                lambda e: e.tensor_scalar(gwork[:, 13:14], gwork[:, 10:11],
                                                          decu_sb[:, 5:6], None, ALU.mult))
                    wvA("tensor", "evv", gdev)
                    wrA("tensor", psX_rel)
                    tmm = incA("tensor", "mm", lambda e: e.transpose(psX[0:6, 128:256], gwork[:, 8:14], eye_f[:]))
                    gate_tmm_prev = tmm
                    wvA("vector", "mm", tmm)
                    if sel_last_mm is not None:
                        wvA("vector", "mm", sel_last_mm)
                    gt = incA("vector", "evv", lambda e: e.tensor_copy(gdT_sb[0:6, :], psX[0:6, 128:256]))
                    psX_rel = ("evv", gt)
                    wvA("tensor", "evv", gt)
                    for n in range(6):
                        b = acqA()
                        smm = incA("tensor", "mm",
                                   lambda e, n=n, b=b: e.matmul(psR[:, b * 512:b * 512 + 128],
                                                                selbf[0:6, n * 128:(n + 1) * 128], gdT_sb[0:6, :],
                                                                start=True, stop=True))
                        wvA("vector", "mm", smm)
                        i3 = n % 3
                        off = RGB + (i3 * 128 if n < 3 else 384 + i3 * 128)
                        ev = incA("vector", "evv",
                                  lambda e, sub=sub, off=off, b=b: e.tensor_copy(recb[:, sub, off:off + 128],
                                                                                 psR[:, b * 512:b * 512 + 128]))
                        bankA[b] = ("evv", ev)
                        sel_last_mm = smm
                    rec_done[sub] = S.c["evv"]

                wo_evacs_started = False
                for sub in (3, 2, 1, 0):
                    roff = sub * RW
                    prep_ev = None
                    if prev_retr_mm is not None:
                        wvA("vector", "mm", prev_retr_mm)
                    for i in range(3):
                        flA("vector", lambda e, sub=sub, i=i: e.tensor_tensor(
                            cmbA[:, i], wloc[:, i], recb[:, sub, RGB + i * 128:RGB + (i + 1) * 128], ALU.mult))
                        brep = bass.AP(recb_h, roff + RGB + 384 + i * 128, [[4 * RW, 128], [0, ADT], [1, 128]])
                        prep_ev = incA("vector", "evv", lambda e, sub=sub, i=i, brep=brep: e.tensor_tensor(
                            qt[:, i], recb[:, sub, RQ:RQ + 512].rearrange("p (a b) -> p a b", b=128), brep, ALU.mult))

                    rb = acqA()
                    first_retr = True
                    retr_mm = None
                    for i in range(3):
                        wrA("tensor", psX_rel)
                        if i == 0:
                            wvA("tensor", "evv", max(rec_done[sub], supd_done, prep_ev))

                        def scmms(e, sub=sub, i=i):
                            for kd in range(ADT):
                                mi = e.matmul(psX[:, 0:128],
                                              recb[:, sub, RKF + i * 512 + kd * 128:RKF + i * 512 + (kd + 1) * 128],
                                              recb[:, sub, RQ + kd * 128:RQ + (kd + 1) * 128],
                                              start=(kd == 0), stop=(kd == ADT - 1))
                            return mi
                        smm = incA("tensor", "mm", scmms)
                        wvA("vector", "mm", smm)
                        scev = incA("vector", "evv",
                                    lambda e, i=i: e.tensor_tensor(scb[:, i], psX[:, 0:128], cmbA[:, i], ALU.mult))
                        psX_rel = ("evv", scev)
                        wvA("tensor", "evv", scev)

                        def rmms(e, sub=sub, i=i, rb=rb, fr=first_retr):
                            for m in range(ADT):
                                e.matmul(psR[:, rb * 512 + m * 128:rb * 512 + (m + 1) * 128],
                                         recb[:, sub, RV + i * 512 + m * 128:RV + i * 512 + (m + 1) * 128],
                                         scb[:, i], start=fr, stop=False)
                                fr = False
                            for m in range(ADT):
                                for kd in range(ADT):
                                    last = (i == 2 and m == ADT - 1 and kd == ADT - 1)
                                    mi = e.matmul(psR[:, rb * 512 + m * 128:rb * 512 + (m + 1) * 128],
                                                  S_sb[:, i, kd, m * 128:(m + 1) * 128],
                                                  qt[:, i, kd * 128:(kd + 1) * 128], start=False, stop=last)
                            return mi
                        retr_mm = incA("tensor", "mm", rmms)
                        first_retr = False
                    prev_retr_mm = retr_mm
                    wvA("vector", "mm", retr_mm)
                    rev2 = incA("vector", "evv",
                                lambda e, rb=rb: e.tensor_scalar(retrbf[:], psR[:, rb * 512:(rb + 1) * 512],
                                                                 osc_col[:], None, ALU.mult))
                    bankA[rb] = ("evv", rev2)

                    for i in range(3):
                        for m in range(ADT):
                            b2 = acqA()

                            def umms(e, sub=sub, i=i, m=m, b2=b2):
                                e.matmul(psR[:, b2 * 512:(b2 + 1) * 512],
                                         recb[:, sub, RKT + i * 512 + m * 128:RKT + i * 512 + (m + 1) * 128],
                                         recb[:, sub, RV + i * 512:RV + (i + 1) * 512], start=True, stop=False)
                                return e.matmul(psR[:, b2 * 512:(b2 + 1) * 512], Isc[:, i],
                                                S_sb[:, i, m, :], start=False, stop=True)
                            umm = incA("tensor", "mm", umms)
                            wvA("vector", "mm", umm)
                            uev = incA("vector", "evv",
                                       lambda e, i=i, m=m, b2=b2: e.tensor_copy(S_sb[:, i, m, :],
                                                                                psR[:, b2 * 512:(b2 + 1) * 512]))
                            bankA[b2] = ("evv", uev)
                    supd_done = S.c["evv"]

                    wvA("tensor", "evv", rev2)
                    for mg in range(4):
                        b3 = acqA()

                        def wmms(e, mg=mg, b3=b3):
                            for lm in range(4):
                                m = mg * 4 + lm
                                for kd in range(ADT):
                                    mi = e.matmul(psR[:, b3 * 512 + lm * 128:b3 * 512 + (lm + 1) * 128],
                                                  wo_sb[:, kd, m * 128:(m + 1) * 128],
                                                  retrbf[:, kd * 128:(kd + 1) * 128],
                                                  start=(kd == 0), stop=(kd == ADT - 1))
                            return mi
                        wmm = incA("tensor", "mm", wmms)
                        wvA("vector", "mm", wmm)
                        if not wo_evacs_started:
                            wvA("vector", "ev", ln1_scalar_ev)
                            wo_evacs_started = True
                        wev = incA("vector", "evv",
                                   lambda e, sub=sub, mg=mg, b3=b3: e.tensor_copy(
                                       fb[:, mg * 4:(mg + 1) * 4, sub * 128:(sub + 1) * 128],
                                       psR[:, b3 * 512:(b3 + 1) * 512].rearrange("p (a b) -> p a b", b=128)))
                        bankA[b3] = ("evv", wev)
                scan_done_mm = S.c["mm"]
                ao_done = S.c["evv"]
                if dbg and gi == 0:
                    wvA("gpsimd", "evv", ao_done)
                    dmaA("gpsimd", "st", dbgout[:, 0:8192].rearrange("p (k t) -> p k t", t=TB), fb[:])

                stat2_mm = {}
                for k in range(KD):
                    if k == 0:
                        wvA("scalar", "evv", ao_done)
                    if k >= 2:
                        wvA("scalar", "mm", stat2_mm[k - 2])
                    incA("scalar", "ev", lambda e, k=k: e.activation(st1[:, k % 2], fb[:, k], AF.Identity))
                    e2 = incA("scalar", "ev", lambda e, k=k: e.activation(st2[:, k % 2], fb[:, k], AF.Square))
                    wvA("tensor", "ev", e2)
                    if k == 0:
                        wrA("tensor", bc_free)

                    def smms2(e, k=k):
                        e.matmul(psBC[0:1, 0:512], onescol[:], st1[:, k % 2], start=(k == 0), stop=(k == KD - 1))
                        return e.matmul(psBC[0:1, 512:1024], onescol[:], st2[:, k % 2], start=(k == 0), stop=(k == KD - 1))
                    stat2_mm[k] = incA("tensor", "mm", smms2)
                rev = rows_mathA(psBC[0:1, 0:512], psBC[0:1, 512:1024], stat2_mm[KD - 1])
                wvA("tensor", "evv", rev)
                bc2_mm = incA("tensor", "mm", bcs)

                sb1 = acqA()
                bankA[sb1] = ("evv", 10 ** 9)
                sb2 = acqA()
                bankA[sb2] = ("evv", 10 ** 9)
                yst = {}
                yev_l = {}
                stat3_mm = {}
                for k in range(KD):
                    if k >= 2:
                        wvA("scalar", "evv", yev_l[k - 2])
                    xld = dmaA("scalar", "xld", xs[:, k % 2], fm(xT)[:, k, g * TB:(g + 1) * TB])
                    wvA("vector", "xld", xld)
                    if k == 0:
                        wvA("vector", "mm", bc2_mm)
                    flA("vector", lambda e, k=k: e.tensor_tensor(ta0[:], fb[:, k], psBC[:, 512:1024], ALU.mult))
                    flA("vector", lambda e: e.tensor_tensor(ta0[:], ta0[:], psBC[:, 0:512], ALU.subtract))
                    yev = incA("vector", "evv", lambda e, k=k: e.tensor_tensor(fb[:, k], ta0[:], xs[:, k % 2], ALU.add))
                    yev_l[k] = yev
                    wvA("gpsimd", "evv", yev)
                    yst[k] = dmaA("gpsimd", "st", fm(y_s)[:, k, g * TB:(g + 1) * TB], fb[:, k, :])
                    wvA("scalar", "evv", yev)
                    if k >= 2:
                        wvA("scalar", "mm", stat3_mm[k - 2])
                    incA("scalar", "ev", lambda e, k=k: e.activation(st1[:, k % 2], fb[:, k], AF.Identity))
                    e2 = incA("scalar", "ev", lambda e, k=k: e.activation(st2[:, k % 2], fb[:, k], AF.Square))
                    wvA("tensor", "ev", e2)

                    def smms3(e, k=k, sb1=sb1, sb2=sb2):
                        e.matmul(psR[0:1, sb1 * 512:sb1 * 512 + 512], onescol[:], st1[:, k % 2],
                                 start=(k == 0), stop=(k == KD - 1))
                        return e.matmul(psR[0:1, sb2 * 512:sb2 * 512 + 512], onescol[:], st2[:, k % 2],
                                        start=(k == 0), stop=(k == KD - 1))
                    stat3_mm[k] = incA("tensor", "mm", smms3)
                rev = rows_mathA(psR[0:1, sb1 * 512:sb1 * 512 + 512], psR[0:1, sb2 * 512:sb2 * 512 + 512], stat3_mm[KD - 1])
                bankA[sb1] = ("evv", rev)
                bankA[sb2] = ("evv", rev)
                wvA("tensor", "evv", rev)
                bc3_mm = incA("tensor", "mm", bcs)

                hlst = {}
                hlev = None
                for k in range(KD):
                    if k == 0:
                        wvA("vector", "mm", bc3_mm)
                    if k >= 2:
                        wvA("vector", "st", hlst[k - 2])
                    elif hl_carry[k] is not None:
                        wrA("vector", hl_carry[k])
                    flA("vector", lambda e, k=k: e.tensor_tensor(ta0[:], fb[:, k], psBC[:, 512:1024], ALU.mult))
                    flA("vector", lambda e: e.tensor_tensor(ta0[:], ta0[:], psBC[:, 0:512], ALU.subtract))
                    flA("vector", lambda e, k=k: e.tensor_copy(hl[:, k % 2, 0], ta0[:]))
                    hlev = incA("vector", "evv",
                                lambda e, k=k: e.tensor_tensor(hl[:, k % 2, 1], ta0[:], hl[:, k % 2, 0], ALU.subtract))
                    wvA("gpsimd", "evv", hlev)
                    dmaA("gpsimd", "st", fm(r0hi_s)[:, k, g * TB:(g + 1) * TB], hl[:, k % 2, 0])
                    hlst[k] = dmaA("gpsimd", "st", fm(r0lo_s)[:, k, g * TB:(g + 1) * TB], hl[:, k % 2, 1])
                fb_free = ("evv", hlev)
                fb_free_st = ("st", yst[KD - 1])
                bc_free = ("evv", hlev)
                hl_carry = {0: ("st", hlst[KD - 2]), 1: ("st", hlst[KD - 1])}

            with nc.Block() as blk:
                @blk.tensor
                def _(te):
                    for fn in QA["tensor"]:
                        fn(te)

                @blk.scalar
                def _(sc):
                    for fn in QA["scalar"]:
                        fn(sc)

                @blk.vector
                def _(v):
                    for fn in QA["vector"]:
                        fn(v)

                @blk.sync
                def _(e):
                    for fn in QA["sync"]:
                        fn(e)

                @blk.gpsimd
                def _(gp):
                    for fn in QA["gpsimd"]:
                        fn(gp)


        TT = 1024
        NT = T // TT
        with contextlib.ExitStack() as pb:
            sbb = lambda name, shape, dt: pb.enter_context(nc.sbuf_tensor(name, shape, dt))
            psA = pb.enter_context(nc.psum_tensor("psA", [128, 2048], F32))
            psW = pb.enter_context(nc.psum_tensor("psW", [128, 1024], F32))
            psG = pb.enter_context(nc.psum_tensor("psG", [128, 512], F32))
            psS = pb.enter_context(nc.psum_tensor("psS", [128, 512], F32))

            xtu = sbb("xtu", [128, KD, TT], BF16)
            ph = sbb("ph", [128, KD, TT], BF16)
            acc = sbb("acc", [128, HT, TT], F32)
            upb = sbb("upb", [128, 3, KD, 128], BF16)
            swb = sbb("swb", [128, 4, ET, 128], BF16)
            dnb = sbb("dnb", [128, 4, ET, 128], BF16)
            wbn = sbb("wbn", [128, TT], BF16)
            wTx = sbb("wTx", [8, TT], BF16)
            xgl = sbb("xgl", [128, 2, KD, 128], BF16)
            rrow = sbb("rrow", [128, 2, TT], F32)
            rb16 = sbb("rb16", [128, 2, 2, TT], BF16)
            stg = sbb("stg", [128, 2, TT], F32)
            zsqb = sbb("zsqb", [128, TT], BF16)
            gg = sbb("gg", [128, 56], F32)

            Q, em, wv, wr, dmaQ, incQ, flQ = mk_rec()

            TOT_TILES = 2 * NT
            TOT_UP = TOT_TILES * NU * 8
            TOT_SW = TOT_TILES * NU * 8
            TOT_DN = TOT_TILES * NU * HT
            up_ld = {}; up_cons = {}
            sw_ld = {}; sw_cons = {}
            dn_ld = {}; dn_cons = {}

            def issue_up(g):
                if g >= TOT_UP:
                    return
                n, c = divmod(g % (NU * 8), 8)
                r = up_cons.get(g - 3)
                if r is not None:
                    wv("sync", "mm", r)
                up_ld[g] = dmaQ("sync", "wld", upb[:, g % 3], wup_s[n, c])

            def issue_sw(g):
                if g >= TOT_SW:
                    return
                n, c = divmod(g % (NU * 8), 8)
                r = sw_cons.get(g - 4)
                if r is not None:
                    wv("sync", "mm", r)
                sw_ld[g] = dmaQ("sync", "wld", swb[:, g % 4], wsw_s[n, c])

            def issue_dn(g):
                if g >= TOT_DN:
                    return
                n, m = divmod(g % (NU * HT), HT)
                r = dn_cons.get(g - 4)
                if r is not None:
                    wv("gpsimd", "mm", r)
                dn_ld[g] = dmaQ("gpsimd", "dld", dnb[:, g % 4], wdn_s[m, :, n, :, :])

            bankrel = [None, None, None, None]
            bank_i = [0]

            def acquire_bank():
                b = bank_i[0] % 4
                bank_i[0] += 1
                wr("tensor", bankrel[b])
                return b

            carry = {"xtu": None, "ph": None, "psG": None, "psW": None, "wTx": None,
                     "stg": None, "rrow": {0: None, 1: None}, "rb16": {0: None, 1: None},
                     "xgl": {0: None, 1: None}}
            zstore = {}
            tile_iter = 0
            for j in range(2):
                hi_src = r0hi_s if j == 0 else xt1hi_s
                lo_src = r0lo_s if j == 0 else xt1lo_s
                for t in range(NT):
                    base = t * TT
                    if tile_iter == 0:
                        st_all = S.c["st"]
                        wv("sync", "st", st_all)
                        wv("scalar", "st", st_all)
                        wv("gpsimd", "st", st_all)
                    if j == 1:
                        wv("sync", "st", zstore[t])
                        wv("scalar", "st", zstore[t])
                    wr("sync", carry["xtu"])
                    xtu_ldv = dmaQ("sync", "ld", xtu[:], fm(hi_src)[:, :, base:base + TT])
                    if tile_iter == 0:
                        for g0 in range(3):
                            issue_up(g0)
                        for g0 in range(4):
                            issue_sw(g0)
                        for g0 in range(4):
                            issue_dn(g0)

                    wr("vector", carry["wTx"])
                    flQ("vector", lambda e: e.memset(wTx[:], 1.0))
                    G8 = gg[:, 0:8]
                    g_mm = {}; gate_rel = {}; gtr = {}; wTxrel = {}
                    for c in range(8):
                        if c >= 2:
                            wv("scalar", "mm", g_mm[c - 2])
                        else:
                            wr("scalar", carry["xgl"][c])
                        xldc = dmaQ("scalar", "xld", xgl[:, c % 2],
                                    fm(lo_src)[:, :, base + c * 128:base + (c + 1) * 128])
                        wv("tensor", "xld", xldc)
                        if c == 0:
                            wv("tensor", "ld", xtu_ldv)
                            wr("tensor", carry["psG"])
                        else:
                            wv("tensor", "evv", gate_rel[c - 1])

                        def g_mms(e, c=c, j=j):
                            for k in range(KD):
                                e.matmul(psG[:, 0:8], xtu[:, k, c * 128:(c + 1) * 128],
                                         wghi[:, j, k, :], start=(k == 0), stop=False)
                            for k in range(KD):
                                e.matmul(psG[:, 0:8], xtu[:, k, c * 128:(c + 1) * 128],
                                         wglo[:, j, k, :], start=False, stop=False)
                            for k in range(KD):
                                mi = e.matmul(psG[:, 0:8], xgl[:, c % 2, k, :],
                                              wghi[:, j, k, :], start=False, stop=(k == KD - 1))
                            return mi
                        g_mm[c] = incQ("tensor", "mm", g_mms)

                        wv("vector", "mm", g_mm[c])
                        if c > 0:
                            wv("vector", "mm", gtr[c - 1])
                        gate_rel[c] = incQ("vector", "evv",
                                           lambda e, j=j: e.tensor_tensor(G8, psG[:, 0:8], bgb[:, j * 8:(j + 1) * 8], ALU.add))
                        wv("vector", "evv", gate_rel[c])
                        flQ("vector", lambda e: e.tensor_single_scalar(gg[:, 8:16], G8, G8[:, 0:1], ALU.is_lt))
                        cur = 8
                        for mI in range(1, 8):
                            flQ("vector", lambda e, mI=mI: e.tensor_single_scalar(gg[:, 24:32], G8, G8[:, mI:mI + 1], ALU.is_lt))
                            nxt = 24 - cur
                            flQ("vector", lambda e, cur=cur, nxt=nxt: e.tensor_add(gg[:, nxt:nxt + 8], gg[:, cur:cur + 8], gg[:, 24:32]))
                            cur = nxt
                        flQ("vector", lambda e, cur=cur: e.tensor_single_scalar(gg[:, 32:40], gg[:, cur:cur + 8], 5.5, ALU.is_lt))
                        ggv = incQ("vector", "evv", lambda e: e.reduce_max(gg[:, 48:49], G8, axis=AX.X, negate=True))
                        wv("scalar", "evv", ggv)
                        gexp = incQ("scalar", "ev", lambda e: e.activation(gg[:, 40:48], gg[:, 0:8], AF.Exp, bias=gg[:, 48:49], scale=1.0))
                        wv("vector", "ev", gexp)
                        flQ("vector", lambda e: e.tensor_mul(gg[:, 24:32], gg[:, 40:48], gg[:, 32:40]))
                        flQ("vector", lambda e: e.reduce_sum(gg[:, 49:50], gg[:, 24:32], axis=AX.X))
                        flQ("vector", lambda e: e.reciprocal(gg[:, 50:51], gg[:, 49:50]))
                        gn = incQ("vector", "evv", lambda e: e.tensor_scalar(gg[:, 0:8], gg[:, 24:32], gg[:, 50:51], None, ALU.mult))
                        wv("tensor", "evv", gn)
                        if c > 0:
                            wv("tensor", "evv", wTxrel[c - 1])
                        gtr[c] = incQ("tensor", "mm", lambda e: e.transpose(psG[0:8, 128:256], gg[:, 0:8], eye_f[:]))
                        wv("vector", "mm", gtr[c])
                        wTxrel[c] = incQ("vector", "evv",
                                         lambda e, c=c: e.tensor_copy(wTx[0:6, c * 128:(c + 1) * 128], psG[0:6, 128:256]))
                    wtx_done = wTxrel[7]
                    carry["xgl"] = {0: ("mm", g_mm[6]), 1: ("mm", g_mm[7])}

                    selmm = {}; wbnrel = {}
                    updone = {}; swdone = {}; dndone = {}
                    for n in range(NU):
                        if n < 6:
                            wv("tensor", "evv", wtx_done if n == 0 else wbnrel[n - 1])
                            if n == 0:
                                wr("tensor", carry["psW"])

                            def sel_mms(e, n=n):
                                e.matmul(psW[:, 0:512], selbf[0:8, n * 128:(n + 1) * 128], wTx[0:8, 0:512], start=True, stop=True)
                                return e.matmul(psW[:, 512:1024], selbf[0:8, n * 128:(n + 1) * 128], wTx[0:8, 512:1024], start=True, stop=True)
                            selmm[n] = incQ("tensor", "mm", sel_mms)
                            wv("vector", "mm", selmm[n])
                            wbnrel[n] = incQ("vector", "evv", lambda e: e.tensor_copy(wbn[:], psW[:]))

                        last_up_ev = {}
                        for c in range(8):
                            gidx = tile_iter * 64 + n * 8 + c
                            wv("tensor", "wld", up_ld[gidx])
                            for sub in range(2):
                                b = acquire_bank()
                                if tile_iter == 0 and n == 0 and c == 0 and sub == 0:
                                    wv("tensor", "ld", xtu_ldv)

                                def up_mms(e, c=c, sub=sub, b=b, slot=gidx % 3):
                                    for k in range(KD):
                                        mi = e.matmul(psA[:, b * 512:(b + 1) * 512], upb[:, slot, k, :],
                                                      xtu[:, k, sub * 512:(sub + 1) * 512],
                                                      start=(k == 0), stop=(k == KD - 1))
                                    return mi
                                gm = incQ("tensor", "mm", up_mms)
                                wv("scalar", "mm", gm)
                                if c == 0 and sub == 0:
                                    if n == 0:
                                        wr("scalar", carry["ph"])
                                    else:
                                        wv("scalar", "mm", swdone[n - 1])
                                bcol = (bcols[:, n * 8 + c:n * 8 + c + 1] if n < 6
                                        else bcols[:, 96 + (n - 6) * 8 + c:96 + (n - 6) * 8 + c + 1])
                                eva = incQ("scalar", "ev",
                                           lambda e, c=c, sub=sub, b=b, bcol=bcol: e.activation(
                                               ph[:, c, sub * 512:(sub + 1) * 512], psA[:, b * 512:(b + 1) * 512],
                                               AF.Identity, bias=bcol, scale=1.0))
                                bankrel[b] = ("ev", eva)
                                last_up_ev[sub] = eva
                            up_cons[gidx] = S.c["mm"]
                            issue_up(gidx + 3)
                        updone[n] = S.c["mm"]

                        h2done = {}
                        for c in range(8):
                            gidx = tile_iter * 64 + n * 8 + c
                            wv("tensor", "wld", sw_ld[gidx])
                            for sub in range(2):
                                b = acquire_bank()
                                if c == 0:
                                    wv("tensor", "ev", last_up_ev[sub])

                                def sw_mms(e, c=c, sub=sub, b=b, slot=gidx % 4):
                                    for k in range(ET):
                                        mi = e.matmul(psA[:, b * 512:(b + 1) * 512], swb[:, slot, k, :],
                                                      ph[:, k, sub * 512:(sub + 1) * 512],
                                                      start=(k == 0), stop=(k == ET - 1))
                                    return mi
                                gm = incQ("tensor", "mm", sw_mms)
                                wv("scalar", "mm", gm)
                                if c == 0 and sub == 0 and n > 0:
                                    wv("scalar", "mm", dndone[n - 1])
                                bcol = (bcols[:, 48 + n * 8 + c:48 + n * 8 + c + 1] if n < 6
                                        else bcols[:, 112 + c:112 + c + 1])
                                eva = incQ("scalar", "ev",
                                           lambda e, c=c, sub=sub, b=b, bcol=bcol: e.activation(
                                               ph[:, ET + c, sub * 512:(sub + 1) * 512], psA[:, b * 512:(b + 1) * 512],
                                               AF.Silu, bias=bcol, scale=1.0))
                                bankrel[b] = ("ev", eva)
                                wv("vector", "ev", eva)
                                h2s = ph[:, ET + c, sub * 512:(sub + 1) * 512]
                                if n < 6:
                                    flQ("vector", lambda e, h2s=h2s, c=c, sub=sub: e.tensor_tensor(
                                        h2s, h2s, ph[:, c, sub * 512:(sub + 1) * 512], ALU.mult))
                                    h2done[sub] = incQ("vector", "evv",
                                                       lambda e, h2s=h2s, sub=sub: e.tensor_tensor(
                                                           h2s, h2s, wbn[:, sub * 512:(sub + 1) * 512], ALU.mult))
                                else:
                                    h2done[sub] = incQ("vector", "evv",
                                                       lambda e, h2s=h2s, c=c, sub=sub: e.tensor_tensor(
                                                           h2s, h2s, ph[:, c, sub * 512:(sub + 1) * 512], ALU.mult))
                            sw_cons[gidx] = S.c["mm"]
                            issue_sw(gidx + 4)
                        swdone[n] = S.c["mm"]

                        for m in range(HT):
                            gidx = tile_iter * 128 + n * 16 + m
                            wv("tensor", "dld", dn_ld[gidx])
                            for sub in range(2):
                                b = acquire_bank()
                                if m == 0:
                                    wv("tensor", "evv", h2done[sub])

                                def dn_mms(e, n=n, m=m, sub=sub, b=b, slot=gidx % 4):
                                    for k in range(ET):
                                        mi = e.matmul(psA[:, b * 512:(b + 1) * 512], dnb[:, slot, k, :],
                                                      ph[:, ET + k, sub * 512:(sub + 1) * 512],
                                                      start=(k == 0), stop=(k == ET - 1 and n != 0))
                                    if n == 0:
                                        mi = e.matmul(psA[:, b * 512:(b + 1) * 512],
                                                      bext_bf[0:7, m * 128:(m + 1) * 128],
                                                      wTx[0:7, sub * 512:(sub + 1) * 512], start=False, stop=True)
                                    return mi
                                gm = incQ("tensor", "mm", dn_mms)
                                wv("vector", "mm", gm)
                                accsl = acc[:, m, sub * 512:(sub + 1) * 512]
                                if n == 0:
                                    av = incQ("vector", "evv",
                                              lambda e, accsl=accsl, b=b: e.tensor_copy(accsl, psA[:, b * 512:(b + 1) * 512]))
                                else:
                                    av = incQ("vector", "evv",
                                              lambda e, accsl=accsl, b=b: e.tensor_tensor(accsl, accsl, psA[:, b * 512:(b + 1) * 512], ALU.add))
                                bankrel[b] = ("evv", av)
                            dn_cons[gidx] = S.c["mm"]
                            issue_dn(gidx + 4)
                        dndone[n] = S.c["mm"]

                    if j == 0:
                        zst = {}
                        for m in range(HT):
                            if m >= 2:
                                wv("scalar", "st", zst[m - 2])
                            else:
                                wr("scalar", carry["rb16"][m % 2])
                            dmaQ("scalar", "xld", rb16[:, m % 2, 0], fm(r0hi_s)[:, m, base:base + TT])
                            rld = dmaQ("scalar", "xld", rb16[:, m % 2, 1], fm(r0lo_s)[:, m, base:base + TT])
                            wv("vector", "xld", rld)
                            flQ("vector", lambda e, m=m: e.tensor_tensor(acc[:, m, :], acc[:, m, :], rb16[:, m % 2, 0], ALU.add))
                            flQ("vector", lambda e, m=m: e.tensor_tensor(acc[:, m, :], acc[:, m, :], rb16[:, m % 2, 1], ALU.add))
                            flQ("vector", lambda e, m=m: e.tensor_copy(rb16[:, m % 2, 0], acc[:, m, :]))
                            zev = incQ("vector", "evv",
                                       lambda e, m=m: e.tensor_tensor(rb16[:, m % 2, 1], acc[:, m, :], rb16[:, m % 2, 0], ALU.subtract))
                            wv("gpsimd", "evv", zev)
                            dmaQ("gpsimd", "st", fm(xt1hi_s)[:, m, base:base + TT], rb16[:, m % 2, 0])
                            zst[m] = dmaQ("gpsimd", "st", fm(xt1lo_s)[:, m, base:base + TT], rb16[:, m % 2, 1])
                        zstore[t] = S.c["st"]
                        carry["xtu"] = ("mm", updone[NU - 1])
                        carry["ph"] = None
                        carry["rb16"] = {0: ("st", zst[14]), 1: ("st", zst[15])}
                        carry["psW"] = ("evv", wbnrel[5])
                        carry["psG"] = ("evv", wtx_done)
                        carry["wTx"] = ("mm", selmm[5])
                    else:
                        zrel = {}; smm = {}
                        for m in range(HT):
                            if m >= 2:
                                wv("scalar", "evv", zrel[m - 2])
                            else:
                                wr("scalar", carry["rb16"][m % 2])
                            dmaQ("scalar", "xld", rb16[:, m % 2, 0], fm(r0hi_s)[:, m, base:base + TT])
                            rld = dmaQ("scalar", "xld", rb16[:, m % 2, 1], fm(r0lo_s)[:, m, base:base + TT])
                            wv("vector", "xld", rld)
                            flQ("vector", lambda e, m=m: e.tensor_tensor(acc[:, m, :], acc[:, m, :], rb16[:, m % 2, 0], ALU.add))
                            flQ("vector", lambda e, m=m: e.tensor_tensor(acc[:, m, :], acc[:, m, :], rb16[:, m % 2, 1], ALU.add))
                            flQ("vector", lambda e, m=m: e.tensor_copy(xtu[:, m, :], acc[:, m, :]))
                            zev = incQ("vector", "evv",
                                       lambda e, m=m: e.tensor_tensor(ph[:, m, :], acc[:, m, :], xtu[:, m, :], ALU.subtract))
                            zrel[m] = zev
                            wv("scalar", "evv", zev)
                            if m >= 1:
                                wv("scalar", "mm", smm[m - 1])
                            sqe = incQ("scalar", "ev", lambda e, m=m: e.activation(zsqb[:], xtu[:, m, :], AF.Square))
                            wv("tensor", "evv", zev)
                            wv("tensor", "ev", sqe)
                            if m == 0:
                                wv("tensor", "evv", wbnrel[5])

                            def stat_mms(e, m=m):
                                for sub in range(2):
                                    tgt = psG if sub == 0 else psS
                                    e.matmul(tgt[0:1, 0:512], onescol[:], xtu[:, m, sub * 512:(sub + 1) * 512],
                                             start=(m == 0), stop=False)
                                    e.matmul(tgt[0:1, 0:512], onescol[:], ph[:, m, sub * 512:(sub + 1) * 512],
                                             start=False, stop=(m == HT - 1))
                                e.matmul(psW[0:1, 0:512], onescol[:], zsqb[:, 0:512],
                                         start=(m == 0), stop=(m == HT - 1))
                                return e.matmul(psW[0:1, 512:1024], onescol[:], zsqb[:, 512:1024],
                                                start=(m == 0), stop=(m == HT - 1))
                            smm[m] = incQ("tensor", "mm", stat_mms)

                        Asl = [rrow[0:1, i // 2, (i % 2) * 512:(i % 2) * 512 + 512] for i in range(4)]
                        Bsl = [stg[0:1, i // 2, (i % 2) * 512:(i % 2) * 512 + 512] for i in range(4)]
                        wv("vector", "mm", smm[HT - 1])
                        wr("vector", carry["stg"])
                        rows_done = None
                        for sub in range(2):
                            s1ap = (psG if sub == 0 else psS)[0:1, 0:512]
                            s2ap = psW[0:1, sub * 512:(sub + 1) * 512]
                            MR = Bsl[0] if sub == 0 else Bsl[2]
                            R = Bsl[1] if sub == 0 else Bsl[3]
                            s0, s1_, s2_, s3_, s4_ = Asl[0], Asl[1], Asl[2], Asl[3], Bsl[3]
                            flQ("vector", lambda e, s0=s0, s1ap=s1ap: e.tensor_scalar(s0, s1ap, 1.0 / D, None, ALU.mult))
                            flQ("vector", lambda e, s1_=s1_, s2ap=s2ap: e.tensor_scalar(s1_, s2ap, 1.0 / D, None, ALU.mult))
                            flQ("vector", lambda e, s2_=s2_, s0=s0: e.tensor_mul(s2_, s0, s0))
                            flQ("vector", lambda e, s4_=s4_, s1_=s1_, s2_=s2_: e.tensor_sub(s4_, s1_, s2_))
                            ve = incQ("vector", "evv", lambda e, s3_=s3_, s4_=s4_: e.tensor_scalar_add(s3_, s4_, EPS))
                            wv("scalar", "evv", ve)
                            flQ("scalar", lambda e, s2_=s2_, s3_=s3_: e.activation(s2_, s3_, AF.Ln))
                            se = incQ("scalar", "ev", lambda e, s4_=s4_, s2_=s2_: e.activation(s4_, s2_, AF.Exp, bias=0.0, scale=-0.5))
                            wv("vector", "ev", se)
                            flQ("vector", lambda e, s1_=s1_, s4_=s4_: e.tensor_mul(s1_, s4_, s4_))
                            flQ("vector", lambda e, s2_=s2_, s1_=s1_, s3_=s3_: e.tensor_mul(s2_, s1_, s3_))
                            flQ("vector", lambda e, s1_=s1_, s2_=s2_: e.tensor_scalar(s1_, s2_, -0.5, 1.5, ALU.mult, ALU.add))
                            flQ("vector", lambda e, R=R, s4_=s4_, s1_=s1_: e.tensor_mul(R, s4_, s1_))
                            rows_done = incQ("vector", "evv", lambda e, MR=MR, s0=s0, R=R: e.tensor_mul(MR, s0, R))
                        wv("tensor", "evv", rows_done)

                        def bc_mms(e):
                            e.matmul(psW[:, 0:512], ones1f[:], Bsl[0], start=True, stop=True)
                            e.matmul(psW[:, 512:1024], ones1f[:], Bsl[2], start=True, stop=True)
                            e.matmul(psG[:, 0:512], ones1f[:], Bsl[1], start=True, stop=True)
                            return e.matmul(psS[:, 0:512], ones1f[:], Bsl[3], start=True, stop=True)
                        bc_mm = incQ("tensor", "mm", bc_mms)

                        yrel = {}; ost = {}
                        for m in range(HT):
                            if m < 2:
                                wv("scalar", "evv", rows_done)
                                wr("scalar", carry["rrow"][m % 2])
                            else:
                                wv("scalar", "evv", yrel[m - 2])
                            yld = dmaQ("scalar", "xld", rrow[:, m % 2], fm(y_s)[:, m, base:base + TT])
                            wv("vector", "xld", yld)
                            if m == 0:
                                wv("vector", "mm", bc_mm)
                            if m >= 2:
                                wv("vector", "st", ost[m - 2])
                            orv = None
                            for sub in range(2):
                                Rps = psG if sub == 0 else psS
                                ss = stg[:, m % 2, sub * 512:(sub + 1) * 512]
                                flQ("vector", lambda e, ss=ss, m=m, sub=sub, Rps=Rps: e.tensor_tensor(
                                    ss, acc[:, m, sub * 512:(sub + 1) * 512], Rps[:, 0:512], ALU.mult))
                                flQ("vector", lambda e, ss=ss, sub=sub: e.tensor_tensor(
                                    ss, ss, psW[:, sub * 512:(sub + 1) * 512], ALU.subtract))
                                opl = lambda e, ss=ss, m=m, sub=sub: e.tensor_tensor(
                                    ss, ss, rrow[:, m % 2, sub * 512:(sub + 1) * 512], ALU.add)
                                if sub == 1:
                                    orv = incQ("vector", "evv", opl)
                                else:
                                    flQ("vector", opl)
                            yrel[m] = orv
                            wv("gpsimd", "evv", orv)
                            ost[m] = dmaQ("gpsimd", "st", fm(x2T)[:, m, base:base + TT], stg[:, m % 2])
                        carry["xtu"] = ("mm", smm[HT - 1])
                        carry["ph"] = ("mm", smm[HT - 1])
                        carry["psG"] = ("evv", yrel[15])
                        carry["psW"] = ("evv", yrel[15])
                        carry["rrow"] = {0: ("evv", yrel[14]), 1: ("evv", yrel[15])}
                        carry["rb16"] = {0: ("evv", zrel[14]), 1: ("evv", zrel[15])}
                        carry["stg"] = ("st", ost[15])
                        carry["wTx"] = ("mm", selmm[5])
                    tile_iter += 1

            wv("gpsimd", "st", S.c["st"])

            with nc.Block() as blk:
                @blk.tensor
                def _(te):
                    for fn in Q["tensor"]:
                        fn(te)

                @blk.scalar
                def _(sc):
                    for fn in Q["scalar"]:
                        fn(sc)

                @blk.vector
                def _(v):
                    for fn in Q["vector"]:
                        fn(v)

                @blk.sync
                def _(e):
                    for fn in Q["sync"]:
                        fn(e)

                @blk.gpsimd
                def _(gp):
                    for fn in Q["gpsimd"]:
                        fn(gp)
    return nc


def _consts():
    u = np.arange(128)
    ubits = ((u[:, None] >> np.arange(7)[None, :]) & 1).astype(np.float32)
    c1 = 127 - u
    c1bits = ((c1[:, None] >> np.arange(7)[None, :]) & 1).astype(np.float32)
    s = np.arange(128)[:, None]
    t = np.arange(128)[None, :]
    expn = np.maximum(s - t - 1, 0)
    expnbits = np.concatenate([((expn >> b) & 1).astype(np.float32) for b in range(7)], axis=1).astype(BF)
    causT = (s > t).astype(np.float32).astype(BF)
    sel = np.zeros((8, 768), np.float32)
    for n in range(6):
        sel[n, n * 128 : (n + 1) * 128] = 1.0
    return dict(
        eye_bfd=np.eye(128, dtype=np.float32).astype(BF),
        eye_fd=np.eye(128, dtype=np.float32),
        ones1fd=np.ones((1, 128), np.float32),
        onescold=np.ones((128, 1), np.float32).astype(BF),
        selbfd=sel.astype(BF),
        ubitsd=ubits, c1bitsd=c1bits, expnbitsd=expnbits, causTd=causT,
    )


def _prep_inputs(inputs, T):
    f = lambda k: np.ascontiguousarray(np.asarray(inputs[k], np.float32))
    wg3 = f("L0_attn_Wg")
    wg3h = np.ascontiguousarray(wg3.reshape(16, 128, 3).transpose(1, 0, 2).reshape(128, 48))
    fg = f("L0_ffn_Wg")
    fgh = np.ascontiguousarray(fg.reshape(2, 16, 128, 8).transpose(0, 2, 1, 3).reshape(2, 128, 128))
    bup = f("L0_ffn_bup").reshape(6, 8, 128).transpose(2, 0, 1).reshape(128, 48)
    bsw = f("L0_ffn_bsw").reshape(6, 8, 128).transpose(2, 0, 1).reshape(128, 48)
    bsu = f("L0_ffn_bsu").reshape(2, 8, 128).transpose(2, 0, 1).reshape(128, 16)
    bsa = f("L0_ffn_bsa").reshape(8, 128).transpose(1, 0).reshape(128, 8)
    biascols = np.ascontiguousarray(np.concatenate([bup, bsw, bsu, bsa], axis=1))
    bext = np.ascontiguousarray(np.concatenate([f("L0_ffn_bdn"), f("L0_ffn_bsd").reshape(1, -1)], 0))
    common = dict(
        wq=f("L0_attn_Wq"), wk=f("L0_attn_Wk"), wv=f("L0_attn_Wv"), wo=f("L0_attn_Wo"),
        wg3h=wg3h, dlog=f("L0_attn_decay_logits").reshape(1, 3),
        oscale=np.asarray(inputs["L0_attn_out_scale"], np.float32).reshape(1, 1),
        fgh=fgh, bgh=f("L0_ffn_bg").reshape(1, 16),
        fwup=f("L0_ffn_Wup"), fwsw=f("L0_ffn_Wsw"), fwdn=f("L0_ffn_Wdn"),
        fwsu=f("L0_ffn_Wsu"), fwsa=f("L0_ffn_Wsa"), fwsd=f("L0_ffn_Wsd"),
        biascols=biascols, bexth=bext,
    )
    common.update(_consts())
    x = np.asarray(inputs["L0_attn_x"], np.float32)
    maps = []
    for b in range(x.shape[0]):
        m = dict(common)
        m["xT"] = np.ascontiguousarray(x[b, :T].T)
        maps.append(m)
    return maps


def kernel(**inputs):
    T = 4096
    nc = build_nc(T)
    in_maps = _prep_inputs(inputs, T)
    res = run_bass_kernel_spmd(nc, in_maps, core_ids=list(range(8)))
    out = np.stack([np.ascontiguousarray(np.asarray(r["x2T"]).T) for r in res.results], 0)
    return out.astype(np.float32)


# version 10
# speedup vs baseline: 1.4423x; 1.0164x over previous
import contextlib
import numpy as np
import ml_dtypes
import concourse.bass as bass
import concourse.mybir as mybir
from concourse.bass_utils import run_bass_kernel_spmd

F32 = mybir.dt.float32
BF16 = mybir.dt.bfloat16
AF = mybir.ActivationFunctionType
ALU = mybir.AluOpType
AX = mybir.AxisListType
BF = ml_dtypes.bfloat16

D = 2048
AD = 512
KD = 16
ADT = 4
C = 128
E = 1024
ET = 8
NU = 8
HT = 16
TB = 512
EPS = 1e-5


class Sems:
    def __init__(self, nc, stack, names):
        self.h = {}
        self.c = {}
        for n in names:
            self.h[n] = stack.enter_context(nc.semaphore(n))
            self.c[n] = 0

    def dma(self, eng, dst, src, name):
        eng.dma_start(dst, src).then_inc(self.h[name], 16)
        self.c[name] += 16
        return self.c[name]

    def inc(self, inst, name):
        inst.then_inc(self.h[name], 1)
        self.c[name] += 1
        return self.c[name]

    def flush(self, eng, inst, name="fl"):
        inst.then_inc(self.h[name], 1)
        self.c[name] += 1
        eng.wait_ge(self.h[name], self.c[name])


def rep_ap(t, offset_elems, pstride, reps, inner):
    h = t.tensor if hasattr(t, "tensor") else t
    return bass.AP(h, offset_elems, [[pstride, 128], [0, reps], [1, inner]])


ABLOCKS = None

def build_nc(T, dbg=False, extk=()):
    def KINDF(name):
        if dbg or name in extk:
            return dict(kind="ExternalOutput")
        return {}
    NCH = T // C
    NB = T // TB
    nc = bass.Bass("TRN2", target_bir_lowering=False)

    xT = nc.dram_tensor("xT", [D, T], F32, kind="ExternalInput")
    wq = nc.dram_tensor("wq", [D, AD], F32, kind="ExternalInput")
    wk = nc.dram_tensor("wk", [3, D, AD], F32, kind="ExternalInput")
    wv = nc.dram_tensor("wv", [3, D, AD], F32, kind="ExternalInput")
    wo = nc.dram_tensor("wo", [AD, D], F32, kind="ExternalInput")
    wg3h = nc.dram_tensor("wg3h", [128, 48], F32, kind="ExternalInput")
    dlog = nc.dram_tensor("dlog", [1, 3], F32, kind="ExternalInput")
    oscale = nc.dram_tensor("oscale", [1, 1], F32, kind="ExternalInput")
    fgh = nc.dram_tensor("fgh", [2, 128, 128], F32, kind="ExternalInput")
    bgh = nc.dram_tensor("bgh", [1, 16], F32, kind="ExternalInput")
    fwup = nc.dram_tensor("fwup", [6, D, E], F32, kind="ExternalInput")
    fwsw = nc.dram_tensor("fwsw", [6, E, E], F32, kind="ExternalInput")
    fwdn = nc.dram_tensor("fwdn", [6, E, D], F32, kind="ExternalInput")
    fwsu = nc.dram_tensor("fwsu", [D, D], F32, kind="ExternalInput")
    fwsa = nc.dram_tensor("fwsa", [E, E], F32, kind="ExternalInput")
    fwsd = nc.dram_tensor("fwsd", [D, D], F32, kind="ExternalInput")
    biascols = nc.dram_tensor("biascols", [128, 120], F32, kind="ExternalInput")
    bexth = nc.dram_tensor("bexth", [7, D], F32, kind="ExternalInput")
    eye_bf_d = nc.dram_tensor("eye_bfd", [128, 128], BF16, kind="ExternalInput")
    eye_f_d = nc.dram_tensor("eye_fd", [128, 128], F32, kind="ExternalInput")
    ones1f_d = nc.dram_tensor("ones1fd", [1, 128], F32, kind="ExternalInput")
    onescol_d = nc.dram_tensor("onescold", [128, 1], BF16, kind="ExternalInput")
    selbf_d = nc.dram_tensor("selbfd", [8, 768], BF16, kind="ExternalInput")
    ubits_d = nc.dram_tensor("ubitsd", [128, 7], F32, kind="ExternalInput")
    c1bits_d = nc.dram_tensor("c1bitsd", [128, 7], F32, kind="ExternalInput")
    expnbits_d = nc.dram_tensor("expnbitsd", [128, 896], BF16, kind="ExternalInput")
    causT_d = nc.dram_tensor("causTd", [128, 128], BF16, kind="ExternalInput")

    y_s = nc.dram_tensor("y_s", [D, T], F32, **KINDF("y_s"))
    r0hi_s = nc.dram_tensor("r0hi_s", [D, T], BF16, **KINDF("r0hi_s"))
    r0lo_s = nc.dram_tensor("r0lo_s", [D, T], BF16, **KINDF("r0lo_s"))
    xt1hi_s = nc.dram_tensor("xt1hi_s", [D, T], BF16, **KINDF("xt1hi_s"))
    xt1lo_s = nc.dram_tensor("xt1lo_s", [D, T], BF16, **KINDF("xt1lo_s"))
    wqkv_s = nc.dram_tensor("wqkv_s", [7, 128, KD, AD], BF16)
    wup_s = nc.dram_tensor("wup_s", [NU, 8, 128, KD, 128], BF16)
    wsw_s = nc.dram_tensor("wsw_s", [NU, 8, 128, ET, 128], BF16)
    wdn_s = nc.dram_tensor("wdn_s", [HT, 128, NU, ET, 128], BF16)
    x2T = nc.dram_tensor("x2T", [D, T], F32, kind="ExternalOutput")
    dbgout = nc.dram_tensor("dbgout", [128, 32768], F32, kind="ExternalOutput") if dbg else None
    dbgb = nc.dram_tensor("dbgb", [128, 8192], BF16, kind="ExternalOutput") if dbg else None

    def fm(dr):
        return dr.rearrange("(k p) t -> p k t", p=128)

    with contextlib.ExitStack() as top:
        S = Sems(nc, top, ["ld", "st", "xld", "wld", "dld", "mm", "ev", "evv", "prog", "fl"])
        sb = lambda name, shape, dt: top.enter_context(nc.sbuf_tensor(name, shape, dt))

        def mk_rec():
            Q = {"tensor": [], "scalar": [], "vector": [], "sync": [], "gpsimd": []}
            TRACE = {k: [] for k in Q}
            if not hasattr(nc, "_dbg_traces"):
                nc._dbg_traces = []
            nc._dbg_traces.append((dict(S.c), TRACE))

            def em(eng, fn):
                Q[eng].append(fn)

            def wv(eng, sem, cnt):
                assert cnt <= S.c[sem], (eng, sem, cnt, S.c[sem])
                TRACE[eng].append(("wait", sem, cnt))
                em(eng, lambda e, sem=sem, cnt=cnt: e.wait_ge(S.h[sem], cnt))

            def wr(eng, rel):
                if rel is not None:
                    wv(eng, rel[0], rel[1])

            def dmaQ(eng, sem, dst, src):
                S.c[sem] += 16
                TRACE[eng].append(("inc", sem, 16))
                em(eng, lambda e, dst=dst, src=src, sem=sem: e.dma_start(dst, src).then_inc(S.h[sem], 16))
                return S.c[sem]

            def incQ(eng, sem, fn):
                S.c[sem] += 1
                TRACE[eng].append(("inc", sem, 1))
                em(eng, lambda e, fn=fn, sem=sem: fn(e).then_inc(S.h[sem], 1))
                return S.c[sem]

            def flQ(eng, fn):
                S.c["fl"] += 1
                cnt = S.c["fl"]
                TRACE[eng].append(("inc", "fl", 1))
                TRACE[eng].append(("wait", "fl", cnt))
                em(eng, lambda e, fn=fn, cnt=cnt: (fn(e).then_inc(S.h["fl"], 1),
                                                   e.wait_ge(S.h["fl"], cnt)))
            return Q, em, wv, wr, dmaQ, incQ, flQ

        eye_bf = sb("eye_bf", [128, 128], BF16)
        eye_f = sb("eye_f", [128, 128], F32)
        ones1f = sb("ones1f", [1, 128], F32)
        onescol = sb("onescol", [128, 1], BF16)
        selbf = sb("selbf", [8, 768], BF16)
        wghi = sb("wghi", [128, 2, 16, 8], BF16)
        wglo = sb("wglo", [128, 2, 16, 8], BF16)
        bgb = sb("bgb", [128, 16], F32)
        bcols = sb("bcols", [128, 120], F32)
        bext_bf = sb("bext_bf", [8, D], BF16)
        osc_col = sb("osc_col", [128, 1], F32)

        with contextlib.ExitStack() as pa:
            sa = lambda name, shape, dt: pa.enter_context(nc.sbuf_tensor(name, shape, dt))

            wo_sb = sa("wo_sb", [128, ADT, D], BF16)
            wloc = sa("wloc", [128, 3, 128], BF16)
            cmb = sa("cmb", [128, 3, 128], BF16)
            Isc = sa("Isc", [128, 3, 128], BF16)
            wtmp = sa("wtmp", [128, 128], BF16)
            dvec = sa("dvec", [128, 12], F32)
            dvecs = [dvec[:, i : i + 1] for i in range(12)]
            decu_sb = sa("decu_sb", [128, 8], F32)
            rowd = sa("rowd", [1, 4], F32)
            wg3_sb = sa("wg3_sb", [128, KD, 3], BF16)

            with contextlib.ExitStack() as ws:
                psA_misc = ws.enter_context(nc.psum_tensor("psW_misc", [128, 512], F32))
                scr_f = ws.enter_context(nc.sbuf_tensor("scr_f", [128, 160], F32))
                ubits = ws.enter_context(nc.sbuf_tensor("ubits", [128, 7], F32))
                c1bits = ws.enter_context(nc.sbuf_tensor("c1bits", [128, 7], F32))
                expnbits = ws.enter_context(nc.sbuf_tensor("expnbits", [128, 896], BF16))
                causT = ws.enter_context(nc.sbuf_tensor("causT", [128, 128], BF16))

                with nc.Block() as blk:
                    @blk.vector
                    def _(v):
                        v.memset(bext_bf[:], 0.0)
                with nc.Block() as blk:
                    @blk.sync
                    def _(e):
                        for dst, src in [
                            (eye_bf[:], eye_bf_d), (eye_f[:], eye_f_d), (ones1f[:], ones1f_d),
                            (onescol[:], onescol_d), (selbf[:], selbf_d), (ubits[:], ubits_d),
                            (c1bits[:], c1bits_d), (expnbits[:], expnbits_d), (causT[:], causT_d),
                            (bcols[:], biascols), (rowd[0:1, 0:3], dlog),
                            (scr_f[0:1, 0:1], oscale), (scr_f[0:1, 1:17], bgh),
                        ]:
                            S.dma(e, dst, src[:], "ld")

                    @blk.gpsimd
                    def _(g):
                        S.dma(g, bext_bf[0:7, :], bexth[:], "st")
                        S.dma(g, wqkv_s[0], wq.rearrange("(k p) d -> p k d", p=128), "st")
                        for i in range(3):
                            S.dma(g, wqkv_s[1 + 2 * i], wk[i].rearrange("(k p) d -> p k d", p=128), "st")
                            S.dma(g, wqkv_s[2 + 2 * i], wv[i].rearrange("(k p) d -> p k d", p=128), "st")
                        S.dma(g, wo_sb[:], wo.rearrange("(k p) d -> p k d", p=128), "st")
                        S.dma(g, wg3_sb[:], wg3h.rearrange("p (k d) -> p k d", d=3), "st")
                        for u in range(NU):
                            if u < 6:
                                upsrc = fwup[u].rearrange("(k p) e -> p k e", p=128)
                                swsrc = fwsw[u].rearrange("(k p) e -> p k e", p=128)
                                dnsrc = fwdn[u].rearrange("(k p) (m j) -> p k m j", p=128, j=128)
                            else:
                                upsrc = fwsu.rearrange("(k p) e -> p k e", p=128)[:, :, (u - 6) * E : (u - 5) * E]
                                swsrc = fwsa.rearrange("(k p) e -> p k e", p=128)
                                dnsrc = fwsd[(u - 6) * E : (u - 5) * E, :].rearrange("(k p) (m j) -> p k m j", p=128, j=128)
                            for cc in range(8):
                                S.dma(g, wup_s[u, cc], upsrc[:, :, cc * 128 : (cc + 1) * 128], "st")
                                S.dma(g, wsw_s[u, cc], swsrc[:, :, cc * 128 : (cc + 1) * 128], "st")
                            for kk in range(ET):
                                S.dma(g, wdn_s[:, :, u, kk, :].rearrange("m p j -> p m j"),
                                      dnsrc[:, kk], "st")

                with nc.Block() as blk:
                    @blk.tensor
                    def _(t):
                        t.wait_ge(S.h["ld"], S.c["ld"])
                        t.matmul(psA_misc[:, 0:1], ones1f[:], scr_f[0:1, 0:1], start=True, stop=True)
                        t.matmul(psA_misc[:, 16:32], ones1f[:], scr_f[0:1, 1:17], start=True, stop=True)
                with nc.Block() as blk:
                    @blk.vector
                    def _(v):
                        v.tensor_copy(osc_col[:], psA_misc[:, 0:1])
                        v.tensor_copy(bgb[:], psA_misc[:, 16:32])
                with nc.Block() as blk:
                    @blk.vector
                    def _(v):
                        v.wait_ge(S.h["st"], S.c["st"])

                for j in range(2):
                    with nc.Block() as blk:
                        @blk.sync
                        def _(e, j=j):
                            S.dma(e, scr_f[:, 32:160], fgh[j], "ld")
                    with nc.Block() as blk:
                        @blk.vector
                        def _(v, j=j):
                            v.wait_ge(S.h["ld"], S.c["ld"])
                            hi = wghi[:, j].rearrange("p k d -> p (k d)")
                            lo = wglo[:, j].rearrange("p k d -> p (k d)")
                            v.tensor_copy(hi, scr_f[:, 32:160])
                            v.tensor_tensor(lo, scr_f[:, 32:160], hi, ALU.subtract)

                for i in range(3):
                    with nc.Block() as blk:
                        @blk.tensor
                        def _(t, i=i):
                            t.matmul(psA_misc[:, 32:33], ones1f[:], rowd[0:1, i : i + 1], start=True, stop=True)
                    with nc.Block() as blk:
                        @blk.scalar
                        def _(sc):
                            sc.activation(dvecs[0], psA_misc[:, 32:33], AF.Sigmoid)
                    with nc.Block() as blk:
                        @blk.vector
                        def _(v, i=i):
                            F = lambda inst: S.flush(v, inst)
                            for b in range(1, 7):
                                F(v.tensor_mul(dvecs[b], dvecs[b - 1], dvecs[b - 1]))
                            F(v.tensor_mul(dvecs[7], dvecs[6], dvecs[6]))
                            for tgt, bits in ((i, ubits), (3 + i, c1bits)):
                                cur = 8
                                for b in range(7):
                                    F(v.tensor_scalar_add(dvecs[10], dvecs[b], -1.0))
                                    F(v.tensor_scalar(dvecs[11], bits[:, b : b + 1], dvecs[10], 1.0, ALU.mult, ALU.add))
                                    if b == 0:
                                        F(v.tensor_copy(dvecs[cur], dvecs[11]))
                                    else:
                                        F(v.tensor_mul(dvecs[cur ^ 1], dvecs[cur], dvecs[11]))
                                        cur ^= 1
                                F(v.tensor_copy(decu_sb[:, tgt : tgt + 1], dvecs[cur]))
                            F(v.tensor_copy(wloc[:, i], causT[:]))
                            cur = 0
                            for b in range(7):
                                F(v.tensor_scalar_add(dvecs[10], dvecs[b], -1.0))
                                F(v.tensor_scalar(wtmp[:], expnbits[:, b * 128 : (b + 1) * 128], dvecs[10], 1.0, ALU.mult, ALU.add))
                                if cur == 0:
                                    F(v.tensor_mul(cmb[:, i], wloc[:, i], wtmp[:]))
                                else:
                                    F(v.tensor_mul(wloc[:, i], cmb[:, i], wtmp[:]))
                                cur ^= 1
                            if cur == 1:
                                F(v.tensor_copy(wloc[:, i], cmb[:, i]))
                            F(v.tensor_scalar(Isc[:, i], eye_bf[:], dvecs[7], None, ALU.mult))

            RQ, RKF, RKT, RV, RGB, RW = 0, 512, 2048, 3584, 5120, 5888

            psR = pa.enter_context(nc.psum_tensor("psR", [128, 2048], F32))
            psBC = pa.enter_context(nc.psum_tensor("psBC", [128, 1024], F32))
            psTP = pa.enter_context(nc.psum_tensor("psTP", [128, 1024], BF16))
            psX = pa.enter_context(nc.psum_tensor("psX", [128, 512], F32))

            fb = sa("fb", [128, KD, TB], F32)
            st1 = sa("st1", [128, 2, TB], BF16)
            st2 = sa("st2", [128, 2, TB], BF16)
            xn = sa("xn", [128, KD, TB], BF16)
            recb = sa("recb", [128, 4, RW], BF16)
            wpj = sa("wpj", [128, 2, KD, AD], BF16)
            qkt = sa("qkt", [128, 2, TB], BF16)
            gwork = sa("gwork", [128, 16], F32)
            gdT_sb = sa("gdT_sb", [8, 128], BF16)
            S_sb = sa("S_sb", [128, 3, ADT, AD], BF16)
            qt = sa("qt", [128, 3, AD], BF16)
            cmbA = sa("cmbA", [128, 3, 128], BF16)
            scb = sa("scb", [128, 3, 128], BF16)
            retrbf = sa("retrbf", [128, AD], BF16)
            xs = sa("xs", [128, 2, TB], F32)
            ta0 = sa("ta0", [128, TB], F32)
            hl = sa("hl", [128, 2, 2, TB], BF16)
            rowsA = sa("rowsA", [1, 3584], F32)
            RS = [rowsA[0:1, i * 512:(i + 1) * 512] for i in range(7)]
            recb_h = recb.tensor if hasattr(recb, "tensor") else recb

            QA, emA, wvA, wrA, dmaA, incA, flA = mk_rec()

            def rows_mathA(s1ap, s2ap, mmcnt):
                wvA("vector", "mm", mmcnt)
                flA("vector", lambda e: e.tensor_scalar(RS[0], s1ap, 1.0 / D, None, ALU.mult))
                flA("vector", lambda e: e.tensor_scalar(RS[1], s2ap, 1.0 / D, None, ALU.mult))
                flA("vector", lambda e: e.tensor_mul(RS[2], RS[0], RS[0]))
                flA("vector", lambda e: e.tensor_sub(RS[4], RS[1], RS[2]))
                ve = incA("vector", "evv", lambda e: e.tensor_scalar_add(RS[3], RS[4], EPS))
                wvA("scalar", "evv", ve)
                flA("scalar", lambda e: e.activation(RS[2], RS[3], AF.Ln))
                se = incA("scalar", "ev", lambda e: e.activation(RS[4], RS[2], AF.Exp, bias=0.0, scale=-0.5))
                wvA("vector", "ev", se)
                flA("vector", lambda e: e.tensor_mul(RS[1], RS[4], RS[4]))
                flA("vector", lambda e: e.tensor_mul(RS[2], RS[1], RS[3]))
                flA("vector", lambda e: e.tensor_scalar(RS[1], RS[2], -0.5, 1.5, ALU.mult, ALU.add))
                flA("vector", lambda e: e.tensor_mul(RS[6], RS[4], RS[1]))
                return incA("vector", "evv", lambda e: e.tensor_mul(RS[5], RS[0], RS[6]))

            bankA = [None, None, None, None]
            bA_i = [0]

            def acqA():
                b = bA_i[0] % 4
                bA_i[0] += 1
                wrA("tensor", bankA[b])
                return b

            def bcs(e):
                e.matmul(psBC[:, 0:512], ones1f[:], RS[5], start=True, stop=True)
                return e.matmul(psBC[:, 512:1024], ones1f[:], RS[6], start=True, stop=True)

            wvA("sync", "st", S.c["st"])
            memev = incA("vector", "evv", lambda e: e.memset(S_sb[:].rearrange("p a b c -> p (a b c)"), 0.0))
            supd_done = memev
            wp_i = [0]
            wpj_ld = {}
            wpj_cons = {}
            PJS = [("q", None), ("k", 0), ("v", 0), ("k", 1), ("v", 1), ("k", 2), ("v", 2)]

            def issue_wpj():
                idx = wp_i[0]
                if idx >= 7 * NB:
                    return
                r = wpj_cons.get(idx - 2)
                if r is not None:
                    wvA("sync", "mm", r)
                wpj_ld[idx] = dmaA("sync", "wld", wpj[:, idx % 2], wqkv_s[idx % 7])
                wp_i[0] += 1

            issue_wpj()
            issue_wpj()

            fb_free = None
            fb_free_st = None
            bc_free = None
            psX_rel = None
            scan_done_mm = None
            prev_retr_mm = None
            sel_last_mm = None
            gate_tmm_prev = None
            hl_carry = {0: None, 1: None}
            tp_state = {"rel": None, "last_tmm": None}
            pend_tp = []

            def emit_tp():
                kind, i, slot, ev = pend_tp.pop(0)
                wvA("tensor", "evv", ev)
                if tp_state["rel"] is not None:
                    wrA("tensor", tp_state["rel"])

                def tmms(e, slot=slot):
                    for jj in range(ADT):
                        mi = e.transpose(psTP[:, jj * 128:(jj + 1) * 128],
                                         qkt[:, slot, jj * 128:(jj + 1) * 128], eye_bf[:])
                    return mi
                tmm = incA("tensor", "mm", tmms)
                tp_state["last_tmm"] = tmm
                wvA("vector", "mm", tmm)
                sub, off = kind
                tev = incA("vector", "evv",
                           lambda e, sub=sub, off=off: e.tensor_copy(recb[:, sub, off:off + 512], psTP[:, 0:512]))
                tp_state["rel"] = ("evv", tev)

            for gi in range(NB if ABLOCKS is None else ABLOCKS):
                g = NB - 1 - gi
                wrA("sync", fb_free)
                wrA("sync", fb_free_st)
                fb_ld = dmaA("sync", "ld", fb[:], fm(xT)[:, :, g * TB:(g + 1) * TB])

                stat_mm = {}
                for k in range(KD):
                    if k == 0:
                        wvA("scalar", "ld", fb_ld)
                    if k >= 2:
                        wvA("scalar", "mm", stat_mm[k - 2])
                    incA("scalar", "ev", lambda e, k=k: e.activation(st1[:, k % 2], fb[:, k], AF.Identity))
                    e2 = incA("scalar", "ev", lambda e, k=k: e.activation(st2[:, k % 2], fb[:, k], AF.Square))
                    wvA("tensor", "ev", e2)
                    if k == 0:
                        wrA("tensor", bc_free)

                    def smms(e, k=k):
                        e.matmul(psBC[0:1, 0:512], onescol[:], st1[:, k % 2], start=(k == 0), stop=(k == KD - 1))
                        return e.matmul(psBC[0:1, 512:1024], onescol[:], st2[:, k % 2], start=(k == 0), stop=(k == KD - 1))
                    stat_mm[k] = incA("tensor", "mm", smms)
                ln1_scalar_ev = S.c["ev"]
                rev = rows_mathA(psBC[0:1, 0:512], psBC[0:1, 512:1024], stat_mm[KD - 1])
                wvA("tensor", "evv", rev)
                bc_mm = incA("tensor", "mm", bcs)

                if dbg and gi == 0:
                    wvA("gpsimd", "mm", bc_mm)
                    dmaA("gpsimd", "st", dbgout[0:1, 8192:8192 + 3584], rowsA[:])
                aev = None
                for k in range(KD):
                    if k == 0:
                        wvA("vector", "mm", bc_mm)
                        wvA("vector", "ld", fb_ld)
                    flA("vector", lambda e, k=k: e.tensor_tensor(ta0[:], fb[:, k], psBC[:, 512:1024], ALU.mult))
                    aev = incA("vector", "evv", lambda e, k=k: e.tensor_tensor(xn[:, k, :], ta0[:], psBC[:, 0:512], ALU.subtract))
                apply_done = aev
                bc_free = ("evv", apply_done)

                qk_ctr = [0]
                for pj in range(7):
                    kind, i = PJS[pj]
                    widx = gi * 7 + pj
                    wslot = widx % 2
                    wvA("tensor", "wld", wpj_ld[widx])
                    for sub in range(4):
                        b = acqA()
                        if pj == 0 and sub == 0:
                            wvA("tensor", "evv", apply_done)

                        def pmms(e, sub=sub, b=b, wslot=wslot):
                            for kk in range(KD):
                                mi = e.matmul(psR[:, b * 512:(b + 1) * 512], xn[:, kk, sub * 128:(sub + 1) * 128],
                                              wpj[:, wslot, kk, :], start=(kk == 0), stop=(kk == KD - 1))
                            return mi
                        pmm = incA("tensor", "mm", pmms)
                        if pend_tp:
                            emit_tp()
                        wvA("vector", "mm", pmm)
                        if pj == 0 and sub == 0 and scan_done_mm is not None:
                            wvA("vector", "mm", scan_done_mm)
                        sl = psR[:, b * 512:(b + 1) * 512]
                        if kind == "v":
                            ev = incA("vector", "evv",
                                      lambda e, sl=sl, sub=sub, i=i: e.tensor_copy(
                                          recb[:, sub, RV + i * 512:RV + (i + 1) * 512], sl))
                        else:
                            tslot = qk_ctr[0] % 2
                            qk_ctr[0] += 1
                            if tp_state["last_tmm"] is not None:
                                wvA("vector", "mm", tp_state["last_tmm"])
                            if kind == "q":
                                ev = incA("vector", "evv",
                                          lambda e, sl=sl, tslot=tslot: e.tensor_copy(qkt[:, tslot], sl))
                                pend_tp.append(((sub, RQ), i, tslot, ev))
                            else:
                                flA("vector", lambda e, sl=sl, tslot=tslot: e.tensor_copy(qkt[:, tslot], sl))
                                ev = incA("vector", "evv",
                                          lambda e, sl=sl, sub=sub, i=i: e.tensor_scalar(
                                              recb[:, sub, RKT + i * 512:RKT + (i + 1) * 512], sl,
                                              decu_sb[:, i:i + 1], None, ALU.mult))
                                pend_tp.append(((sub, RKF + i * 512), i, tslot, ev))
                        bankA[b] = ("evv", ev)
                    wpj_cons[widx] = S.c["mm"]
                    issue_wpj()
                while pend_tp:
                    emit_tp()
                if dbg and gi == 0:
                    wvA("gpsimd", "evv", apply_done)
                    dmaA("gpsimd", "st", dbgb[:, 0:8192].rearrange("p (k t) -> p k t", t=TB), xn[:])

                rec_done = {}
                for sub in range(4):
                    wrA("tensor", psX_rel)

                    def gmms(e, sub=sub):
                        for kk in range(KD):
                            mi = e.matmul(psX[:, 256:259], xn[:, kk, sub * 128:(sub + 1) * 128],
                                          wg3_sb[:, kk, :], start=(kk == 0), stop=(kk == KD - 1))
                        return mi
                    gmm = incA("tensor", "mm", gmms)
                    wvA("vector", "mm", gmm)
                    ge = incA("vector", "evv", lambda e: e.tensor_copy(gwork[:, 0:3], psX[:, 256:259]))
                    psX_rel = ("evv", ge)
                    wvA("vector", "evv", ge)
                    wvA("scalar", "evv", ge)
                    se = incA("scalar", "ev", lambda e: e.activation(gwork[:, 3:6], gwork[:, 0:3], AF.Exp))
                    wvA("vector", "ev", se)
                    flA("vector", lambda e: e.reduce_sum(gwork[:, 6:7], gwork[:, 3:6], axis=AX.X))
                    flA("vector", lambda e: e.reciprocal(gwork[:, 7:8], gwork[:, 6:7]))
                    if gate_tmm_prev is not None:
                        wvA("vector", "mm", gate_tmm_prev)
                    flA("vector", lambda e: e.tensor_scalar(gwork[:, 8:11], gwork[:, 3:6], gwork[:, 7:8], None, ALU.mult))
                    for i3 in range(2):
                        emA("vector", lambda e, i3=i3: e.tensor_scalar(gwork[:, 11 + i3:12 + i3],
                                                                       gwork[:, 8 + i3:9 + i3],
                                                                       decu_sb[:, 3 + i3:4 + i3], None, ALU.mult))
                    gdev = incA("vector", "evv",
                                lambda e: e.tensor_scalar(gwork[:, 13:14], gwork[:, 10:11],
                                                          decu_sb[:, 5:6], None, ALU.mult))
                    wvA("tensor", "evv", gdev)
                    wrA("tensor", psX_rel)
                    tmm = incA("tensor", "mm", lambda e: e.transpose(psX[0:6, 128:256], gwork[:, 8:14], eye_f[:]))
                    gate_tmm_prev = tmm
                    wvA("vector", "mm", tmm)
                    if sel_last_mm is not None:
                        wvA("vector", "mm", sel_last_mm)
                    gt = incA("vector", "evv", lambda e: e.tensor_copy(gdT_sb[0:6, :], psX[0:6, 128:256]))
                    psX_rel = ("evv", gt)
                    wvA("tensor", "evv", gt)
                    for n in range(6):
                        b = acqA()
                        smm = incA("tensor", "mm",
                                   lambda e, n=n, b=b: e.matmul(psR[:, b * 512:b * 512 + 128],
                                                                selbf[0:6, n * 128:(n + 1) * 128], gdT_sb[0:6, :],
                                                                start=True, stop=True))
                        wvA("vector", "mm", smm)
                        i3 = n % 3
                        off = RGB + (i3 * 128 if n < 3 else 384 + i3 * 128)
                        ev = incA("vector", "evv",
                                  lambda e, sub=sub, off=off, b=b: e.tensor_copy(recb[:, sub, off:off + 128],
                                                                                 psR[:, b * 512:b * 512 + 128]))
                        bankA[b] = ("evv", ev)
                        sel_last_mm = smm
                    rec_done[sub] = S.c["evv"]

                wo_evacs_started = False
                for sub in (3, 2, 1, 0):
                    roff = sub * RW
                    prep_ev = None
                    if prev_retr_mm is not None:
                        wvA("vector", "mm", prev_retr_mm)
                    for i in range(3):
                        flA("vector", lambda e, sub=sub, i=i: e.tensor_tensor(
                            cmbA[:, i], wloc[:, i], recb[:, sub, RGB + i * 128:RGB + (i + 1) * 128], ALU.mult))
                        brep = bass.AP(recb_h, roff + RGB + 384 + i * 128, [[4 * RW, 128], [0, ADT], [1, 128]])
                        prep_ev = incA("vector", "evv", lambda e, sub=sub, i=i, brep=brep: e.tensor_tensor(
                            qt[:, i], recb[:, sub, RQ:RQ + 512].rearrange("p (a b) -> p a b", b=128), brep, ALU.mult))

                    rb = acqA()
                    first_retr = True
                    retr_mm = None
                    for i in range(3):
                        wrA("tensor", psX_rel)
                        if i == 0:
                            wvA("tensor", "evv", max(rec_done[sub], supd_done, prep_ev))

                        def scmms(e, sub=sub, i=i):
                            for kd in range(ADT):
                                mi = e.matmul(psX[:, 0:128],
                                              recb[:, sub, RKF + i * 512 + kd * 128:RKF + i * 512 + (kd + 1) * 128],
                                              recb[:, sub, RQ + kd * 128:RQ + (kd + 1) * 128],
                                              start=(kd == 0), stop=(kd == ADT - 1))
                            return mi
                        smm = incA("tensor", "mm", scmms)
                        wvA("vector", "mm", smm)
                        scev = incA("vector", "evv",
                                    lambda e, i=i: e.tensor_tensor(scb[:, i], psX[:, 0:128], cmbA[:, i], ALU.mult))
                        psX_rel = ("evv", scev)
                        wvA("tensor", "evv", scev)

                        def rmms(e, sub=sub, i=i, rb=rb, fr=first_retr):
                            for m in range(ADT):
                                e.matmul(psR[:, rb * 512 + m * 128:rb * 512 + (m + 1) * 128],
                                         recb[:, sub, RV + i * 512 + m * 128:RV + i * 512 + (m + 1) * 128],
                                         scb[:, i], start=fr, stop=False)
                                fr = False
                            for m in range(ADT):
                                for kd in range(ADT):
                                    last = (i == 2 and m == ADT - 1 and kd == ADT - 1)
                                    mi = e.matmul(psR[:, rb * 512 + m * 128:rb * 512 + (m + 1) * 128],
                                                  S_sb[:, i, kd, m * 128:(m + 1) * 128],
                                                  qt[:, i, kd * 128:(kd + 1) * 128], start=False, stop=last)
                            return mi
                        retr_mm = incA("tensor", "mm", rmms)
                        first_retr = False
                    prev_retr_mm = retr_mm
                    wvA("vector", "mm", retr_mm)
                    rev2 = incA("vector", "evv",
                                lambda e, rb=rb: e.tensor_scalar(retrbf[:], psR[:, rb * 512:(rb + 1) * 512],
                                                                 osc_col[:], None, ALU.mult))
                    bankA[rb] = ("evv", rev2)

                    for i in range(3):
                        for m in range(ADT):
                            b2 = acqA()

                            def umms(e, sub=sub, i=i, m=m, b2=b2):
                                e.matmul(psR[:, b2 * 512:(b2 + 1) * 512],
                                         recb[:, sub, RKT + i * 512 + m * 128:RKT + i * 512 + (m + 1) * 128],
                                         recb[:, sub, RV + i * 512:RV + (i + 1) * 512], start=True, stop=False)
                                return e.matmul(psR[:, b2 * 512:(b2 + 1) * 512], Isc[:, i],
                                                S_sb[:, i, m, :], start=False, stop=True)
                            umm = incA("tensor", "mm", umms)
                            wvA("vector", "mm", umm)
                            uev = incA("vector", "evv",
                                       lambda e, i=i, m=m, b2=b2: e.tensor_copy(S_sb[:, i, m, :],
                                                                                psR[:, b2 * 512:(b2 + 1) * 512]))
                            bankA[b2] = ("evv", uev)
                    supd_done = S.c["evv"]

                    wvA("tensor", "evv", rev2)
                    for mg in range(4):
                        b3 = acqA()

                        def wmms(e, mg=mg, b3=b3):
                            for lm in range(4):
                                m = mg * 4 + lm
                                for kd in range(ADT):
                                    mi = e.matmul(psR[:, b3 * 512 + lm * 128:b3 * 512 + (lm + 1) * 128],
                                                  wo_sb[:, kd, m * 128:(m + 1) * 128],
                                                  retrbf[:, kd * 128:(kd + 1) * 128],
                                                  start=(kd == 0), stop=(kd == ADT - 1))
                            return mi
                        wmm = incA("tensor", "mm", wmms)
                        wvA("vector", "mm", wmm)
                        if not wo_evacs_started:
                            wvA("vector", "ev", ln1_scalar_ev)
                            wo_evacs_started = True
                        wev = incA("vector", "evv",
                                   lambda e, sub=sub, mg=mg, b3=b3: e.tensor_copy(
                                       fb[:, mg * 4:(mg + 1) * 4, sub * 128:(sub + 1) * 128],
                                       psR[:, b3 * 512:(b3 + 1) * 512].rearrange("p (a b) -> p a b", b=128)))
                        bankA[b3] = ("evv", wev)
                scan_done_mm = S.c["mm"]
                ao_done = S.c["evv"]
                if dbg and gi == 0:
                    wvA("gpsimd", "evv", ao_done)
                    dmaA("gpsimd", "st", dbgout[:, 0:8192].rearrange("p (k t) -> p k t", t=TB), fb[:])

                stat2_mm = {}
                for k in range(KD):
                    if k == 0:
                        wvA("scalar", "evv", ao_done)
                    if k >= 2:
                        wvA("scalar", "mm", stat2_mm[k - 2])
                    incA("scalar", "ev", lambda e, k=k: e.activation(st1[:, k % 2], fb[:, k], AF.Identity))
                    e2 = incA("scalar", "ev", lambda e, k=k: e.activation(st2[:, k % 2], fb[:, k], AF.Square))
                    wvA("tensor", "ev", e2)
                    if k == 0:
                        wrA("tensor", bc_free)

                    def smms2(e, k=k):
                        e.matmul(psBC[0:1, 0:512], onescol[:], st1[:, k % 2], start=(k == 0), stop=(k == KD - 1))
                        return e.matmul(psBC[0:1, 512:1024], onescol[:], st2[:, k % 2], start=(k == 0), stop=(k == KD - 1))
                    stat2_mm[k] = incA("tensor", "mm", smms2)
                rev = rows_mathA(psBC[0:1, 0:512], psBC[0:1, 512:1024], stat2_mm[KD - 1])
                wvA("tensor", "evv", rev)
                bc2_mm = incA("tensor", "mm", bcs)

                sb1 = acqA()
                bankA[sb1] = ("evv", 10 ** 9)
                sb2 = acqA()
                bankA[sb2] = ("evv", 10 ** 9)
                yst = {}
                yev_l = {}
                stat3_mm = {}
                for k in range(KD):
                    if k >= 2:
                        wvA("scalar", "evv", yev_l[k - 2])
                    xld = dmaA("scalar", "xld", xs[:, k % 2], fm(xT)[:, k, g * TB:(g + 1) * TB])
                    wvA("vector", "xld", xld)
                    if k == 0:
                        wvA("vector", "mm", bc2_mm)
                    flA("vector", lambda e, k=k: e.tensor_tensor(ta0[:], fb[:, k], psBC[:, 512:1024], ALU.mult))
                    flA("vector", lambda e: e.tensor_tensor(ta0[:], ta0[:], psBC[:, 0:512], ALU.subtract))
                    yev = incA("vector", "evv", lambda e, k=k: e.tensor_tensor(fb[:, k], ta0[:], xs[:, k % 2], ALU.add))
                    yev_l[k] = yev
                    wvA("gpsimd", "evv", yev)
                    yst[k] = dmaA("gpsimd", "st", fm(y_s)[:, k, g * TB:(g + 1) * TB], fb[:, k, :])
                    wvA("scalar", "evv", yev)
                    if k >= 2:
                        wvA("scalar", "mm", stat3_mm[k - 2])
                    incA("scalar", "ev", lambda e, k=k: e.activation(st1[:, k % 2], fb[:, k], AF.Identity))
                    e2 = incA("scalar", "ev", lambda e, k=k: e.activation(st2[:, k % 2], fb[:, k], AF.Square))
                    wvA("tensor", "ev", e2)

                    def smms3(e, k=k, sb1=sb1, sb2=sb2):
                        e.matmul(psR[0:1, sb1 * 512:sb1 * 512 + 512], onescol[:], st1[:, k % 2],
                                 start=(k == 0), stop=(k == KD - 1))
                        return e.matmul(psR[0:1, sb2 * 512:sb2 * 512 + 512], onescol[:], st2[:, k % 2],
                                        start=(k == 0), stop=(k == KD - 1))
                    stat3_mm[k] = incA("tensor", "mm", smms3)
                rev = rows_mathA(psR[0:1, sb1 * 512:sb1 * 512 + 512], psR[0:1, sb2 * 512:sb2 * 512 + 512], stat3_mm[KD - 1])
                bankA[sb1] = ("evv", rev)
                bankA[sb2] = ("evv", rev)
                wvA("tensor", "evv", rev)
                bc3_mm = incA("tensor", "mm", bcs)

                hlst = {}
                hlev = None
                for k in range(KD):
                    if k == 0:
                        wvA("vector", "mm", bc3_mm)
                    if k >= 2:
                        wvA("vector", "st", hlst[k - 2])
                    elif hl_carry[k] is not None:
                        wrA("vector", hl_carry[k])
                    flA("vector", lambda e, k=k: e.tensor_tensor(ta0[:], fb[:, k], psBC[:, 512:1024], ALU.mult))
                    flA("vector", lambda e: e.tensor_tensor(ta0[:], ta0[:], psBC[:, 0:512], ALU.subtract))
                    flA("vector", lambda e, k=k: e.tensor_copy(hl[:, k % 2, 0], ta0[:]))
                    hlev = incA("vector", "evv",
                                lambda e, k=k: e.tensor_tensor(hl[:, k % 2, 1], ta0[:], hl[:, k % 2, 0], ALU.subtract))
                    wvA("gpsimd", "evv", hlev)
                    dmaA("gpsimd", "st", fm(r0hi_s)[:, k, g * TB:(g + 1) * TB], hl[:, k % 2, 0])
                    hlst[k] = dmaA("gpsimd", "st", fm(r0lo_s)[:, k, g * TB:(g + 1) * TB], hl[:, k % 2, 1])
                fb_free = ("evv", hlev)
                fb_free_st = ("st", yst[KD - 1])
                bc_free = ("evv", hlev)
                hl_carry = {0: ("st", hlst[KD - 2]), 1: ("st", hlst[KD - 1])}

            with nc.Block() as blk:
                @blk.tensor
                def _(te):
                    for fn in QA["tensor"]:
                        fn(te)

                @blk.scalar
                def _(sc):
                    for fn in QA["scalar"]:
                        fn(sc)

                @blk.vector
                def _(v):
                    for fn in QA["vector"]:
                        fn(v)

                @blk.sync
                def _(e):
                    for fn in QA["sync"]:
                        fn(e)

                @blk.gpsimd
                def _(gp):
                    for fn in QA["gpsimd"]:
                        fn(gp)


        TT = 1024
        NT = T // TT
        with contextlib.ExitStack() as pb:
            sbb = lambda name, shape, dt: pb.enter_context(nc.sbuf_tensor(name, shape, dt))
            psA = pb.enter_context(nc.psum_tensor("psA", [128, 2048], F32))
            psW = pb.enter_context(nc.psum_tensor("psW", [128, 1024], F32))
            psG = pb.enter_context(nc.psum_tensor("psG", [128, 512], F32))
            psS = pb.enter_context(nc.psum_tensor("psS", [128, 512], F32))

            xtu = sbb("xtu", [128, KD, TT], BF16)
            ph = sbb("ph", [128, KD, TT], BF16)
            acc = sbb("acc", [128, HT, TT], F32)
            upb = sbb("upb", [128, 3, KD, 128], BF16)
            swb = sbb("swb", [128, 4, ET, 128], BF16)
            dnb = sbb("dnb", [128, 4, ET, 128], BF16)
            wbn = sbb("wbn", [128, TT], BF16)
            wTx = sbb("wTx", [8, TT], BF16)
            xgl = sbb("xgl", [128, 2, KD, 128], BF16)
            rrow = sbb("rrow", [128, 2, TT], F32)
            rb16 = sbb("rb16", [128, 2, 2, TT], BF16)
            stg = sbb("stg", [128, 2, TT], F32)
            zsqb = sbb("zsqb", [128, TT], BF16)
            gg = sbb("gg", [128, 56], F32)

            Q, em, wv, wr, dmaQ, incQ, flQ = mk_rec()

            TOT_TILES = 2 * NT
            TOT_UP = TOT_TILES * NU * 8
            TOT_SW = TOT_TILES * NU * 8
            TOT_DN = TOT_TILES * NU * HT
            up_ld = {}; up_cons = {}
            sw_ld = {}; sw_cons = {}
            dn_ld = {}; dn_cons = {}

            def issue_up(g):
                if g >= TOT_UP:
                    return
                n, c = divmod(g % (NU * 8), 8)
                r = up_cons.get(g - 3)
                if r is not None:
                    wv("sync", "mm", r)
                up_ld[g] = dmaQ("sync", "wld", upb[:, g % 3], wup_s[n, c])

            def issue_sw(g):
                if g >= TOT_SW:
                    return
                n, c = divmod(g % (NU * 8), 8)
                r = sw_cons.get(g - 4)
                if r is not None:
                    wv("sync", "mm", r)
                sw_ld[g] = dmaQ("sync", "wld", swb[:, g % 4], wsw_s[n, c])

            def issue_dn(g):
                if g >= TOT_DN:
                    return
                n, m = divmod(g % (NU * HT), HT)
                r = dn_cons.get(g - 4)
                if r is not None:
                    wv("gpsimd", "mm", r)
                dn_ld[g] = dmaQ("gpsimd", "dld", dnb[:, g % 4], wdn_s[m, :, n, :, :])

            bankrel = [None, None, None, None]
            bank_i = [0]

            def acquire_bank():
                b = bank_i[0] % 4
                bank_i[0] += 1
                wr("tensor", bankrel[b])
                return b

            carry = {"xtu": None, "ph": None, "psG": None, "psW": None, "wTx": None,
                     "stg": None, "rrow": {0: None, 1: None}, "rb16": {0: None, 1: None},
                     "xgl": {0: None, 1: None}}
            zstore = {}
            tile_iter = 0
            for j in range(2):
                hi_src = r0hi_s if j == 0 else xt1hi_s
                lo_src = r0lo_s if j == 0 else xt1lo_s
                for t in range(NT):
                    base = t * TT
                    if tile_iter == 0:
                        st_all = S.c["st"]
                        wv("sync", "st", st_all)
                        wv("scalar", "st", st_all)
                        wv("gpsimd", "st", st_all)
                    if j == 1:
                        wv("sync", "st", zstore[t])
                        wv("scalar", "st", zstore[t])
                    wr("sync", carry["xtu"])
                    xtu_ldv = dmaQ("sync", "ld", xtu[:], fm(hi_src)[:, :, base:base + TT])
                    if tile_iter == 0:
                        for g0 in range(3):
                            issue_up(g0)
                        for g0 in range(4):
                            issue_sw(g0)
                        for g0 in range(4):
                            issue_dn(g0)

                    wr("vector", carry["wTx"])
                    flQ("vector", lambda e: e.memset(wTx[:], 1.0))
                    G8 = gg[:, 0:8]
                    g_mm = {}; gate_rel = {}; gtr = {}; wTxrel = {}
                    for c in range(8):
                        if c >= 2:
                            wv("scalar", "mm", g_mm[c - 2])
                        else:
                            wr("scalar", carry["xgl"][c])
                        xldc = dmaQ("scalar", "xld", xgl[:, c % 2],
                                    fm(lo_src)[:, :, base + c * 128:base + (c + 1) * 128])
                        wv("tensor", "xld", xldc)
                        if c == 0:
                            wv("tensor", "ld", xtu_ldv)
                            wr("tensor", carry["psG"])
                        else:
                            wv("tensor", "evv", gate_rel[c - 1])

                        def g_mms(e, c=c, j=j):
                            for k in range(KD):
                                e.matmul(psG[:, 0:8], xtu[:, k, c * 128:(c + 1) * 128],
                                         wghi[:, j, k, :], start=(k == 0), stop=False)
                            for k in range(KD):
                                e.matmul(psG[:, 0:8], xtu[:, k, c * 128:(c + 1) * 128],
                                         wglo[:, j, k, :], start=False, stop=False)
                            for k in range(KD):
                                mi = e.matmul(psG[:, 0:8], xgl[:, c % 2, k, :],
                                              wghi[:, j, k, :], start=False, stop=(k == KD - 1))
                            return mi
                        g_mm[c] = incQ("tensor", "mm", g_mms)

                        wv("vector", "mm", g_mm[c])
                        if c > 0:
                            wv("vector", "mm", gtr[c - 1])
                        gate_rel[c] = incQ("vector", "evv",
                                           lambda e, j=j: e.tensor_tensor(G8, psG[:, 0:8], bgb[:, j * 8:(j + 1) * 8], ALU.add))
                        wv("vector", "evv", gate_rel[c])
                        flQ("vector", lambda e: e.tensor_single_scalar(gg[:, 8:16], G8, G8[:, 0:1], ALU.is_lt))
                        cur = 8
                        for mI in range(1, 8):
                            flQ("vector", lambda e, mI=mI: e.tensor_single_scalar(gg[:, 24:32], G8, G8[:, mI:mI + 1], ALU.is_lt))
                            nxt = 24 - cur
                            flQ("vector", lambda e, cur=cur, nxt=nxt: e.tensor_add(gg[:, nxt:nxt + 8], gg[:, cur:cur + 8], gg[:, 24:32]))
                            cur = nxt
                        flQ("vector", lambda e, cur=cur: e.tensor_single_scalar(gg[:, 32:40], gg[:, cur:cur + 8], 5.5, ALU.is_lt))
                        ggv = incQ("vector", "evv", lambda e: e.reduce_max(gg[:, 48:49], G8, axis=AX.X, negate=True))
                        wv("scalar", "evv", ggv)
                        gexp = incQ("scalar", "ev", lambda e: e.activation(gg[:, 40:48], gg[:, 0:8], AF.Exp, bias=gg[:, 48:49], scale=1.0))
                        wv("vector", "ev", gexp)
                        flQ("vector", lambda e: e.tensor_mul(gg[:, 24:32], gg[:, 40:48], gg[:, 32:40]))
                        flQ("vector", lambda e: e.reduce_sum(gg[:, 49:50], gg[:, 24:32], axis=AX.X))
                        flQ("vector", lambda e: e.reciprocal(gg[:, 50:51], gg[:, 49:50]))
                        gn = incQ("vector", "evv", lambda e: e.tensor_scalar(gg[:, 0:8], gg[:, 24:32], gg[:, 50:51], None, ALU.mult))
                        wv("tensor", "evv", gn)
                        if c > 0:
                            wv("tensor", "evv", wTxrel[c - 1])
                        gtr[c] = incQ("tensor", "mm", lambda e: e.transpose(psG[0:8, 128:256], gg[:, 0:8], eye_f[:]))
                        wv("vector", "mm", gtr[c])
                        wTxrel[c] = incQ("vector", "evv",
                                         lambda e, c=c: e.tensor_copy(wTx[0:6, c * 128:(c + 1) * 128], psG[0:6, 128:256]))
                    wtx_done = wTxrel[7]
                    carry["xgl"] = {0: ("mm", g_mm[6]), 1: ("mm", g_mm[7])}

                    selmm = {}; wbnrel = {}
                    updone = {}; swdone = {}; dndone = {}
                    for n in range(NU):
                        if n < 6:
                            wv("tensor", "evv", wtx_done if n == 0 else wbnrel[n - 1])
                            if n == 0:
                                wr("tensor", carry["psW"])

                            def sel_mms(e, n=n):
                                e.matmul(psW[:, 0:512], selbf[0:8, n * 128:(n + 1) * 128], wTx[0:8, 0:512], start=True, stop=True)
                                return e.matmul(psW[:, 512:1024], selbf[0:8, n * 128:(n + 1) * 128], wTx[0:8, 512:1024], start=True, stop=True)
                            selmm[n] = incQ("tensor", "mm", sel_mms)
                            wv("vector", "mm", selmm[n])
                            wbnrel[n] = incQ("vector", "evv", lambda e: e.tensor_copy(wbn[:], psW[:]))

                        last_up_ev = {}
                        for c in range(8):
                            gidx = tile_iter * 64 + n * 8 + c
                            wv("tensor", "wld", up_ld[gidx])
                            for sub in range(2):
                                b = acquire_bank()
                                if tile_iter == 0 and n == 0 and c == 0 and sub == 0:
                                    wv("tensor", "ld", xtu_ldv)

                                def up_mms(e, c=c, sub=sub, b=b, slot=gidx % 3):
                                    for k in range(KD):
                                        mi = e.matmul(psA[:, b * 512:(b + 1) * 512], upb[:, slot, k, :],
                                                      xtu[:, k, sub * 512:(sub + 1) * 512],
                                                      start=(k == 0), stop=(k == KD - 1))
                                    return mi
                                gm = incQ("tensor", "mm", up_mms)
                                wv("vector", "mm", gm)
                                if c == 0 and sub == 0:
                                    if n == 0:
                                        wr("vector", carry["ph"])
                                    else:
                                        wv("vector", "mm", swdone[n - 1])
                                bcol = (bcols[:, n * 8 + c:n * 8 + c + 1] if n < 6
                                        else bcols[:, 96 + (n - 6) * 8 + c:96 + (n - 6) * 8 + c + 1])
                                eva = incQ("vector", "evv",
                                           lambda e, c=c, sub=sub, b=b, bcol=bcol: e.tensor_scalar(
                                               ph[:, c, sub * 512:(sub + 1) * 512], psA[:, b * 512:(b + 1) * 512],
                                               bcol, None, ALU.add))
                                bankrel[b] = ("evv", eva)
                                last_up_ev[sub] = eva
                            up_cons[gidx] = S.c["mm"]
                            issue_up(gidx + 3)
                        updone[n] = S.c["mm"]

                        h2done = {}
                        for c in range(8):
                            gidx = tile_iter * 64 + n * 8 + c
                            wv("tensor", "wld", sw_ld[gidx])
                            for sub in range(2):
                                b = acquire_bank()
                                if c == 0:
                                    wv("tensor", "evv", last_up_ev[sub])

                                def sw_mms(e, c=c, sub=sub, b=b, slot=gidx % 4):
                                    for k in range(ET):
                                        mi = e.matmul(psA[:, b * 512:(b + 1) * 512], swb[:, slot, k, :],
                                                      ph[:, k, sub * 512:(sub + 1) * 512],
                                                      start=(k == 0), stop=(k == ET - 1))
                                    return mi
                                gm = incQ("tensor", "mm", sw_mms)
                                wv("scalar", "mm", gm)
                                if c == 0 and sub == 0 and n > 0:
                                    wv("scalar", "mm", dndone[n - 1])
                                bcol = (bcols[:, 48 + n * 8 + c:48 + n * 8 + c + 1] if n < 6
                                        else bcols[:, 112 + c:112 + c + 1])
                                eva = incQ("scalar", "ev",
                                           lambda e, c=c, sub=sub, b=b, bcol=bcol: e.activation(
                                               ph[:, ET + c, sub * 512:(sub + 1) * 512], psA[:, b * 512:(b + 1) * 512],
                                               AF.Silu, bias=bcol, scale=1.0))
                                bankrel[b] = ("ev", eva)
                                wv("vector", "ev", eva)
                                h2s = ph[:, ET + c, sub * 512:(sub + 1) * 512]
                                if n < 6:
                                    flQ("vector", lambda e, h2s=h2s, c=c, sub=sub: e.tensor_tensor(
                                        h2s, h2s, ph[:, c, sub * 512:(sub + 1) * 512], ALU.mult))
                                    h2done[sub] = incQ("vector", "evv",
                                                       lambda e, h2s=h2s, sub=sub: e.tensor_tensor(
                                                           h2s, h2s, wbn[:, sub * 512:(sub + 1) * 512], ALU.mult))
                                else:
                                    h2done[sub] = incQ("vector", "evv",
                                                       lambda e, h2s=h2s, c=c, sub=sub: e.tensor_tensor(
                                                           h2s, h2s, ph[:, c, sub * 512:(sub + 1) * 512], ALU.mult))
                            sw_cons[gidx] = S.c["mm"]
                            issue_sw(gidx + 4)
                        swdone[n] = S.c["mm"]

                        for m in range(HT):
                            gidx = tile_iter * 128 + n * 16 + m
                            wv("tensor", "dld", dn_ld[gidx])
                            for sub in range(2):
                                b = acquire_bank()
                                if m == 0:
                                    wv("tensor", "evv", h2done[sub])

                                def dn_mms(e, n=n, m=m, sub=sub, b=b, slot=gidx % 4):
                                    for k in range(ET):
                                        mi = e.matmul(psA[:, b * 512:(b + 1) * 512], dnb[:, slot, k, :],
                                                      ph[:, ET + k, sub * 512:(sub + 1) * 512],
                                                      start=(k == 0), stop=(k == ET - 1 and n != 0))
                                    if n == 0:
                                        mi = e.matmul(psA[:, b * 512:(b + 1) * 512],
                                                      bext_bf[0:7, m * 128:(m + 1) * 128],
                                                      wTx[0:7, sub * 512:(sub + 1) * 512], start=False, stop=True)
                                    return mi
                                gm = incQ("tensor", "mm", dn_mms)
                                wv("vector", "mm", gm)
                                accsl = acc[:, m, sub * 512:(sub + 1) * 512]
                                if n == 0:
                                    av = incQ("vector", "evv",
                                              lambda e, accsl=accsl, b=b: e.tensor_copy(accsl, psA[:, b * 512:(b + 1) * 512]))
                                else:
                                    av = incQ("vector", "evv",
                                              lambda e, accsl=accsl, b=b: e.tensor_tensor(accsl, accsl, psA[:, b * 512:(b + 1) * 512], ALU.add))
                                bankrel[b] = ("evv", av)
                            dn_cons[gidx] = S.c["mm"]
                            issue_dn(gidx + 4)
                        dndone[n] = S.c["mm"]

                    if j == 0:
                        zst = {}
                        for m in range(HT):
                            if m >= 2:
                                wv("scalar", "st", zst[m - 2])
                            else:
                                wr("scalar", carry["rb16"][m % 2])
                            dmaQ("scalar", "xld", rb16[:, m % 2, 0], fm(r0hi_s)[:, m, base:base + TT])
                            rld = dmaQ("scalar", "xld", rb16[:, m % 2, 1], fm(r0lo_s)[:, m, base:base + TT])
                            wv("vector", "xld", rld)
                            flQ("vector", lambda e, m=m: e.tensor_tensor(acc[:, m, :], acc[:, m, :], rb16[:, m % 2, 0], ALU.add))
                            flQ("vector", lambda e, m=m: e.tensor_tensor(acc[:, m, :], acc[:, m, :], rb16[:, m % 2, 1], ALU.add))
                            flQ("vector", lambda e, m=m: e.tensor_copy(rb16[:, m % 2, 0], acc[:, m, :]))
                            zev = incQ("vector", "evv",
                                       lambda e, m=m: e.tensor_tensor(rb16[:, m % 2, 1], acc[:, m, :], rb16[:, m % 2, 0], ALU.subtract))
                            wv("gpsimd", "evv", zev)
                            dmaQ("gpsimd", "st", fm(xt1hi_s)[:, m, base:base + TT], rb16[:, m % 2, 0])
                            zst[m] = dmaQ("gpsimd", "st", fm(xt1lo_s)[:, m, base:base + TT], rb16[:, m % 2, 1])
                        zstore[t] = S.c["st"]
                        carry["xtu"] = ("mm", updone[NU - 1])
                        carry["ph"] = None
                        carry["rb16"] = {0: ("st", zst[14]), 1: ("st", zst[15])}
                        carry["psW"] = ("evv", wbnrel[5])
                        carry["psG"] = ("evv", wtx_done)
                        carry["wTx"] = ("mm", selmm[5])
                    else:
                        zrel = {}; smm = {}
                        for m in range(HT):
                            if m >= 2:
                                wv("scalar", "evv", zrel[m - 2])
                            else:
                                wr("scalar", carry["rb16"][m % 2])
                            dmaQ("scalar", "xld", rb16[:, m % 2, 0], fm(r0hi_s)[:, m, base:base + TT])
                            rld = dmaQ("scalar", "xld", rb16[:, m % 2, 1], fm(r0lo_s)[:, m, base:base + TT])
                            wv("vector", "xld", rld)
                            flQ("vector", lambda e, m=m: e.tensor_tensor(acc[:, m, :], acc[:, m, :], rb16[:, m % 2, 0], ALU.add))
                            flQ("vector", lambda e, m=m: e.tensor_tensor(acc[:, m, :], acc[:, m, :], rb16[:, m % 2, 1], ALU.add))
                            flQ("vector", lambda e, m=m: e.tensor_copy(xtu[:, m, :], acc[:, m, :]))
                            zev = incQ("vector", "evv",
                                       lambda e, m=m: e.tensor_tensor(ph[:, m, :], acc[:, m, :], xtu[:, m, :], ALU.subtract))
                            zrel[m] = zev
                            wv("scalar", "evv", zev)
                            if m >= 1:
                                wv("scalar", "mm", smm[m - 1])
                            sqe = incQ("scalar", "ev", lambda e, m=m: e.activation(zsqb[:], xtu[:, m, :], AF.Square))
                            wv("tensor", "evv", zev)
                            wv("tensor", "ev", sqe)
                            if m == 0:
                                wv("tensor", "evv", wbnrel[5])

                            def stat_mms(e, m=m):
                                for sub in range(2):
                                    tgt = psG if sub == 0 else psS
                                    e.matmul(tgt[0:1, 0:512], onescol[:], xtu[:, m, sub * 512:(sub + 1) * 512],
                                             start=(m == 0), stop=False)
                                    e.matmul(tgt[0:1, 0:512], onescol[:], ph[:, m, sub * 512:(sub + 1) * 512],
                                             start=False, stop=(m == HT - 1))
                                e.matmul(psW[0:1, 0:512], onescol[:], zsqb[:, 0:512],
                                         start=(m == 0), stop=(m == HT - 1))
                                return e.matmul(psW[0:1, 512:1024], onescol[:], zsqb[:, 512:1024],
                                                start=(m == 0), stop=(m == HT - 1))
                            smm[m] = incQ("tensor", "mm", stat_mms)

                        Asl = [rrow[0:1, i // 2, (i % 2) * 512:(i % 2) * 512 + 512] for i in range(4)]
                        Bsl = [stg[0:1, i // 2, (i % 2) * 512:(i % 2) * 512 + 512] for i in range(4)]
                        wv("vector", "mm", smm[HT - 1])
                        wr("vector", carry["stg"])
                        rows_done = None
                        for sub in range(2):
                            s1ap = (psG if sub == 0 else psS)[0:1, 0:512]
                            s2ap = psW[0:1, sub * 512:(sub + 1) * 512]
                            MR = Bsl[0] if sub == 0 else Bsl[2]
                            R = Bsl[1] if sub == 0 else Bsl[3]
                            s0, s1_, s2_, s3_, s4_ = Asl[0], Asl[1], Asl[2], Asl[3], Bsl[3]
                            flQ("vector", lambda e, s0=s0, s1ap=s1ap: e.tensor_scalar(s0, s1ap, 1.0 / D, None, ALU.mult))
                            flQ("vector", lambda e, s1_=s1_, s2ap=s2ap: e.tensor_scalar(s1_, s2ap, 1.0 / D, None, ALU.mult))
                            flQ("vector", lambda e, s2_=s2_, s0=s0: e.tensor_mul(s2_, s0, s0))
                            flQ("vector", lambda e, s4_=s4_, s1_=s1_, s2_=s2_: e.tensor_sub(s4_, s1_, s2_))
                            ve = incQ("vector", "evv", lambda e, s3_=s3_, s4_=s4_: e.tensor_scalar_add(s3_, s4_, EPS))
                            wv("scalar", "evv", ve)
                            flQ("scalar", lambda e, s2_=s2_, s3_=s3_: e.activation(s2_, s3_, AF.Ln))
                            se = incQ("scalar", "ev", lambda e, s4_=s4_, s2_=s2_: e.activation(s4_, s2_, AF.Exp, bias=0.0, scale=-0.5))
                            wv("vector", "ev", se)
                            flQ("vector", lambda e, s1_=s1_, s4_=s4_: e.tensor_mul(s1_, s4_, s4_))
                            flQ("vector", lambda e, s2_=s2_, s1_=s1_, s3_=s3_: e.tensor_mul(s2_, s1_, s3_))
                            flQ("vector", lambda e, s1_=s1_, s2_=s2_: e.tensor_scalar(s1_, s2_, -0.5, 1.5, ALU.mult, ALU.add))
                            flQ("vector", lambda e, R=R, s4_=s4_, s1_=s1_: e.tensor_mul(R, s4_, s1_))
                            rows_done = incQ("vector", "evv", lambda e, MR=MR, s0=s0, R=R: e.tensor_mul(MR, s0, R))
                        wv("tensor", "evv", rows_done)

                        def bc_mms(e):
                            e.matmul(psW[:, 0:512], ones1f[:], Bsl[0], start=True, stop=True)
                            e.matmul(psW[:, 512:1024], ones1f[:], Bsl[2], start=True, stop=True)
                            e.matmul(psG[:, 0:512], ones1f[:], Bsl[1], start=True, stop=True)
                            return e.matmul(psS[:, 0:512], ones1f[:], Bsl[3], start=True, stop=True)
                        bc_mm = incQ("tensor", "mm", bc_mms)

                        yrel = {}; ost = {}
                        for m in range(HT):
                            if m < 2:
                                wv("scalar", "evv", rows_done)
                                wr("scalar", carry["rrow"][m % 2])
                            else:
                                wv("scalar", "evv", yrel[m - 2])
                            yld = dmaQ("scalar", "xld", rrow[:, m % 2], fm(y_s)[:, m, base:base + TT])
                            wv("vector", "xld", yld)
                            if m == 0:
                                wv("vector", "mm", bc_mm)
                            if m >= 2:
                                wv("vector", "st", ost[m - 2])
                            orv = None
                            for sub in range(2):
                                Rps = psG if sub == 0 else psS
                                ss = stg[:, m % 2, sub * 512:(sub + 1) * 512]
                                flQ("vector", lambda e, ss=ss, m=m, sub=sub, Rps=Rps: e.tensor_tensor(
                                    ss, acc[:, m, sub * 512:(sub + 1) * 512], Rps[:, 0:512], ALU.mult))
                                flQ("vector", lambda e, ss=ss, sub=sub: e.tensor_tensor(
                                    ss, ss, psW[:, sub * 512:(sub + 1) * 512], ALU.subtract))
                                opl = lambda e, ss=ss, m=m, sub=sub: e.tensor_tensor(
                                    ss, ss, rrow[:, m % 2, sub * 512:(sub + 1) * 512], ALU.add)
                                if sub == 1:
                                    orv = incQ("vector", "evv", opl)
                                else:
                                    flQ("vector", opl)
                            yrel[m] = orv
                            wv("gpsimd", "evv", orv)
                            ost[m] = dmaQ("gpsimd", "st", fm(x2T)[:, m, base:base + TT], stg[:, m % 2])
                        carry["xtu"] = ("mm", smm[HT - 1])
                        carry["ph"] = ("mm", smm[HT - 1])
                        carry["psG"] = ("evv", yrel[15])
                        carry["psW"] = ("evv", yrel[15])
                        carry["rrow"] = {0: ("evv", yrel[14]), 1: ("evv", yrel[15])}
                        carry["rb16"] = {0: ("evv", zrel[14]), 1: ("evv", zrel[15])}
                        carry["stg"] = ("st", ost[15])
                        carry["wTx"] = ("mm", selmm[5])
                    tile_iter += 1

            wv("gpsimd", "st", S.c["st"])

            with nc.Block() as blk:
                @blk.tensor
                def _(te):
                    for fn in Q["tensor"]:
                        fn(te)

                @blk.scalar
                def _(sc):
                    for fn in Q["scalar"]:
                        fn(sc)

                @blk.vector
                def _(v):
                    for fn in Q["vector"]:
                        fn(v)

                @blk.sync
                def _(e):
                    for fn in Q["sync"]:
                        fn(e)

                @blk.gpsimd
                def _(gp):
                    for fn in Q["gpsimd"]:
                        fn(gp)
    return nc


def _consts():
    u = np.arange(128)
    ubits = ((u[:, None] >> np.arange(7)[None, :]) & 1).astype(np.float32)
    c1 = 127 - u
    c1bits = ((c1[:, None] >> np.arange(7)[None, :]) & 1).astype(np.float32)
    s = np.arange(128)[:, None]
    t = np.arange(128)[None, :]
    expn = np.maximum(s - t - 1, 0)
    expnbits = np.concatenate([((expn >> b) & 1).astype(np.float32) for b in range(7)], axis=1).astype(BF)
    causT = (s > t).astype(np.float32).astype(BF)
    sel = np.zeros((8, 768), np.float32)
    for n in range(6):
        sel[n, n * 128 : (n + 1) * 128] = 1.0
    return dict(
        eye_bfd=np.eye(128, dtype=np.float32).astype(BF),
        eye_fd=np.eye(128, dtype=np.float32),
        ones1fd=np.ones((1, 128), np.float32),
        onescold=np.ones((128, 1), np.float32).astype(BF),
        selbfd=sel.astype(BF),
        ubitsd=ubits, c1bitsd=c1bits, expnbitsd=expnbits, causTd=causT,
    )


def _prep_inputs(inputs, T):
    f = lambda k: np.ascontiguousarray(np.asarray(inputs[k], np.float32))
    wg3 = f("L0_attn_Wg")
    wg3h = np.ascontiguousarray(wg3.reshape(16, 128, 3).transpose(1, 0, 2).reshape(128, 48))
    fg = f("L0_ffn_Wg")
    fgh = np.ascontiguousarray(fg.reshape(2, 16, 128, 8).transpose(0, 2, 1, 3).reshape(2, 128, 128))
    bup = f("L0_ffn_bup").reshape(6, 8, 128).transpose(2, 0, 1).reshape(128, 48)
    bsw = f("L0_ffn_bsw").reshape(6, 8, 128).transpose(2, 0, 1).reshape(128, 48)
    bsu = f("L0_ffn_bsu").reshape(2, 8, 128).transpose(2, 0, 1).reshape(128, 16)
    bsa = f("L0_ffn_bsa").reshape(8, 128).transpose(1, 0).reshape(128, 8)
    biascols = np.ascontiguousarray(np.concatenate([bup, bsw, bsu, bsa], axis=1))
    bext = np.ascontiguousarray(np.concatenate([f("L0_ffn_bdn"), f("L0_ffn_bsd").reshape(1, -1)], 0))
    common = dict(
        wq=f("L0_attn_Wq"), wk=f("L0_attn_Wk"), wv=f("L0_attn_Wv"), wo=f("L0_attn_Wo"),
        wg3h=wg3h, dlog=f("L0_attn_decay_logits").reshape(1, 3),
        oscale=np.asarray(inputs["L0_attn_out_scale"], np.float32).reshape(1, 1),
        fgh=fgh, bgh=f("L0_ffn_bg").reshape(1, 16),
        fwup=f("L0_ffn_Wup"), fwsw=f("L0_ffn_Wsw"), fwdn=f("L0_ffn_Wdn"),
        fwsu=f("L0_ffn_Wsu"), fwsa=f("L0_ffn_Wsa"), fwsd=f("L0_ffn_Wsd"),
        biascols=biascols, bexth=bext,
    )
    common.update(_consts())
    x = np.asarray(inputs["L0_attn_x"], np.float32)
    maps = []
    for b in range(x.shape[0]):
        m = dict(common)
        m["xT"] = np.ascontiguousarray(x[b, :T].T)
        maps.append(m)
    return maps


def kernel(**inputs):
    T = 4096
    nc = build_nc(T)
    in_maps = _prep_inputs(inputs, T)
    res = run_bass_kernel_spmd(nc, in_maps, core_ids=list(range(8)))
    out = np.stack([np.ascontiguousarray(np.asarray(r["x2T"]).T) for r in res.results], 0)
    return out.astype(np.float32)
